# version 13
# baseline (speedup 1.0000x reference)
import numpy as np
import concourse.bass as bass
import concourse.mybir as mybir
from concourse.bass_utils import run_bass_kernel_spmd
from concourse.alu_op_type import AluOpType as OP

F32 = mybir.dt.float32
BF16 = mybir.dt.bfloat16
AF = mybir.ActivationFunctionType
AX = mybir.AxisListType

NEG_BIG = -1.0e30
NLANES = 8

ENGINES = ("pe", "act", "dve", "pool", "sp")
SECTION = dict(pe="tensor", act="scalar", dve="vector", pool="gpsimd", sp="sync")


class H:
    __slots__ = ("eng", "idx", "dma", "rank", "lane")

    def __init__(self, eng, idx, dma):
        self.eng, self.idx, self.dma = eng, idx, dma
        self.rank = 0
        self.lane = 0


class Buf:
    __slots__ = ("ap", "writers", "readers", "name")

    def __init__(self, ap, name=""):
        self.ap = ap
        self.writers = []
        self.readers = []
        self.name = name


class Prog:
    def __init__(self, sync_all=False):
        self.ops = {e: [] for e in ENGINES}
        self.sync_all = sync_all
        self.dma_counts = {e: 0 for e in ENGINES}

    @staticmethod
    def _reduce(handles):
        best = {}
        for h in handles:
            key = (h.eng, h.dma, h.lane)
            if key not in best or h.idx > best[key].idx:
                best[key] = h
        return list(best.values())

    def op(self, eng, fn, reads=(), writes=(), pwrites=(), dma=False, tag="", small=False, sync_self=False):
        deps = []
        for b in reads:
            deps += b.writers
        for b in writes:
            deps += b.writers + b.readers
        for b in pwrites:
            deps += b.writers + b.readers
        h = H(eng, len(self.ops[eng]), dma)
        if dma:
            h.lane = self.dma_counts[eng] % NLANES
            self.dma_counts[eng] += 1
        if len(deps) > 12:
            deps = Prog._reduce(deps)
        self.ops[eng].append(dict(fn=fn, deps=deps, h=h, tag=tag, marked=False,
                                  small=small, sync_self=sync_self))
        for b in reads:
            b.readers.append(h)
        for b in writes:
            b.writers = [h]
            b.readers = []
        for b in pwrites:
            b.writers.append(h)
            b.readers = []
        return h

    def wait_only(self, eng, deps, tag="w"):
        h = H(eng, len(self.ops[eng]), False)
        self.ops[eng].append(dict(fn=None, deps=list(deps), h=h, tag=tag, marked=False,
                                  small=False, sync_self=False))
        return h

    @staticmethod
    def handoff(outs, ins):
        acc = []
        for b in outs:
            acc += b.writers
            acc += b.readers
        deps = Prog._reduce(acc)
        for nb in ins:
            nb.writers = Prog._reduce(nb.writers + deps)

    def barrier(self, tag="bar"):
        lasts = []
        for e in ENGINES:
            lastc = None
            lastd = {}
            nd = 0
            for o in self.ops[e]:
                if o["h"].dma:
                    lastd[nd % NLANES] = o["h"]
                    nd += 1
                elif o["fn"] is not None:
                    lastc = o["h"]
            if lastc is not None:
                lasts.append(lastc)
            lasts += list(lastd.values())
        for e in ENGINES:
            deps = [h for h in lasts if not (h.eng == e and not h.dma)]
            if deps:
                self.wait_only(e, deps, tag=tag)

    def emit(self, nc):
        for e in ENGINES:
            lane_cnt = [0] * NLANES
            for o in self.ops[e]:
                if o["h"].dma:
                    lane = o["h"].lane
                    lane_cnt[lane] += 1
                    o["h"].rank = (lane, lane_cnt[lane])
        for e in ENGINES:
            for o in self.ops[e]:
                byeng = {}
                for d in o["deps"]:
                    if (d.eng == e and not d.dma and not self.ops[e][d.idx]["small"]
                            and not o["sync_self"] and not self.sync_all):
                        continue
                    key = (d.eng, d.dma, d.rank[0] if d.dma else 0)
                    if key not in byeng or d.idx > byeng[key].idx:
                        byeng[key] = d
                o["wdeps"] = list(byeng.values())
                for d in o["wdeps"]:
                    self.ops[d.eng][d.idx]["marked"] = True
        for e in ENGINES:
            r = 0
            for o in self.ops[e]:
                if not o["h"].dma and o["marked"]:
                    r += 1
                    o["h"].rank = r
        sems = {e: nc.alloc_semaphore(f"sem_{e}") for e in ENGINES}
        dsems = {e: [nc.alloc_semaphore(f"dsem_{e}{i}") for i in range(NLANES)]
                 for e in ("sp", "pool")}

        with nc.Block() as blk:
            for e in ENGINES:

                def make(e):
                    def body(eng):
                        hw = {}
                        for o in self.ops[e]:
                            for d in o["wdeps"]:
                                if d.dma:
                                    lane, cnt = d.rank
                                    key = ("d", d.eng, lane)
                                    val = 16 * cnt
                                    sem = dsems[d.eng][lane]
                                else:
                                    key = ("c", d.eng)
                                    val = d.rank
                                    sem = sems[d.eng]
                                if hw.get(key, 0) < val:
                                    eng.wait_ge(sem, val)
                                    hw[key] = val
                            if o["fn"] is not None:
                                if o["h"].dma:
                                    lane, cnt = o["h"].rank
                                    if cnt >= 2:
                                        key = ("d", e, lane)
                                        val = 16 * (cnt - 1)
                                        if hw.get(key, 0) < val:
                                            eng.wait_ge(dsems[e][lane], val)
                                            hw[key] = val
                                inst = o["fn"](eng)
                                if o["h"].dma:
                                    inst.then_inc(dsems[e][o["h"].rank[0]], 16)
                                elif o["marked"]:
                                    inst.then_inc(sems[e], 1)

                    return body

                getattr(blk, SECTION[e])(make(e))
        return sems, dsems


class Carve:
    def __init__(self, ap, cap, unit):
        self.ap_full = ap
        self.cap = cap
        self.off = 0
        self.hi = 0
        self.stack = []
        self.unit = unit

    def take(self, n, name=""):
        assert self.off + n <= self.cap, f"arena overflow {name}: {self.off}+{n}>{self.cap}"
        ap = self.ap_full[:, self.off:self.off + n]
        self.off += n
        self.hi = max(self.hi, self.off)
        return Buf(ap, name)

    def push(self):
        self.stack.append(self.off)

    def pop(self):
        self.off = self.stack.pop()




def make_cfg(D=2048, DI=4096, NST=16, RDT=128, KCV=4, SEQ=2048, NSEQ=2, Tc=512,
             E=32, FI=1408, SI=2816, taps=(), stop_after=None):
    c = dict(D=D, DI=DI, NST=NST, RDT=RDT, KCV=KCV, SEQ=SEQ, NSEQ=NSEQ, Tc=Tc,
             E=E, FI=FI, SI=SI, taps=set(taps), stop_after=stop_after)
    c["KB"] = D // 128
    c["DB"] = DI // 128
    c["MBo"] = D // 128
    c["IB"] = FI // 128
    c["SIB"] = SI // 128
    c["NCH"] = SEQ // Tc
    c["TT"] = Tc // 128
    c["SLM"] = Tc
    c["XSL"] = DI // Tc
    c["OSL"] = D // Tc
    c["WKH"] = max(1, c["DB"] // 4)
    return c


def _kblock(a):
    K, M = a.shape
    return np.ascontiguousarray(
        a.reshape(K // 128, 128, M).transpose(1, 0, 2).reshape(128, (K // 128) * M)
    )


def pack_host(cfg, w):
    D, DI, NST, RDT = cfg["D"], cfg["DI"], cfg["NST"], cfg["RDT"]
    E = cfg["E"]
    KB, DB, MBo = cfg["KB"], cfg["DB"], cfg["MBo"]
    SLM, XSL, OSL, WKH = cfg["SLM"], cfg["XSL"], cfg["OSL"], cfg["WKH"]

    out = {}
    W_in = w["L0_attn_W_in"]

    def slabify(Wrows):
        a = Wrows.reshape(XSL, SLM, KB, 128)
        return np.ascontiguousarray(a.transpose(0, 3, 2, 1))

    out["winx"] = slabify(W_in[:DI])
    out["winz"] = slabify(W_in[DI:])

    WoT = np.ascontiguousarray(w["L0_attn_W_out"].T)
    a = WoT.reshape(DB // WKH, WKH, 128, OSL, SLM)
    out["wout"] = np.ascontiguousarray(a.transpose(3, 0, 2, 1, 4))

    W_x = w["L0_attn_W_x"]
    out["wxdt"] = _kblock(np.ascontiguousarray(W_x[:RDT].T))
    out["wxbc"] = _kblock(np.ascontiguousarray(W_x[RDT:].T))
    out["wdt"] = _kblock(np.ascontiguousarray(w["L0_attn_W_dt"].T))

    def chan(v):
        return np.ascontiguousarray(v.reshape(DB, 128).T)

    out["convw"] = np.ascontiguousarray(
        w["L0_attn_conv_w"].reshape(DB, 128, cfg["KCV"]).transpose(1, 0, 2).reshape(128, DB * cfg["KCV"])
    )
    out["convb"] = chan(w["L0_attn_conv_b"])
    out["bdt"] = chan(w["L0_attn_b_dt"])
    out["dchan"] = chan(w["L0_attn_D"])
    out["bout"] = np.ascontiguousarray(w["L0_attn_b_out"].reshape(MBo, 128).T)
    out["alog"] = np.ascontiguousarray(
        w["L0_attn_A_log"].reshape(DB, 128, NST).transpose(1, 0, 2).reshape(128, DB * NST)
    )

    out["gate"] = _kblock(np.ascontiguousarray(w["L0_ffn_gate_w"].T))
    out["rgT"] = _kblock(np.ascontiguousarray(w["L0_ffn_Rg"].T))
    out["ruT"] = _kblock(np.ascontiguousarray(w["L0_ffn_Ru"].T))
    out["rdT"] = _kblock(np.ascontiguousarray(w["L0_ffn_Rd"].T))
    out["srgT"] = _kblock(np.ascontiguousarray(w["L0_ffn_s_Rg"].T))
    out["sruT"] = _kblock(np.ascontiguousarray(w["L0_ffn_s_Ru"].T))
    out["sugT"] = np.ascontiguousarray(w["L0_ffn_s_Ug"].T)
    out["suuT"] = np.ascontiguousarray(w["L0_ffn_s_Uu"].T)
    out["sudT"] = np.ascontiguousarray(w["L0_ffn_s_Ud"].T)
    out["scgT"] = np.ascontiguousarray(w["L0_ffn_s_Cg"].T)
    out["scuT"] = np.ascontiguousarray(w["L0_ffn_s_Cu"].T)
    out["srdT"] = _kblock(np.ascontiguousarray(w["L0_ffn_s_Rd"].T))
    out["scdT"] = np.ascontiguousarray(w["L0_ffn_s_Cd"].T)

    expw = np.concatenate(
        [
            w["L0_ffn_Ug"].transpose(0, 2, 1),
            w["L0_ffn_Cg"].transpose(0, 2, 1),
            w["L0_ffn_Uu"].transpose(0, 2, 1),
            w["L0_ffn_Cu"].transpose(0, 2, 1),
            w["L0_ffn_Ud"].transpose(0, 2, 1),
        ],
        axis=2,
    )
    out["expw"] = np.ascontiguousarray(expw)
    CdT = w["L0_ffn_Cd"].transpose(0, 2, 1)
    cdmb = CdT.reshape(E, 128, MBo, 128).transpose(2, 1, 0, 3).reshape(MBo, 128, E * 128)
    out["cdmb"] = np.ascontiguousarray(cdmb)

    out["ident"] = np.eye(128, dtype=np.float32)
    return out




def build_program(cfg):
    D, DI, NST, RDT, KCV = cfg["D"], cfg["DI"], cfg["NST"], cfg["RDT"], cfg["KCV"]
    SEQ, NSEQ, Tc = cfg["SEQ"], cfg["NSEQ"], cfg["Tc"]
    E, FI, SI = cfg["E"], cfg["FI"], cfg["SI"]
    KB, DB, MBo, IB, SIB = cfg["KB"], cfg["DB"], cfg["MBo"], cfg["IB"], cfg["SIB"]
    NCH, TT, SLM, XSL, OSL, WKH = (
        cfg["NCH"], cfg["TT"], cfg["SLM"], cfg["XSL"], cfg["OSL"], cfg["WKH"]
    )
    EXPC = 2 * FI + 3 * 128
    TOK = NSEQ * SEQ
    stop = cfg["stop_after"]

    nc = bass.Bass("TRN2", target_bir_lowering=False)
    p = Prog(sync_all=cfg.get("sync_all", False))

    def din(name, shape, dtype=F32):
        return nc.dram_tensor(name, list(shape), dtype, kind="ExternalInput").ap()

    x_in = din("x", (TOK, D))
    winx = din("winx", (XSL, 128, KB * SLM))
    winz = din("winz", (XSL, 128, KB * SLM))
    wout_in = din("wout", (OSL, DB // WKH, 128, WKH * SLM))
    wxdt_in = din("wxdt", (128, DB * RDT))
    wxbc_in = din("wxbc", (128, DB * 2 * NST))
    wdt_in = din("wdt", (128, DB * 128))
    convw_in = din("convw", (128, DB * KCV))
    convb_in = din("convb", (128, DB))
    bdt_in = din("bdt", (128, DB))
    dchan_in = din("dchan", (128, DB))
    bout_in = din("bout", (128, MBo))
    alog_in = din("alog", (128, DB * NST))
    gate_in = din("gate", (128, KB * E))
    rgT_in = din("rgT", (128, KB * 128))
    ruT_in = din("ruT", (128, KB * 128))
    rdT_in = din("rdT", (128, IB * 128))
    srgT_in = din("srgT", (128, KB * 128))
    sruT_in = din("sruT", (128, KB * 128))
    sugT_in = din("sugT", (128, 128))
    suuT_in = din("suuT", (128, 128))
    sudT_in = din("sudT", (128, 128))
    scgT_in = din("scgT", (128, SI))
    scuT_in = din("scuT", (128, SI))
    srdT_in = din("srdT", (128, SIB * 128))
    scdT_in = din("scdT", (128, D))
    expw_in = din("expw", (E, 128, EXPC))
    cdmb_in = din("cdmb", (MBo, 128, E * 128))
    ident_in = din("ident", (128, 128))

    out_hbm = nc.dram_tensor("out", [TOK, D], F32, kind="ExternalOutput").ap()

    def scr(name, shape, dtype=BF16):
        return nc.dram_tensor(name, list(shape), dtype, kind="Internal").ap()

    winx_bf = scr("winx_bf", (XSL, 128, KB * SLM))
    winz_bf = scr("winz_bf", (XSL, 128, KB * SLM))
    wout_bf = scr("wout_bf", (OSL, DB // WKH, 128, WKH * SLM))
    expw_bf = scr("expw_bf", (E, 128, EXPC))
    cdmb_bf = scr("cdmb_bf", (MBo, 128, E * 128))
    bc_dram = [Buf(scr(f"bc_dram{i}", (2 * NST, Tc)), f"bcd{i}") for i in range(2)]
    wall_dram = [Buf(scr(f"wall_dram{i}", (E, Tc), F32), f"wal{i}") for i in range(2)]
    x1_dram = Buf(scr("x1_dram", (TOK, D), F32), "x1d")

    def sbp(name, cols, dtype=F32):
        t = nc.alloc_sbuf_tensor(name, [128, cols], dtype)
        return Buf(t.ap(), name)

    wdt_r = sbp("wdt_r", DB * 128, BF16)
    wxdt_r = sbp("wxdt_r", DB * RDT, BF16)
    wxbc_r = sbp("wxbc_r", DB * 2 * NST, BF16)
    idb_r = sbp("idb_r", 128, BF16)
    u_f = sbp("u_f", KB * Tc, BF16)
    convw_r = sbp("convw_r", DB * KCV)
    convb_r = sbp("convb_r", DB)
    bdt_r = sbp("bdt_r", DB)
    dchan_r = sbp("dchan_r", DB)
    bout_r = sbp("bout_r", MBo)
    a_r = sbp("a_r", DB * NST)
    idf_r = sbp("idf_r", 128)
    state = sbp("state", DB * NST)
    carry = sbp("carry", DB * (KCV - 1))
    lnt = [sbp(f"lnt{i}", 8) for i in range(2)]
    cst = sbp("cst", 2)

    FFNWT_COLS = (KB * E + 2 * KB * 128 + IB * 128 + 2 * KB * 128 + 3 * 128
                  + 2 * SI + SIB * 128 + D)
    M_BF = DB * Tc + 2 * KB * SLM + max(
        2 * D,
        2 * Tc + NST * Tc + NST * Tc,
        2 * WKH * SLM + MBo * Tc + 2 * D)
    F_BF = E * Tc + FFNWT_COLS + 3 * Tc + max(
        2 * D,
        4 * Tc + Tc + max(IB, SIB // 2) * Tc + 2 * Tc + 2 * Tc + 2 * EXPC,
        2 * E * 128 + MBo * Tc + 2 * D)
    NBF = max(M_BF, F_BF) + 64
    M_F32 = max(2 * D + 2 * (Tc + KCV) + 4 * Tc,
                20 * Tc,
                4 * D)
    F_F32 = (TT * E + 6 * E + 16 + 2 * Tc) + max(2 * D, 4 * D)
    NF32 = 2048 + max(M_F32, F_F32) + 32
    arb_t = nc.alloc_sbuf_tensor("arena_bf", [128, NBF], BF16)
    arf_t = nc.alloc_sbuf_tensor("arena_f32", [128, NF32], F32)
    bfar = Carve(arb_t.ap(), NBF, 2)
    far = Carve(arf_t.ap(), NF32, 4)

    sqs = far.take(2048, "sqs")

    bfar.push()
    far.push()
    xcy_all_ap = bfar.ap_full[:, bfar.off:bfar.off + DB * Tc]
    xcy = [Buf(xcy_all_ap[:, i * Tc:(i + 1) * Tc], f"xcy{i}") for i in range(DB)]
    bfar.off += DB * Tc
    slab = [bfar.take(KB * SLM, f"slab{i}") for i in range(2)]
    bfar.push()
    far.push()
    ut = [bfar.take(D, f"ut{i}") for i in range(2)]
    xt = [far.take(D, f"xt{i}") for i in range(2)]
    xfull = [far.take(Tc + KCV, f"xfull{i}") for i in range(2)]
    cva = [far.take(Tc, f"cva{i}") for i in range(2)]
    cvb = [far.take(Tc, f"cvb{i}") for i in range(2)]
    bfar.pop()
    far.pop()
    bfar.push()
    far.push()
    dtin_bf = bfar.take(Tc, "dtin")
    bcf = bfar.take(Tc, "bcf")
    cbc = [bfar.take(Tc, f"cbc{i}") for i in range(NST)]
    bbc = [bfar.take(Tc, f"bbc{i}") for i in range(NST)]
    szb = [far.take(Tc, f"szb{i}") for i in range(2)]
    delta = [far.take(Tc, f"delta{i}") for i in range(2)]
    sfp = [far.take(Tc, f"sfp{i}") for i in range(3)]
    du = [far.take(Tc, f"du{i}") for i in range(2)]
    atile = [far.take(Tc, f"at{i}") for i in range(2)]
    dub = [far.take(Tc, f"dub{i}") for i in range(2)]
    stile = [far.take(Tc, f"st{i}") for i in range(2)]
    e4t = [far.take(Tc, f"e4{i}") for i in range(2)]
    yacc = far.take(Tc, "yacc")
    yfin = [far.take(Tc, "yfin0")]
    bfar.pop()
    far.pop()
    bfar.push()
    far.push()
    wo_slab = [bfar.take(WKH * SLM, f"wosl{i}") for i in range(2)]
    mo_bf = [bfar.take(Tc, f"mo_bf{i}") for i in range(MBo)]
    mobt = [bfar.take(D, f"mobt{i}") for i in range(2)]
    xr = [far.take(D, f"xr{i}") for i in range(2)]
    x1t = [far.take(D, f"x1t{i}") for i in range(2)]
    bfar.pop()
    far.pop()
    M_bf_hi, M_f_hi = bfar.hi, far.hi
    bfar.pop()
    far.pop()

    bfar.push()
    far.push()
    h2e = [bfar.take(Tc, f"h2e{i}") for i in range(E)]
    gate_r = bfar.take(KB * E, "gate_r")
    rgT_r = bfar.take(KB * 128, "rgT_r")
    ruT_r = bfar.take(KB * 128, "ruT_r")
    rdT_r = bfar.take(IB * 128, "rdT_r")
    srgT_r = bfar.take(KB * 128, "srgT_r")
    sruT_r = bfar.take(KB * 128, "sruT_r")
    sugT_r = bfar.take(128, "sugT_r")
    suuT_r = bfar.take(128, "suuT_r")
    sudT_r = bfar.take(128, "sudT_r")
    scgT_r = bfar.take(SI, "scgT_r")
    scuT_r = bfar.take(SI, "scuT_r")
    srdT_r = bfar.take(SIB * 128, "srdT_r")
    scdT_r = bfar.take(D, "scdT_r")
    rg_bf = bfar.take(Tc, "rg_bf")
    ru_bf = bfar.take(Tc, "ru_bf")
    h2sh = bfar.take(Tc, "h2sh")
    we = far.take(TT * E, "we")
    esb = [far.take(E, f"esb{i}") for i in range(2)]
    tcur = [far.take(E, f"tcur{i}") for i in range(2)]
    tc2 = [far.take(E, f"tc2{i}") for i in range(2)]
    gsm = [far.take(8, f"gsm{i}") for i in range(2)]
    wbc = [far.take(Tc, f"wbc{i}") for i in range(2)]
    bfar.push()
    far.push()
    u2t = [bfar.take(D, f"u2t{i}") for i in range(2)]
    xf1 = [far.take(D, f"xf1_{i}") for i in range(2)]
    bfar.pop()
    far.pop()
    bfar.push()
    far.push()
    h1a = [bfar.take(Tc, f"h1a{i}") for i in range(2)]
    h1b = [bfar.take(Tc, f"h1b{i}") for i in range(2)]
    gtile = [bfar.take(Tc, f"gt{i}") for i in range(max(IB, SIB // 2))]
    gut = [bfar.take(Tc, f"gut{i}") for i in range(2)]
    rdw = [bfar.take(Tc, f"rdw{i}") for i in range(2)]
    expwt = [bfar.take(EXPC, f"expwt{i}") for i in range(2)]
    bfar.pop()
    far.pop()
    bfar.push()
    far.push()
    cdt = [bfar.take(E * 128, f"cdt{i}") for i in range(2)]
    fo_bf = [bfar.take(Tc, f"fo_bf{i}") for i in range(MBo)]
    fot = [bfar.take(D, f"fot{i}") for i in range(2)]
    xf4 = [far.take(D, f"xf4_{i}") for i in range(2)]
    ot = [far.take(D, f"ot{i}") for i in range(2)]
    bfar.pop()
    far.pop()
    F_bf_hi, F_f_hi = bfar.hi, far.hi
    bfar.pop()
    far.pop()

    M01set = ut + xt + xfull + cva + cvb
    M23set = ([dtin_bf, bcf, yacc] + cbc + bbc + szb + delta + sfp + du + atile
              + dub + stile + e4t + yfin)
    M4set = wo_slab + mo_bf + mobt + xr + x1t
    Mcommon = list(xcy) + slab
    Mall = M01set + M23set + M4set + Mcommon
    F0set = u2t + xf1
    F23set = h1a + h1b + gtile + gut + rdw + expwt
    F4set = cdt + fo_bf + fot + xf4 + ot
    Fcommon = h2e + [gate_r, rgT_r, ruT_r, rdT_r, srgT_r, sruT_r, sugT_r, suuT_r,
               sudT_r, scgT_r, scuT_r, srdT_r, scdT_r, rg_bf, ru_bf, h2sh, we,
               wbc[0], wbc[1]] + esb + tcur + tc2 + gsm
    Fall = F0set + F23set + F4set + Fcommon

    def psb(name, cols, dtype=F32):
        t = nc.alloc_psum_tensor(name, [128, cols], dtype)
        return Buf(t.ap(), name)

    psg = [psb(f"psg{i}", Tc if Tc <= 512 else 512) for i in range(5)]
    ptb = [psb(f"ptb{i}", 1024, BF16) for i in range(2)]
    ptf = psb("ptf", Tc if Tc <= 512 else 512, F32)

    taps = []

    def tap(name, bufs, ap):
        if name not in cfg["taps"]:
            return
        t = nc.dram_tensor("tap_" + name, list(ap.shape), ap.dtype, kind="ExternalOutput").ap()
        p.op("sp", lambda e, t=t, ap=ap: e.dma_start(t, ap), reads=list(bufs), dma=True, tag="tap")
        taps.append(name)

    def dma(dst_buf, src_ap, eng="sp", cols=None, tag=""):
        if cols is None:
            dap, w, pw = dst_buf.ap, [dst_buf], []
        else:
            dap, w, pw = dst_buf.ap[:, cols[0]:cols[1]], [], [dst_buf]
        return p.op(eng, lambda e, d=dap, s=src_ap: e.dma_start(d, s),
                    writes=w, pwrites=pw, dma=True, tag=tag)

    def mm(psbuf, lhs, lcols, rhs, rcols, start, last, pslice=None, tag="mm"):
        lap = lhs.ap[:, lcols[0]:lcols[1]]
        rap = rhs.ap[:, rcols[0]:rcols[1]]
        pap = psbuf.ap[:, 0:rcols[1] - rcols[0]] if pslice is None else psbuf.ap[
            pslice[0]:pslice[1], pslice[2]:pslice[3]]
        return p.op("pe",
                    lambda e, o=pap, l=lap, r=rap, s=start, t=last: e.matmul(o, l, r, start=s, stop=t),
                    reads=[lhs, rhs],
                    writes=[psbuf] if start else (),
                    pwrites=[psbuf] if not start else (),
                    tag=tag)

    def ev_copy(dst, src, eng="act", dcols=None, scols=None, srows=None, tag="ev"):
        dap = dst.ap if dcols is None else dst.ap[:, dcols[0]:dcols[1]]
        sap = src.ap
        if scols is not None:
            sap = sap[:, scols[0]:scols[1]]
        if srows is not None:
            sap = sap[srows[0]:srows[1], :]
        fn = (lambda e, d=dap, s=sap: e.copy(d, s)) if eng == "act" else (
            lambda e, d=dap, s=sap: e.tensor_copy(d, s))
        return p.op(eng, fn, reads=[src],
                    writes=[dst] if dcols is None else (),
                    pwrites=[dst] if dcols is not None else (), tag=tag)

    def ln_phase(xbuf, ubuf, st, tag):
        p.op("dve", lambda e, x=xbuf, s=st: e.reduce_sum(
            s.ap[:, 0:1], x.ap, axis=AX.X, negate=True), reads=[xbuf], writes=[st],
            tag=tag + "s", small=True)
        p.op("act", lambda e, x=xbuf: e.activation(
            sqs.ap[:, 0:D], x.ap, AF.Square),
            reads=[xbuf], writes=[sqs], tag=tag + "sq")
        p.op("dve", lambda e, s=st: e.reduce_sum(
            s.ap[:, 1:2], sqs.ap[:, 0:D], axis=AX.X), reads=[sqs], pwrites=[st],
            tag=tag + "sq2", small=True)
        p.op("dve", lambda e, s=st: e.tensor_scalar(
            s.ap[:, 2:3], s.ap[:, 0:1], 1.0 / D, None, OP.mult), reads=[st], pwrites=[st],
            tag=tag + "m", small=True)
        p.op("dve", lambda e, s=st: e.tensor_scalar(
            s.ap[:, 3:4], s.ap[:, 1:2], 1.0 / D, None, OP.mult), reads=[st], pwrites=[st],
            tag=tag + "e2", small=True)
        p.op("dve", lambda e, s=st: e.tensor_tensor(
            s.ap[:, 4:5], s.ap[:, 2:3], s.ap[:, 2:3], OP.mult), reads=[st], pwrites=[st],
            tag=tag + "m2", small=True)
        p.op("dve", lambda e, s=st: e.tensor_tensor(
            s.ap[:, 5:6], s.ap[:, 3:4], s.ap[:, 4:5], OP.subtract), reads=[st], pwrites=[st],
            tag=tag + "v", small=True)
        p.op("act", lambda e, s=st: e.activation(
            s.ap[:, 6:7], s.ap[:, 5:6], AF.Ln, bias=cst.ap[:, 0:1]),
            reads=[st, cst], pwrites=[st], tag=tag + "ln", small=True)
        p.op("act", lambda e, s=st: e.activation(
            s.ap[:, 7:8], s.ap[:, 6:7], AF.Exp, scale=-0.5), reads=[st], pwrites=[st],
            tag=tag + "rs", small=True)
        p.op("dve", lambda e, x=xbuf, s=st, u=ubuf: e.tensor_scalar(
            u.ap, x.ap, s.ap[:, 2:3], s.ap[:, 7:8], OP.add, OP.mult),
            reads=[xbuf, st], writes=[ubuf], tag=tag + "u")

    def transpose_to(dst, dst_cols, src, src_cols, pt, tag):
        p.op("pe", lambda e, s=src, sc=src_cols, pt=pt: e.transpose(
            pt.ap[:, 0:128], s.ap[:, sc[0]:sc[1]], idb_r.ap),
            reads=[src, idb_r], writes=[pt], tag=tag)
        p.op("act", lambda e, d=dst, dc=dst_cols, pt=pt: e.copy(
            d.ap[:, dc[0]:dc[1]], pt.ap[:, 0:128]),
            reads=[pt], pwrites=[dst], tag=tag + "e")

    for dst, src in ((winx_bf, winx), (winz_bf, winz), (wout_bf, wout_in),
                     (expw_bf, expw_in), (cdmb_bf, cdmb_in)):
        p.op("pool", lambda e, d=dst, s=src: e.dma_start(d, s), dma=True, tag="cast")
    castw = Buf(None, "castw")
    castw.writers = [p.ops["pool"][-1]["h"]]

    for dst, src in ((wdt_r, wdt_in), (wxdt_r, wxdt_in), (wxbc_r, wxbc_in),
                     (idb_r, ident_in)):
        p.op("pool", lambda e, d=dst, s=src: e.dma_start(d.ap, s), writes=[dst], dma=True, tag="res")
    for dst, src in ((convw_r, convw_in), (convb_r, convb_in), (bdt_r, bdt_in),
                     (dchan_r, dchan_in), (bout_r, bout_in), (idf_r, ident_in)):
        dma(dst, src, tag="res")
    p.op("dve", lambda e: e.memset(cst.ap[:, 0:1], 1e-5), writes=[cst], tag="cst0")
    p.op("dve", lambda e: e.memset(cst.ap[:, 1:2], 1.0), pwrites=[cst], tag="cst1")
    dma(sqs, alog_in, cols=(0, DB * NST), tag="alog")
    p.op("act", lambda e: e.activation(sqs.ap[:, DB * NST:2 * DB * NST], sqs.ap[:, 0:DB * NST], AF.Exp),
         reads=[sqs], pwrites=[sqs], tag="aexp")
    p.op("dve", lambda e: e.tensor_scalar(a_r.ap, sqs.ap[:, DB * NST:2 * DB * NST], -1.0, None, OP.mult),
         reads=[sqs], writes=[a_r], tag="aneg")

    p.barrier("ph0")

    for sqi in range(NSEQ):
        p.op("dve", lambda e: e.memset(state.ap, 0.0), writes=[state], tag="st0")
        p.op("dve", lambda e: e.memset(carry.ap, 0.0), writes=[carry], tag="cy0")
        for ci in range(NCH):
            t0 = sqi * SEQ + ci * Tc

            for tt in range(TT):
                xb, ub, st = xt[tt % 2], ut[tt % 2], lnt[tt % 2]
                dma(xb, x_in[t0 + tt * 128: t0 + (tt + 1) * 128, :], tag="xload")
                tap(f"xt_{sqi}_{ci}_{tt}", [xb], xb.ap)
                ln_phase(xb, ub, st, "ln")
                tap(f"lnt_{sqi}_{ci}_{tt}", [st], st.ap)
                tap(f"ut_{sqi}_{ci}_{tt}", [ub], ub.ap)
                for kb in range(KB):
                    transpose_to(u_f, (kb * Tc + tt * 128, kb * Tc + (tt + 1) * 128),
                                 ub, (kb * 128, (kb + 1) * 128), ptb[kb % 2], "tru")
            tap(f"uf_{sqi}_{ci}", [u_f], u_f.ap)

            for s in range(XSL):
                slb = slab[s % 2]
                p.op("sp", lambda e, d=slb, s=s: e.dma_start(d.ap, winx_bf[s]),
                     reads=[castw], writes=[slb], dma=True, tag="wslab")
                nmb = SLM // 128
                for mb in range(nmb):
                    dblk = s * nmb + mb
                    pg = psg[dblk % 2]
                    for kb in range(KB):
                        mm(pg, slb, (kb * SLM + mb * 128, kb * SLM + (mb + 1) * 128),
                           u_f, (kb * Tc, (kb + 1) * Tc), kb == 0, kb == KB - 1, tag="inproj")
                    xf = xfull[dblk % 2]
                    p.op("dve", lambda e, xf=xf, dblk=dblk: e.tensor_copy(
                        xf.ap[:, 0:KCV - 1], carry.ap[:, dblk * (KCV - 1):(dblk + 1) * (KCV - 1)]),
                        reads=[carry], writes=[xf], tag="cv_c", small=True)
                    p.op("act", lambda e, xf=xf, pg=pg: e.copy(xf.ap[:, KCV - 1:KCV - 1 + Tc], pg.ap),
                         reads=[pg], pwrites=[xf], tag="cv_ev")
                    p.op("dve", lambda e, xf=xf, dblk=dblk: e.tensor_copy(
                        carry.ap[:, dblk * (KCV - 1):(dblk + 1) * (KCV - 1)],
                        xf.ap[:, Tc:Tc + KCV - 1]),
                        reads=[xf], pwrites=[carry], tag="cv_s")
                    ca, cb = cva[dblk % 2], cvb[dblk % 2]
                    p.op("dve", lambda e, xf=xf, dblk=dblk, o=ca: e.tensor_scalar(
                        o.ap, xf.ap[:, 0:Tc], convw_r.ap[:, dblk * KCV:dblk * KCV + 1], None, OP.mult),
                        reads=[xf, convw_r], writes=[ca], tag="cv0")
                    src_, dst_ = ca, cb
                    for k in range(1, KCV):
                        p.op("dve", lambda e, xf=xf, dblk=dblk, k=k, s2=src_, d2=dst_: e.scalar_tensor_tensor(
                            d2.ap, xf.ap[:, k:k + Tc], convw_r.ap[:, dblk * KCV + k:dblk * KCV + k + 1],
                            s2.ap, OP.mult, OP.add),
                            reads=[xf, convw_r, src_], writes=[dst_], tag="cvk")
                        src_, dst_ = dst_, src_
                    p.op("act", lambda e, s2=src_, dblk=dblk: e.activation(
                        xcy[dblk].ap, s2.ap, AF.Silu, bias=convb_r.ap[:, dblk:dblk + 1]),
                        reads=[src_, convb_r], writes=[xcy[dblk]], tag="cvsilu")
            tap(f"xc_{sqi}_{ci}", xcy, xcy_all_ap)
            p.handoff(M01set, M23set)
            if stop == "M1":
                p.handoff(Mall + Fall, Mall)
                continue

            pgdt = psg[2]
            for kb in range(DB):
                mm(pgdt, wxdt_r, (kb * RDT, (kb + 1) * RDT), xcy[kb], (0, Tc),
                   kb == 0, kb == DB - 1, tag="xd_dt")
            ev_copy(dtin_bf, pgdt, tag="dtin_ev")
            pgbc = psg[3]
            for kb in range(DB):
                mm(pgbc, wxbc_r, (kb * 2 * NST, (kb + 1) * 2 * NST), xcy[kb], (0, Tc),
                   kb == 0, kb == DB - 1, pslice=(0, 2 * NST, 0, Tc), tag="xd_bc")
            p.op("act", lambda e: e.copy(bcf.ap[0:2 * NST, :], pgbc.ap[0:2 * NST, 0:Tc]),
                 reads=[pgbc], writes=[bcf], tag="bc_ev")
            bcd = bc_dram[ci % 2]
            p.op("sp", lambda e, d=bcd.ap: e.dma_start(d, bcf.ap[0:2 * NST, :]),
                 reads=[bcf], writes=[bcd], dma=True, tag="bc_w")
            for n in range(NST):
                p.op("sp", lambda e, n=n, s=bcd.ap: e.dma_start(
                    bbc[n].ap,
                    s[n:n + 1, :].partition_broadcast(128).squeeze(1)),
                    reads=[bcd], writes=[bbc[n]], dma=True, tag="bbcast")
            for n in range(NST):
                p.op("sp", lambda e, n=n, s=bcd.ap: e.dma_start(
                    cbc[n].ap,
                    s[NST + n:NST + n + 1, :].partition_broadcast(128).squeeze(1)),
                    reads=[bcd], writes=[cbc[n]], dma=True, tag="cbcast")
            tap(f"dtin_{sqi}_{ci}", [dtin_bf], dtin_bf.ap)
            tap(f"bbc_{sqi}_{ci}", bbc, bbc[0].ap)

            for dblk in range(DB):
                pg = psg[dblk % 2]
                mm(pg, wdt_r, (dblk * 128, (dblk + 1) * 128), dtin_bf, (0, Tc), True, True, tag="dtmm")
                s0, s1, s2 = sfp
                p.op("act", lambda e, pg=pg, dblk=dblk: e.activation(
                    s0.ap, pg.ap, AF.Abs, bias=bdt_r.ap[:, dblk:dblk + 1]),
                    reads=[pg, bdt_r], writes=[s0], tag="sp_abs")
                p.op("act", lambda e: e.activation(s1.ap, s0.ap, AF.Exp, scale=-1.0),
                     reads=[s0], writes=[s1], tag="sp_exp")
                p.op("act", lambda e: e.activation(s2.ap, s1.ap, AF.Ln, bias=cst.ap[:, 1:2]),
                     reads=[s1, cst], writes=[s2], tag="sp_ln")
                p.op("act", lambda e, pg=pg, dblk=dblk: e.activation(
                    s0.ap, pg.ap, AF.Relu, bias=bdt_r.ap[:, dblk:dblk + 1]),
                    reads=[pg, bdt_r], writes=[s0], tag="sp_relu")
                dl = delta[dblk % 2]
                p.op("dve", lambda e, d=dl: e.tensor_tensor(d.ap, s2.ap, s0.ap, OP.add),
                     reads=[s2, s0], writes=[dl], tag="sp_add")
                db_ = du[dblk % 2]
                p.op("dve", lambda e, d=dl, o=db_, dblk=dblk: e.tensor_tensor(
                    o.ap, d.ap, xcy[dblk].ap, OP.mult),
                    reads=[dl, xcy[dblk]], writes=[db_], tag="du")
                p.op("dve", lambda e, dblk=dblk: e.tensor_scalar(
                    yacc.ap, xcy[dblk].ap, dchan_r.ap[:, dblk:dblk + 1], None, OP.mult),
                    reads=[xcy[dblk], dchan_r], writes=[yacc], tag="yinit")
                for n in range(NST):
                    col = dblk * NST + n
                    at = atile[n % 2]
                    p.op("act", lambda e, d=dl, o=at, col=col: e.activation(
                        o.ap, d.ap, AF.Exp, scale=a_r.ap[:, col:col + 1]),
                        reads=[dl, a_r], writes=[at], tag="aexp")
                    dbt = dub[n % 2]
                    p.op("dve", lambda e, d=db_, o=dbt, n=n: e.tensor_tensor(
                        o.ap, d.ap, bbc[n].ap, OP.mult),
                        reads=[db_, bbc[n]], writes=[dbt], tag="dub")
                    stl = stile[n % 2]
                    p.op("dve", lambda e, a=at, b=dbt, o=stl, col=col: e.tensor_tensor_scan(
                        o.ap, a.ap, b.ap, state.ap[:, col:col + 1], OP.mult, OP.add),
                        reads=[at, dbt, state], writes=[stl], tag="scan")
                    p.op("dve", lambda e, s=stl, col=col: e.tensor_copy(
                        state.ap[:, col:col + 1], s.ap[:, Tc - 1:Tc]),
                        reads=[stl], pwrites=[state], tag="sc_cy", sync_self=True)
                    et = e4t[n % 2]
                    p.op("dve", lambda e, s=stl, o=et, n=n: e.tensor_tensor(
                        o.ap, s.ap, cbc[n].ap, OP.mult),
                        reads=[stl, cbc[n]], writes=[et], tag="e4m")
                    if n == 1:
                        p.op("pool", lambda e, t=et: e.tensor_copy(yfin[0].ap, t.ap),
                             reads=[et], writes=[yfin[0]], tag="yo0")
                    elif n % 2 == 1:
                        p.op("pool", lambda e, t=et: e.tensor_tensor(
                            yfin[0].ap, yfin[0].ap, t.ap, OP.add),
                            reads=[et, yfin[0]], pwrites=[yfin[0]], tag="yoadd")
                    else:
                        p.op("dve", lambda e, t=et: e.tensor_tensor(
                            yacc.ap, yacc.ap, t.ap, OP.add),
                            reads=[et, yacc], pwrites=[yacc], tag="yeadd")
                nmb = SLM // 128
                zs, zmb = dblk // nmb, dblk % nmb
                zsl = slab[zs % 2]
                if zmb == 0:
                    p.op("sp", lambda e, d=zsl, s=zs: e.dma_start(d.ap, winz_bf[s]),
                         reads=[castw], writes=[zsl], dma=True, tag="zslab")
                pgz = psg[2 + dblk % 2]
                for kb in range(KB):
                    mm(pgz, zsl, (kb * SLM + zmb * 128, kb * SLM + (zmb + 1) * 128),
                       u_f, (kb * Tc, (kb + 1) * Tc), kb == 0, kb == KB - 1, tag="zmm")
                sz = szb[dblk % 2]
                z1, z2 = sfp[1], sfp[2]
                p.op("act", lambda e, pg=pgz, o=z1: e.activation(o.ap, pg.ap, AF.Tanh, scale=0.5),
                     reads=[pgz], writes=[z1], tag="ztanh")
                p.op("dve", lambda e, a=z1, o=z2: e.tensor_scalar(
                    o.ap, a.ap, 0.5, 0.5, OP.mult, OP.add),
                    reads=[z1], writes=[z2], tag="zsig")
                p.op("dve", lambda e, pg=pgz, a=z2, o=sz: e.tensor_tensor(o.ap, pg.ap, a.ap, OP.mult),
                     reads=[pgz, z2], writes=[sz], tag="zmul")
                yf = sfp[0]
                p.op("dve", lambda e, o=yf: e.tensor_tensor(
                    o.ap, yacc.ap, yfin[0].ap, OP.add),
                    reads=[yacc, yfin[0]], writes=[yf], tag="ysum")
                p.op("dve", lambda e, a=yf, b=sz, dblk=dblk: e.tensor_tensor(
                    xcy[dblk].ap, a.ap, b.ap, OP.mult),
                    reads=[yf, sz], writes=[xcy[dblk]], tag="ymul")
            tap(f"ybf_{sqi}_{ci}", xcy, xcy_all_ap)
            p.handoff(M23set, M4set)
            if stop == "M3":
                p.handoff(Mall + Fall, Mall)
                continue

            nmb = SLM // 128
            for s in range(OSL):
                for kh in range(DB // WKH):
                    wsl = wo_slab[kh % 2]
                    p.op("sp", lambda e, d=wsl, s=s, kh=kh: e.dma_start(d.ap, wout_bf[s, kh]),
                         reads=[castw], writes=[wsl], dma=True, tag="woslab")
                    for mb in range(nmb):
                        pg = psg[mb]
                        for kk in range(WKH):
                            kb = kh * WKH + kk
                            mm(pg, wsl, (kk * SLM + mb * 128, kk * SLM + (mb + 1) * 128),
                               xcy[kb], (0, Tc), kb == 0, kb == DB - 1, tag="womm")
                for mb in range(nmb):
                    mo = s * nmb + mb
                    p.op("act", lambda e, pg=psg[mb], mo=mo: e.activation(
                        mo_bf[mo].ap, pg.ap, AF.Identity,
                        bias=bout_r.ap[:, mo:mo + 1]),
                        reads=[psg[mb], bout_r], writes=[mo_bf[mo]], tag="wo_ev")
            tap(f"mobf_{sqi}_{ci}", mo_bf, mo_bf[0].ap)
            for tt in range(TT):
                mt = mobt[tt % 2]
                for mo in range(MBo):
                    transpose_to(mt, (mo * 128, (mo + 1) * 128),
                                 mo_bf[mo], (tt * 128, (tt + 1) * 128),
                                 ptb[mo % 2], "trmo")
                tap(f"mot_{sqi}_{ci}_{tt}", [mt], mt.ap)
                xrb = xr[tt % 2]
                dma(xrb, x_in[t0 + tt * 128: t0 + (tt + 1) * 128, :], tag="xreload")
                xo = x1t[tt % 2]
                p.op("dve", lambda e, a=xrb, b=mt, o=xo: e.tensor_tensor(
                    o.ap, a.ap, b.ap, OP.add), reads=[xrb, mt], writes=[xo], tag="res1")
                tap(f"x1t_{sqi}_{ci}_{tt}", [xo], xo.ap)
                p.op("sp", lambda e, o=xo, tt=tt, t0=t0: e.dma_start(
                    x1_dram.ap[t0 + tt * 128: t0 + (tt + 1) * 128, :], o.ap),
                    reads=[xo], pwrites=[x1_dram], dma=True, tag="x1w")
            tap(f"x1_{sqi}_{ci}", [x1_dram], x1_dram.ap[t0:t0 + Tc, :])
            p.handoff(Mall, Fall)
            if stop == "M4":
                p.handoff(Mall + Fall, Mall)
                continue

            for tt in range(TT):
                xb, ub, st = xf1[tt % 2], u2t[tt % 2], lnt[tt % 2]
                p.op("sp", lambda e, d=xb, tt=tt, t0=t0: e.dma_start(
                    d.ap, x1_dram.ap[t0 + tt * 128: t0 + (tt + 1) * 128, :]),
                    reads=[x1_dram], writes=[xb], dma=True, tag="x1r")
                ln_phase(xb, ub, st, "l2")
                for kb in range(KB):
                    transpose_to(u_f, (kb * Tc + tt * 128, kb * Tc + (tt + 1) * 128),
                                 ub, (kb * 128, (kb + 1) * 128), ptb[kb % 2], "tru2")
            tap(f"u2f_{sqi}_{ci}", [u_f], u_f.ap)
            p.handoff(F0set, F23set)
            if stop == "F0":
                p.handoff(Mall + Fall, Mall)
                continue

            for dst, src in ((gate_r, gate_in), (rgT_r, rgT_in), (ruT_r, ruT_in),
                             (rdT_r, rdT_in), (srgT_r, srgT_in), (sruT_r, sruT_in),
                             (sugT_r, sugT_in), (suuT_r, suuT_in), (sudT_r, sudT_in),
                             (scgT_r, scgT_in), (scuT_r, scuT_in), (srdT_r, srdT_in),
                             (scdT_r, scdT_in)):
                p.op("pool", lambda e, d=dst, s=src: e.dma_start(d.ap, s),
                     writes=[dst], dma=True, tag="fwt")
            for tt in range(TT):
                pg = psg[4]
                for kb in range(KB):
                    mm(pg, u_f, (kb * Tc + tt * 128, kb * Tc + (tt + 1) * 128),
                       gate_r, (kb * E, (kb + 1) * E), kb == 0, kb == KB - 1,
                       pslice=(0, 128, 0, E), tag="gmm")
                es, tcu, tcv, gs = esb[tt % 2], tcur[tt % 2], tc2[tt % 2], gsm[tt % 2]
                p.op("dve", lambda e, pg=pg, g=gs: e.reduce_max(
                    g.ap[:, 0:1], pg.ap[:, 0:E], axis=AX.X, negate=True),
                    reads=[pg], writes=[gs], tag="g_max", small=True)
                p.op("act", lambda e, pg=pg, g=gs, o=es: e.activation(
                    o.ap, pg.ap[:, 0:E], AF.Exp, bias=g.ap[:, 0:1]),
                    reads=[pg, gs], writes=[es], tag="g_exp", small=True)
                p.op("dve", lambda e, a=es, o=tcu: e.tensor_copy(o.ap, a.ap),
                     reads=[es], writes=[tcu], tag="g_cp", small=True)
                cur, nxt = tcu, tcv
                for it in range(3):
                    p.op("dve", lambda e, t=cur, g=gs: e.reduce_max(
                        g.ap[:, 1:2], t.ap, axis=AX.X), reads=[cur], pwrites=[gs], tag="g_m", small=True)
                    p.op("dve", lambda e, t=cur, g=gs: e.tensor_scalar(
                        sqs.ap[:, 0:E], t.ap, g.ap[:, 1:2], None, OP.is_equal),
                        reads=[cur, gs], writes=[sqs], tag="g_eq", small=True)
                    p.op("dve", lambda e, t=cur, o=nxt: e.scalar_tensor_tensor(
                        o.ap, sqs.ap[:, 0:E], NEG_BIG, t.ap, OP.mult, OP.add),
                        reads=[sqs, cur], writes=[nxt], tag="g_rm", small=True)
                    cur, nxt = nxt, cur
                p.op("dve", lambda e, t=cur, g=gs: e.reduce_max(
                    g.ap[:, 1:2], t.ap, axis=AX.X), reads=[cur], pwrites=[gs], tag="g_m4", small=True)
                p.op("dve", lambda e, a=es, g=gs: e.tensor_scalar(
                    sqs.ap[:, 0:E], a.ap, g.ap[:, 1:2], None, OP.is_le),
                    reads=[es, gs], writes=[sqs], tag="g_le", small=True)
                p.op("dve", lambda e, a=es, g=gs: e.tensor_scalar(
                    sqs.ap[:, E:2 * E], a.ap, g.ap[:, 1:2], None, OP.is_equal),
                    reads=[es, gs], pwrites=[sqs], tag="g_eq4", small=True)
                p.op("dve", lambda e: e.tensor_tensor(
                    sqs.ap[:, 2 * E:3 * E], sqs.ap[:, E:2 * E], sqs.ap[:, 0:E], OP.subtract),
                    reads=[sqs], pwrites=[sqs], tag="g_k1", small=True)
                p.op("dve", lambda e, a=es, t=cur: e.scalar_tensor_tensor(
                    t.ap, sqs.ap[:, 2 * E:3 * E], 1.0, a.ap, OP.add, OP.mult),
                    reads=[sqs, es], writes=[cur], tag="g_wr", small=True)
                p.op("dve", lambda e, t=cur, g=gs: e.reduce_sum(
                    g.ap[:, 2:3], t.ap, axis=AX.X), reads=[cur], pwrites=[gs], tag="g_s4", small=True)
                p.op("dve", lambda e, g=gs: e.tensor_scalar(
                    g.ap[:, 3:4], g.ap[:, 2:3], 1e-20, None, OP.add),
                    reads=[gs], pwrites=[gs], tag="g_eps", small=True)
                p.op("dve", lambda e, g=gs: e.reciprocal(g.ap[:, 4:5], g.ap[:, 3:4]),
                     reads=[gs], pwrites=[gs], tag="g_rcp", small=True)
                p.op("dve", lambda e, t=cur, g=gs, tt=tt: e.tensor_scalar(
                    we.ap[:, tt * E:(tt + 1) * E], t.ap, g.ap[:, 4:5], None, OP.mult),
                    reads=[cur, gs], pwrites=[we], tag="g_we", small=True)
            wde = wall_dram[ci % 2]
            for tt in range(TT):
                p.op("pe", lambda e, tt=tt: e.transpose(
                    ptf.ap[0:E, 0:128], we.ap[:, tt * E:(tt + 1) * E], idf_r.ap),
                    reads=[we, idf_r], writes=[ptf], tag="trwe")
                p.op("dve", lambda e, tt=tt: e.tensor_copy(
                    sqs.ap[0:E, tt * 128:(tt + 1) * 128], ptf.ap[0:E, 0:128]),
                    reads=[ptf], pwrites=[sqs], tag="trwe_ev")
            p.op("sp", lambda e, d=wde.ap: e.dma_start(d, sqs.ap[0:E, 0:Tc]),
                 reads=[sqs], writes=[wde], dma=True, tag="wall_w")
            tap(f"we_{sqi}_{ci}", [we], we.ap)
            if stop == "F1":
                p.handoff(Mall + Fall, Mall)
                continue

            def kproj(wbuf, dstbuf, pgi, tag):
                pg = psg[pgi]
                for kb in range(KB):
                    mm(pg, wbuf, (kb * 128, (kb + 1) * 128), u_f, (kb * Tc, (kb + 1) * Tc),
                       kb == 0, kb == KB - 1, tag=tag)
                ev_copy(dstbuf, pg, tag=tag + "ev")
                return pg

            kproj(rgT_r, rg_bf, 0, "rg")
            kproj(ruT_r, ru_bf, 1, "ru")
            kproj(srgT_r, h1a[0], 2, "srg")
            pg = psg[2]
            mm(pg, sugT_r, (0, 128), h1a[0], (0, Tc), True, True, tag="sug")
            ev_copy(h1a[1], pg, tag="sugev")
            kproj(sruT_r, h1b[0], 3, "sru")
            pg = psg[3]
            mm(pg, suuT_r, (0, 128), h1b[0], (0, Tc), True, True, tag="suu")
            ev_copy(h1b[1], pg, tag="suuev")
            HF = SIB // 2
            pgr = psg[4]
            for hf in range(2):
                for ib in range(HF):
                    sib = hf * HF + ib
                    pg = psg[ib % 2]
                    mm(pg, scgT_r, (sib * 128, (sib + 1) * 128), h1a[1], (0, Tc), True, True, tag="scg")
                    p.op("act", lambda e, pg=pg, ib=ib: e.activation(
                        gtile[ib].ap, pg.ap, AF.Silu),
                        reads=[pg], writes=[gtile[ib]], tag="scgsilu")
                for ib in range(HF):
                    sib = hf * HF + ib
                    pg = psg[ib % 2]
                    mm(pg, scuT_r, (sib * 128, (sib + 1) * 128), h1b[1], (0, Tc), True, True, tag="scu")
                    gb = gut[ib % 2]
                    p.op("dve", lambda e, pg=pg, ib=ib, o=gb: e.scalar_tensor_tensor(
                        o.ap, pg.ap, 1.0, gtile[ib].ap, OP.mult, OP.mult),
                        reads=[pg, gtile[ib]], writes=[gb], tag="sgu")
                    mm(pgr, srdT_r, (sib * 128, (sib + 1) * 128), gb, (0, Tc),
                       sib == 0, sib == SIB - 1, tag="srd")
            ev_copy(h1a[0], pgr, tag="srdev")
            pg = psg[2]
            mm(pg, sudT_r, (0, 128), h1a[0], (0, Tc), True, True, tag="sud")
            ev_copy(h2sh, pg, tag="sudev")
            tap(f"h2sh_{sqi}_{ci}", [h2sh], h2sh.ap)
            if stop == "F2":
                p.handoff(Mall + Fall, Mall)
                continue

            rot = [0]

            def npg():
                pg = psg[rot[0] % 4]
                rot[0] += 1
                return pg

            for ei in range(E):
                ew = expwt[ei % 2]
                p.op("sp", lambda e, d=ew, ei=ei: e.dma_start(d.ap, expw_bf[ei]),
                     reads=[castw], writes=[ew], dma=True, tag="expw")
                wb = wbc[ei % 2]
                p.op("sp", lambda e, d=wb, ei=ei, w=wde.ap: e.dma_start(
                    d.ap, w[ei:ei + 1, :].partition_broadcast(128).squeeze(1)),
                    reads=[wde], writes=[wb], dma=True, tag="wbcast")
                h1g, h1u = h1a[ei % 2], h1b[ei % 2]
                pg = npg()
                mm(pg, ew, (0, 128), rg_bf, (0, Tc), True, True, tag="ug")
                ev_copy(h1g, pg, tag="ugev")
                pg = npg()
                mm(pg, ew, (FI + 128, FI + 256), ru_bf, (0, Tc), True, True, tag="uu")
                ev_copy(h1u, pg, tag="uuev")
                for ib in range(IB):
                    pg = npg()
                    mm(pg, ew, (128 + ib * 128, 128 + (ib + 1) * 128), h1g, (0, Tc),
                       True, True, tag="cg")
                    p.op("act", lambda e, pg=pg, ib=ib: e.activation(
                        gtile[ib].ap, pg.ap, AF.Silu),
                        reads=[pg], writes=[gtile[ib]], tag="cgsilu")
                pgrd = psg[4] if ei % 2 == 0 else ptf
                for ib in range(IB):
                    pg = npg()
                    mm(pg, ew, (FI + 256 + ib * 128, FI + 256 + (ib + 1) * 128), h1u, (0, Tc),
                       True, True, tag="cu")
                    gb = gut[ib % 2]
                    p.op("dve", lambda e, pg=pg, ib=ib, o=gb: e.scalar_tensor_tensor(
                        o.ap, pg.ap, 1.0, gtile[ib].ap, OP.mult, OP.mult),
                        reads=[pg, gtile[ib]], writes=[gb], tag="gu")
                    mm(pgrd, rdT_r, (ib * 128, (ib + 1) * 128), gb, (0, Tc),
                       ib == 0, ib == IB - 1, tag="rdmm")
                rw = rdw[ei % 2]
                p.op("dve", lambda e, pg=pgrd, o=rw, w=wb: e.scalar_tensor_tensor(
                    o.ap, pg.ap, 1.0, w.ap, OP.mult, OP.mult),
                    reads=[pgrd, wb], writes=[rw], tag="rdwm")
                pg = npg()
                mm(pg, ew, (2 * FI + 256, 2 * FI + 384), rw, (0, Tc), True, True, tag="ud")
                p.op("act", lambda e, pg=pg, ei=ei: e.copy(
                    h2e[ei].ap, pg.ap), reads=[pg], writes=[h2e[ei]], tag="h2ev")
            tap(f"h2e_{sqi}_{ci}", h2e, h2e[0].ap)
            p.handoff(F23set + F0set, F4set)
            if stop == "F3":
                p.handoff(Mall + Fall, Mall)
                continue

            for mo in range(MBo):
                cd = cdt[mo % 2]
                p.op("sp", lambda e, d=cd, mo=mo: e.dma_start(d.ap, cdmb_bf[mo]),
                     reads=[castw], writes=[cd], dma=True, tag="cdT")
                pg = psg[mo % 2]
                for ei in range(E):
                    mm(pg, cd, (ei * 128, (ei + 1) * 128), h2e[ei], (0, Tc),
                       ei == 0, False, tag="cdmm")
                mm(pg, scdT_r, (mo * 128, (mo + 1) * 128), h2sh, (0, Tc), False, True, tag="scdmm")
                p.op("act", lambda e, pg=pg, mo=mo: e.copy(
                    fo_bf[mo].ap, pg.ap), reads=[pg], writes=[fo_bf[mo]], tag="foev")
            for tt in range(TT):
                ft = fot[tt % 2]
                for mo in range(MBo):
                    transpose_to(ft, (mo * 128, (mo + 1) * 128),
                                 fo_bf[mo], (tt * 128, (tt + 1) * 128),
                                 ptb[mo % 2], "trfo")
                xb = xf4[tt % 2]
                p.op("sp", lambda e, d=xb, tt=tt, t0=t0: e.dma_start(
                    d.ap, x1_dram.ap[t0 + tt * 128: t0 + (tt + 1) * 128, :]),
                    reads=[x1_dram], writes=[xb], dma=True, tag="x1r4")
                ob = ot[tt % 2]
                p.op("dve", lambda e, a=xb, b=ft, o=ob: e.tensor_tensor(
                    o.ap, a.ap, b.ap, OP.add), reads=[xb, ft], writes=[ob], tag="res2")
                p.op("sp", lambda e, o=ob, tt=tt, t0=t0: e.dma_start(
                    out_hbm[t0 + tt * 128: t0 + (tt + 1) * 128, :], o.ap),
                    reads=[ob], dma=True, tag="outw")
            p.handoff(Fall, Mall)

    ndma_sp = sum(1 for o in p.ops["sp"] if o["h"].dma)
    sems, dsems = p.emit(nc)
    with nc.Block() as blk:
        @blk.sync
        def _(sync):
            for lane in range(NLANES):
                cnt = ndma_sp // NLANES + (1 if lane < ndma_sp % NLANES else 0)
                if cnt:
                    sync.wait_ge(dsems["sp"][lane], 16 * cnt)

    return nc




def kernel(**inputs):
    cfg = make_cfg()
    N_CORES = 8
    B = 16
    BPC = B // N_CORES

    x = np.asarray(inputs["L0_attn_hidden_states"], np.float32)
    packed = pack_host(cfg, {k: np.asarray(v, np.float32) for k, v in inputs.items()
                             if k != "L0_attn_hidden_states"})

    nc = build_program(cfg)

    in_maps = []
    for c in range(N_CORES):
        m = dict(packed)
        xs = x[c * BPC:(c + 1) * BPC].reshape(BPC * cfg["SEQ"], cfg["D"])
        m["x"] = np.ascontiguousarray(xs)
        in_maps.append(m)

    res = run_bass_kernel_spmd(nc, in_maps, core_ids=list(range(N_CORES)))
    outs = [np.asarray(res.results[c]["out"]).reshape(BPC, cfg["SEQ"], cfg["D"])
            for c in range(N_CORES)]
    return np.concatenate(outs, axis=0)


# version 14
# speedup vs baseline: 1.0424x; 1.0424x over previous
import numpy as np
import concourse.bass as bass
import concourse.mybir as mybir
from concourse.bass_utils import run_bass_kernel_spmd
from concourse.alu_op_type import AluOpType as OP

F32 = mybir.dt.float32
BF16 = mybir.dt.bfloat16
AF = mybir.ActivationFunctionType
AX = mybir.AxisListType

NEG_BIG = -1.0e30
NLANES = 8

ENGINES = ("pe", "act", "dve", "pool", "sp")
SECTION = dict(pe="tensor", act="scalar", dve="vector", pool="gpsimd", sp="sync")


class H:
    __slots__ = ("eng", "idx", "dma", "rank", "lane")

    def __init__(self, eng, idx, dma):
        self.eng, self.idx, self.dma = eng, idx, dma
        self.rank = 0
        self.lane = 0


class Buf:
    __slots__ = ("ap", "writers", "readers", "name")

    def __init__(self, ap, name=""):
        self.ap = ap
        self.writers = []
        self.readers = []
        self.name = name


class Prog:
    def __init__(self, sync_all=False):
        self.ops = {e: [] for e in ENGINES}
        self.sync_all = sync_all
        self.dma_counts = {e: 0 for e in ENGINES}

    @staticmethod
    def _reduce(handles):
        best = {}
        for h in handles:
            key = (h.eng, h.dma, h.lane)
            if key not in best or h.idx > best[key].idx:
                best[key] = h
        return list(best.values())

    def op(self, eng, fn, reads=(), writes=(), pwrites=(), dma=False, tag="", small=False, sync_self=False, after=()):
        deps = list(after)
        for b in reads:
            deps += b.writers
        for b in writes:
            deps += b.writers + b.readers
        for b in pwrites:
            deps += b.writers + b.readers
        h = H(eng, len(self.ops[eng]), dma)
        if dma:
            h.lane = self.dma_counts[eng] % NLANES
            self.dma_counts[eng] += 1
        if len(deps) > 12:
            deps = Prog._reduce(deps)
        self.ops[eng].append(dict(fn=fn, deps=deps, h=h, tag=tag, marked=False,
                                  small=small, sync_self=sync_self))
        for b in reads:
            b.readers.append(h)
        for b in writes:
            b.writers = [h]
            b.readers = []
        for b in pwrites:
            b.writers.append(h)
            b.readers = []
        return h

    def wait_only(self, eng, deps, tag="w"):
        h = H(eng, len(self.ops[eng]), False)
        self.ops[eng].append(dict(fn=None, deps=list(deps), h=h, tag=tag, marked=False,
                                  small=False, sync_self=False))
        return h

    @staticmethod
    def handoff(outs, ins):
        acc = []
        for b in outs:
            acc += b.writers
            acc += b.readers
        deps = Prog._reduce(acc)
        for nb in ins:
            nb.writers = Prog._reduce(nb.writers + deps)

    def barrier(self, tag="bar"):
        lasts = []
        for e in ENGINES:
            lastc = None
            lastd = {}
            nd = 0
            for o in self.ops[e]:
                if o["h"].dma:
                    lastd[nd % NLANES] = o["h"]
                    nd += 1
                elif o["fn"] is not None:
                    lastc = o["h"]
            if lastc is not None:
                lasts.append(lastc)
            lasts += list(lastd.values())
        for e in ENGINES:
            deps = [h for h in lasts if not (h.eng == e and not h.dma)]
            if deps:
                self.wait_only(e, deps, tag=tag)

    def emit(self, nc):
        for e in ENGINES:
            lane_cnt = [0] * NLANES
            for o in self.ops[e]:
                if o["h"].dma:
                    lane = o["h"].lane
                    lane_cnt[lane] += 1
                    o["h"].rank = (lane, lane_cnt[lane])
        for e in ENGINES:
            for o in self.ops[e]:
                byeng = {}
                for d in o["deps"]:
                    if (d.eng == e and not d.dma and not self.ops[e][d.idx]["small"]
                            and not o["sync_self"] and not self.sync_all):
                        continue
                    key = (d.eng, d.dma, d.rank[0] if d.dma else 0)
                    if key not in byeng or d.idx > byeng[key].idx:
                        byeng[key] = d
                o["wdeps"] = list(byeng.values())
                for d in o["wdeps"]:
                    self.ops[d.eng][d.idx]["marked"] = True
        for e in ENGINES:
            r = 0
            for o in self.ops[e]:
                if not o["h"].dma and o["marked"]:
                    r += 1
                    o["h"].rank = r
        sems = {e: nc.alloc_semaphore(f"sem_{e}") for e in ENGINES}
        dsems = {e: [nc.alloc_semaphore(f"dsem_{e}{i}") for i in range(NLANES)]
                 for e in ("sp", "pool")}

        with nc.Block() as blk:
            for e in ENGINES:

                def make(e):
                    def body(eng):
                        hw = {}
                        for o in self.ops[e]:
                            for d in o["wdeps"]:
                                if d.dma:
                                    lane, cnt = d.rank
                                    key = ("d", d.eng, lane)
                                    val = 16 * cnt
                                    sem = dsems[d.eng][lane]
                                else:
                                    key = ("c", d.eng)
                                    val = d.rank
                                    sem = sems[d.eng]
                                if hw.get(key, 0) < val:
                                    eng.wait_ge(sem, val)
                                    hw[key] = val
                            if o["fn"] is not None:
                                if o["h"].dma:
                                    lane, cnt = o["h"].rank
                                    if cnt >= 2:
                                        key = ("d", e, lane)
                                        val = 16 * (cnt - 1)
                                        if hw.get(key, 0) < val:
                                            eng.wait_ge(dsems[e][lane], val)
                                            hw[key] = val
                                inst = o["fn"](eng)
                                if o["h"].dma:
                                    inst.then_inc(dsems[e][o["h"].rank[0]], 16)
                                elif o["marked"]:
                                    inst.then_inc(sems[e], 1)

                    return body

                getattr(blk, SECTION[e])(make(e))
        return sems, dsems


class Carve:
    def __init__(self, ap, cap, unit):
        self.ap_full = ap
        self.cap = cap
        self.off = 0
        self.hi = 0
        self.stack = []
        self.unit = unit

    def take(self, n, name=""):
        assert self.off + n <= self.cap, f"arena overflow {name}: {self.off}+{n}>{self.cap}"
        ap = self.ap_full[:, self.off:self.off + n]
        self.off += n
        self.hi = max(self.hi, self.off)
        return Buf(ap, name)

    def push(self):
        self.stack.append(self.off)

    def pop(self):
        self.off = self.stack.pop()




def make_cfg(D=2048, DI=4096, NST=16, RDT=128, KCV=4, SEQ=2048, NSEQ=2, Tc=512,
             E=32, FI=1408, SI=2816, taps=(), stop_after=None):
    c = dict(D=D, DI=DI, NST=NST, RDT=RDT, KCV=KCV, SEQ=SEQ, NSEQ=NSEQ, Tc=Tc,
             E=E, FI=FI, SI=SI, taps=set(taps), stop_after=stop_after)
    c["KB"] = D // 128
    c["DB"] = DI // 128
    c["MBo"] = D // 128
    c["IB"] = FI // 128
    c["SIB"] = SI // 128
    c["NCH"] = SEQ // Tc
    c["TT"] = Tc // 128
    c["SLM"] = Tc
    c["XSL"] = DI // Tc
    c["OSL"] = D // Tc
    c["WKH"] = max(1, c["DB"] // 4)
    return c


def _kblock(a):
    K, M = a.shape
    return np.ascontiguousarray(
        a.reshape(K // 128, 128, M).transpose(1, 0, 2).reshape(128, (K // 128) * M)
    )


def pack_host(cfg, w):
    D, DI, NST, RDT = cfg["D"], cfg["DI"], cfg["NST"], cfg["RDT"]
    E = cfg["E"]
    KB, DB, MBo = cfg["KB"], cfg["DB"], cfg["MBo"]
    SLM, XSL, OSL, WKH = cfg["SLM"], cfg["XSL"], cfg["OSL"], cfg["WKH"]

    out = {}
    W_in = w["L0_attn_W_in"]

    def slabify(Wrows):
        a = Wrows.reshape(XSL, SLM, KB, 128)
        return np.ascontiguousarray(a.transpose(0, 3, 2, 1))

    out["winx"] = slabify(W_in[:DI])
    out["winz"] = slabify(W_in[DI:])

    WoT = np.ascontiguousarray(w["L0_attn_W_out"].T)
    a = WoT.reshape(DB // WKH, WKH, 128, OSL, SLM)
    out["wout"] = np.ascontiguousarray(a.transpose(3, 0, 2, 1, 4))

    W_x = w["L0_attn_W_x"]
    out["wxdt"] = _kblock(np.ascontiguousarray(W_x[:RDT].T))
    out["wxbc"] = _kblock(np.ascontiguousarray(W_x[RDT:].T))
    out["wdt"] = _kblock(np.ascontiguousarray(w["L0_attn_W_dt"].T))

    def chan(v):
        return np.ascontiguousarray(v.reshape(DB, 128).T)

    out["convw"] = np.ascontiguousarray(
        w["L0_attn_conv_w"].reshape(DB, 128, cfg["KCV"]).transpose(1, 0, 2).reshape(128, DB * cfg["KCV"])
    )
    out["convb"] = chan(w["L0_attn_conv_b"])
    out["bdt"] = chan(w["L0_attn_b_dt"])
    out["dchan"] = chan(w["L0_attn_D"])
    out["bout"] = np.ascontiguousarray(w["L0_attn_b_out"].reshape(MBo, 128).T)
    out["alog"] = np.ascontiguousarray(
        w["L0_attn_A_log"].reshape(DB, 128, NST).transpose(1, 0, 2).reshape(128, DB * NST)
    )

    out["gate"] = _kblock(np.ascontiguousarray(w["L0_ffn_gate_w"].T))
    out["rgT"] = _kblock(np.ascontiguousarray(w["L0_ffn_Rg"].T))
    out["ruT"] = _kblock(np.ascontiguousarray(w["L0_ffn_Ru"].T))
    out["rdT"] = _kblock(np.ascontiguousarray(w["L0_ffn_Rd"].T))
    out["srgT"] = _kblock(np.ascontiguousarray(w["L0_ffn_s_Rg"].T))
    out["sruT"] = _kblock(np.ascontiguousarray(w["L0_ffn_s_Ru"].T))
    out["sugT"] = np.ascontiguousarray(w["L0_ffn_s_Ug"].T)
    out["suuT"] = np.ascontiguousarray(w["L0_ffn_s_Uu"].T)
    out["sudT"] = np.ascontiguousarray(w["L0_ffn_s_Ud"].T)
    out["scgT"] = np.ascontiguousarray(w["L0_ffn_s_Cg"].T)
    out["scuT"] = np.ascontiguousarray(w["L0_ffn_s_Cu"].T)
    out["srdT"] = _kblock(np.ascontiguousarray(w["L0_ffn_s_Rd"].T))
    out["scdT"] = np.ascontiguousarray(w["L0_ffn_s_Cd"].T)

    expw = np.concatenate(
        [
            w["L0_ffn_Ug"].transpose(0, 2, 1),
            w["L0_ffn_Cg"].transpose(0, 2, 1),
            w["L0_ffn_Uu"].transpose(0, 2, 1),
            w["L0_ffn_Cu"].transpose(0, 2, 1),
            w["L0_ffn_Ud"].transpose(0, 2, 1),
        ],
        axis=2,
    )
    out["expw"] = np.ascontiguousarray(expw)
    CdT = w["L0_ffn_Cd"].transpose(0, 2, 1)
    cdmb = CdT.reshape(E, 128, MBo, 128).transpose(2, 1, 0, 3).reshape(MBo, 128, E * 128)
    out["cdmb"] = np.ascontiguousarray(cdmb)

    out["ident"] = np.eye(128, dtype=np.float32)
    return out




def build_program(cfg):
    D, DI, NST, RDT, KCV = cfg["D"], cfg["DI"], cfg["NST"], cfg["RDT"], cfg["KCV"]
    SEQ, NSEQ, Tc = cfg["SEQ"], cfg["NSEQ"], cfg["Tc"]
    E, FI, SI = cfg["E"], cfg["FI"], cfg["SI"]
    KB, DB, MBo, IB, SIB = cfg["KB"], cfg["DB"], cfg["MBo"], cfg["IB"], cfg["SIB"]
    NCH, TT, SLM, XSL, OSL, WKH = (
        cfg["NCH"], cfg["TT"], cfg["SLM"], cfg["XSL"], cfg["OSL"], cfg["WKH"]
    )
    EXPC = 2 * FI + 3 * 128
    TOK = NSEQ * SEQ
    stop = cfg["stop_after"]

    nc = bass.Bass("TRN2", target_bir_lowering=False)
    p = Prog(sync_all=cfg.get("sync_all", False))

    def din(name, shape, dtype=F32):
        return nc.dram_tensor(name, list(shape), dtype, kind="ExternalInput").ap()

    x_in = din("x", (TOK, D))
    winx = din("winx", (XSL, 128, KB * SLM))
    winz = din("winz", (XSL, 128, KB * SLM))
    wout_in = din("wout", (OSL, DB // WKH, 128, WKH * SLM))
    wxdt_in = din("wxdt", (128, DB * RDT))
    wxbc_in = din("wxbc", (128, DB * 2 * NST))
    wdt_in = din("wdt", (128, DB * 128))
    convw_in = din("convw", (128, DB * KCV))
    convb_in = din("convb", (128, DB))
    bdt_in = din("bdt", (128, DB))
    dchan_in = din("dchan", (128, DB))
    bout_in = din("bout", (128, MBo))
    alog_in = din("alog", (128, DB * NST))
    gate_in = din("gate", (128, KB * E))
    rgT_in = din("rgT", (128, KB * 128))
    ruT_in = din("ruT", (128, KB * 128))
    rdT_in = din("rdT", (128, IB * 128))
    srgT_in = din("srgT", (128, KB * 128))
    sruT_in = din("sruT", (128, KB * 128))
    sugT_in = din("sugT", (128, 128))
    suuT_in = din("suuT", (128, 128))
    sudT_in = din("sudT", (128, 128))
    scgT_in = din("scgT", (128, SI))
    scuT_in = din("scuT", (128, SI))
    srdT_in = din("srdT", (128, SIB * 128))
    scdT_in = din("scdT", (128, D))
    expw_in = din("expw", (E, 128, EXPC))
    cdmb_in = din("cdmb", (MBo, 128, E * 128))
    ident_in = din("ident", (128, 128))

    out_hbm = nc.dram_tensor("out", [TOK, D], F32, kind="ExternalOutput").ap()

    def scr(name, shape, dtype=BF16):
        return nc.dram_tensor(name, list(shape), dtype, kind="Internal").ap()

    winx_bf = scr("winx_bf", (XSL, 128, KB * SLM))
    winz_bf = scr("winz_bf", (XSL, 128, KB * SLM))
    wout_bf = scr("wout_bf", (OSL, DB // WKH, 128, WKH * SLM))
    expw_bf = scr("expw_bf", (E, 128, EXPC))
    cdmb_bf = scr("cdmb_bf", (MBo, 128, E * 128))
    bc_dram = [Buf(scr(f"bc_dram{i}", (2 * NST, Tc)), f"bcd{i}") for i in range(2)]
    wall_dram = [Buf(scr(f"wall_dram{i}", (E, Tc), F32), f"wal{i}") for i in range(2)]
    x1_dram = Buf(scr("x1_dram", (TOK, D), F32), "x1d")

    def sbp(name, cols, dtype=F32):
        t = nc.alloc_sbuf_tensor(name, [128, cols], dtype)
        return Buf(t.ap(), name)

    wdt_r = sbp("wdt_r", DB * 128, BF16)
    wxdt_r = sbp("wxdt_r", DB * RDT, BF16)
    wxbc_r = sbp("wxbc_r", DB * 2 * NST, BF16)
    idb_r = sbp("idb_r", 128, BF16)
    u_f = sbp("u_f", KB * Tc, BF16)
    convw_r = sbp("convw_r", DB * KCV)
    convb_r = sbp("convb_r", DB)
    bdt_r = sbp("bdt_r", DB)
    dchan_r = sbp("dchan_r", DB)
    bout_r = sbp("bout_r", MBo)
    a_r = sbp("a_r", DB * NST)
    idf_r = sbp("idf_r", 128)
    state = sbp("state", DB * NST)
    carry = sbp("carry", DB * (KCV - 1))
    lnt = [sbp(f"lnt{i}", 8) for i in range(2)]
    cst = sbp("cst", 2)

    FFNWT_COLS = (KB * E + 2 * KB * 128 + IB * 128 + 2 * KB * 128 + 3 * 128
                  + 2 * SI + SIB * 128 + D)
    M_BF = DB * Tc + 2 * KB * SLM + max(
        2 * D,
        2 * Tc + NST * Tc + NST * Tc,
        2 * WKH * SLM + MBo * Tc + 2 * D)
    F_BF = E * Tc + FFNWT_COLS + 3 * Tc + max(
        2 * D,
        4 * Tc + Tc + max(IB, SIB // 2) * Tc + 2 * Tc + 2 * Tc + 2 * EXPC,
        2 * E * 128 + MBo * Tc + 2 * D)
    NBF = max(M_BF, F_BF) + 64
    M_F32 = max(2 * D + 2 * (Tc + KCV) + 4 * Tc,
                20 * Tc,
                4 * D)
    F_F32 = (TT * E + 6 * E + 16 + 2 * Tc) + max(2 * D, 4 * D)
    NF32 = 2048 + max(M_F32, F_F32) + 32
    arb_t = nc.alloc_sbuf_tensor("arena_bf", [128, NBF], BF16)
    arf_t = nc.alloc_sbuf_tensor("arena_f32", [128, NF32], F32)
    bfar = Carve(arb_t.ap(), NBF, 2)
    far = Carve(arf_t.ap(), NF32, 4)

    sqs = far.take(2048, "sqs")

    bfar.push()
    far.push()
    xcy_all_ap = bfar.ap_full[:, bfar.off:bfar.off + DB * Tc]
    xcy = [Buf(xcy_all_ap[:, i * Tc:(i + 1) * Tc], f"xcy{i}") for i in range(DB)]
    bfar.off += DB * Tc
    slab = [bfar.take(KB * SLM, f"slab{i}") for i in range(2)]
    bfar.push()
    far.push()
    ut = [bfar.take(D, f"ut{i}") for i in range(2)]
    xt = [far.take(D, f"xt{i}") for i in range(2)]
    xfull = [far.take(Tc + KCV, f"xfull{i}") for i in range(2)]
    cva = [far.take(Tc, f"cva{i}") for i in range(2)]
    cvb = [far.take(Tc, f"cvb{i}") for i in range(2)]
    bfar.pop()
    far.pop()
    bfar.push()
    far.push()
    dtin_bf = bfar.take(Tc, "dtin")
    bcf = bfar.take(Tc, "bcf")
    cbc = [bfar.take(Tc, f"cbc{i}") for i in range(NST)]
    bbc = [bfar.take(Tc, f"bbc{i}") for i in range(NST)]
    szb = [far.take(Tc, f"szb{i}") for i in range(2)]
    delta = [far.take(Tc, f"delta{i}") for i in range(2)]
    sfp = [far.take(Tc, f"sfp{i}") for i in range(3)]
    du = [far.take(Tc, f"du{i}") for i in range(2)]
    atile = [far.take(Tc, f"at{i}") for i in range(2)]
    dub = [far.take(Tc, f"dub{i}") for i in range(2)]
    stile = [far.take(Tc, f"st{i}") for i in range(2)]
    e4t = [far.take(Tc, f"e4{i}") for i in range(2)]
    yacc = far.take(Tc, "yacc")
    yfin = [far.take(Tc, "yfin0")]
    bfar.pop()
    far.pop()
    bfar.push()
    far.push()
    wo_slab = [bfar.take(WKH * SLM, f"wosl{i}") for i in range(2)]
    mo_bf = [bfar.take(Tc, f"mo_bf{i}") for i in range(MBo)]
    mobt = [bfar.take(D, f"mobt{i}") for i in range(2)]
    xr = [far.take(D, f"xr{i}") for i in range(2)]
    x1t = [far.take(D, f"x1t{i}") for i in range(2)]
    bfar.pop()
    far.pop()
    M_bf_hi, M_f_hi = bfar.hi, far.hi
    bfar.pop()
    far.pop()

    bfar.push()
    far.push()
    h2e = [bfar.take(Tc, f"h2e{i}") for i in range(E)]
    gate_r = bfar.take(KB * E, "gate_r")
    rgT_r = bfar.take(KB * 128, "rgT_r")
    ruT_r = bfar.take(KB * 128, "ruT_r")
    rdT_r = bfar.take(IB * 128, "rdT_r")
    srgT_r = bfar.take(KB * 128, "srgT_r")
    sruT_r = bfar.take(KB * 128, "sruT_r")
    sugT_r = bfar.take(128, "sugT_r")
    suuT_r = bfar.take(128, "suuT_r")
    sudT_r = bfar.take(128, "sudT_r")
    scgT_r = bfar.take(SI, "scgT_r")
    scuT_r = bfar.take(SI, "scuT_r")
    srdT_r = bfar.take(SIB * 128, "srdT_r")
    scdT_r = bfar.take(D, "scdT_r")
    rg_bf = bfar.take(Tc, "rg_bf")
    ru_bf = bfar.take(Tc, "ru_bf")
    h2sh = bfar.take(Tc, "h2sh")
    we = far.take(TT * E, "we")
    esb = [far.take(E, f"esb{i}") for i in range(2)]
    tcur = [far.take(E, f"tcur{i}") for i in range(2)]
    tc2 = [far.take(E, f"tc2{i}") for i in range(2)]
    gsm = [far.take(8, f"gsm{i}") for i in range(2)]
    wbc = [far.take(Tc, f"wbc{i}") for i in range(2)]
    bfar.push()
    far.push()
    u2t = [bfar.take(D, f"u2t{i}") for i in range(2)]
    xf1 = [far.take(D, f"xf1_{i}") for i in range(2)]
    bfar.pop()
    far.pop()
    bfar.push()
    far.push()
    h1a = [bfar.take(Tc, f"h1a{i}") for i in range(2)]
    h1b = [bfar.take(Tc, f"h1b{i}") for i in range(2)]
    gtile = [bfar.take(Tc, f"gt{i}") for i in range(max(IB, SIB // 2))]
    gut = [bfar.take(Tc, f"gut{i}") for i in range(2)]
    rdw = [bfar.take(Tc, f"rdw{i}") for i in range(2)]
    expwt = [bfar.take(EXPC, f"expwt{i}") for i in range(2)]
    bfar.pop()
    far.pop()
    bfar.push()
    far.push()
    cdt = [bfar.take(E * 128, f"cdt{i}") for i in range(2)]
    fo_bf = [bfar.take(Tc, f"fo_bf{i}") for i in range(MBo)]
    fot = [bfar.take(D, f"fot{i}") for i in range(2)]
    xf4 = [far.take(D, f"xf4_{i}") for i in range(2)]
    ot = [far.take(D, f"ot{i}") for i in range(2)]
    bfar.pop()
    far.pop()
    F_bf_hi, F_f_hi = bfar.hi, far.hi
    bfar.pop()
    far.pop()

    M01set = ut + xt + xfull + cva + cvb
    M23set = ([dtin_bf, bcf, yacc] + cbc + bbc + szb + delta + sfp + du + atile
              + dub + stile + e4t + yfin)
    M4set = wo_slab + mo_bf + mobt + xr + x1t
    Mcommon = list(xcy) + slab
    Mall = M01set + M23set + M4set + Mcommon
    F0set = u2t + xf1
    F23set = h1a + h1b + gtile + gut + rdw + expwt
    F4set = cdt + fo_bf + fot + xf4 + ot
    Fcommon = h2e + [gate_r, rgT_r, ruT_r, rdT_r, srgT_r, sruT_r, sugT_r, suuT_r,
               sudT_r, scgT_r, scuT_r, srdT_r, scdT_r, rg_bf, ru_bf, h2sh, we,
               wbc[0], wbc[1]] + esb + tcur + tc2 + gsm
    Fall = F0set + F23set + F4set + Fcommon

    def psb(name, cols, dtype=F32):
        t = nc.alloc_psum_tensor(name, [128, cols], dtype)
        return Buf(t.ap(), name)

    psg = [psb(f"psg{i}", Tc if Tc <= 512 else 512) for i in range(5)]
    ptb = [psb(f"ptb{i}", 1024, BF16) for i in range(2)]
    ptf = psb("ptf", Tc if Tc <= 512 else 512, F32)

    taps = []

    def tap(name, bufs, ap):
        if name not in cfg["taps"]:
            return
        t = nc.dram_tensor("tap_" + name, list(ap.shape), ap.dtype, kind="ExternalOutput").ap()
        p.op("sp", lambda e, t=t, ap=ap: e.dma_start(t, ap), reads=list(bufs), dma=True, tag="tap")
        taps.append(name)

    def dma(dst_buf, src_ap, eng="sp", cols=None, tag=""):
        if cols is None:
            dap, w, pw = dst_buf.ap, [dst_buf], []
        else:
            dap, w, pw = dst_buf.ap[:, cols[0]:cols[1]], [], [dst_buf]
        return p.op(eng, lambda e, d=dap, s=src_ap: e.dma_start(d, s),
                    writes=w, pwrites=pw, dma=True, tag=tag)

    def mm(psbuf, lhs, lcols, rhs, rcols, start, last, pslice=None, tag="mm"):
        lap = lhs.ap[:, lcols[0]:lcols[1]]
        rap = rhs.ap[:, rcols[0]:rcols[1]]
        pap = psbuf.ap[:, 0:rcols[1] - rcols[0]] if pslice is None else psbuf.ap[
            pslice[0]:pslice[1], pslice[2]:pslice[3]]
        return p.op("pe",
                    lambda e, o=pap, l=lap, r=rap, s=start, t=last: e.matmul(o, l, r, start=s, stop=t),
                    reads=[lhs, rhs],
                    writes=[psbuf] if start else (),
                    pwrites=[psbuf] if not start else (),
                    tag=tag)

    def ev_copy(dst, src, eng="act", dcols=None, scols=None, srows=None, tag="ev"):
        dap = dst.ap if dcols is None else dst.ap[:, dcols[0]:dcols[1]]
        sap = src.ap
        if scols is not None:
            sap = sap[:, scols[0]:scols[1]]
        if srows is not None:
            sap = sap[srows[0]:srows[1], :]
        fn = (lambda e, d=dap, s=sap: e.copy(d, s)) if eng == "act" else (
            lambda e, d=dap, s=sap: e.tensor_copy(d, s))
        return p.op(eng, fn, reads=[src],
                    writes=[dst] if dcols is None else (),
                    pwrites=[dst] if dcols is not None else (), tag=tag)

    def ln_phase(xbuf, ubuf, st, tag):
        p.op("dve", lambda e, x=xbuf, s=st: e.reduce_sum(
            s.ap[:, 0:1], x.ap, axis=AX.X, negate=True), reads=[xbuf], writes=[st],
            tag=tag + "s", small=True)
        p.op("act", lambda e, x=xbuf: e.activation(
            sqs.ap[:, 0:D], x.ap, AF.Square),
            reads=[xbuf], writes=[sqs], tag=tag + "sq")
        p.op("dve", lambda e, s=st: e.reduce_sum(
            s.ap[:, 1:2], sqs.ap[:, 0:D], axis=AX.X), reads=[sqs], pwrites=[st],
            tag=tag + "sq2", small=True)
        p.op("dve", lambda e, s=st: e.tensor_scalar(
            s.ap[:, 2:3], s.ap[:, 0:1], 1.0 / D, None, OP.mult), reads=[st], pwrites=[st],
            tag=tag + "m", small=True)
        p.op("dve", lambda e, s=st: e.tensor_scalar(
            s.ap[:, 3:4], s.ap[:, 1:2], 1.0 / D, None, OP.mult), reads=[st], pwrites=[st],
            tag=tag + "e2", small=True)
        p.op("dve", lambda e, s=st: e.tensor_tensor(
            s.ap[:, 4:5], s.ap[:, 2:3], s.ap[:, 2:3], OP.mult), reads=[st], pwrites=[st],
            tag=tag + "m2", small=True)
        p.op("dve", lambda e, s=st: e.tensor_tensor(
            s.ap[:, 5:6], s.ap[:, 3:4], s.ap[:, 4:5], OP.subtract), reads=[st], pwrites=[st],
            tag=tag + "v", small=True)
        p.op("act", lambda e, s=st: e.activation(
            s.ap[:, 6:7], s.ap[:, 5:6], AF.Ln, bias=cst.ap[:, 0:1]),
            reads=[st, cst], pwrites=[st], tag=tag + "ln", small=True)
        p.op("act", lambda e, s=st: e.activation(
            s.ap[:, 7:8], s.ap[:, 6:7], AF.Exp, scale=-0.5), reads=[st], pwrites=[st],
            tag=tag + "rs", small=True)
        p.op("dve", lambda e, x=xbuf, s=st, u=ubuf: e.tensor_scalar(
            u.ap, x.ap, s.ap[:, 2:3], s.ap[:, 7:8], OP.add, OP.mult),
            reads=[xbuf, st], writes=[ubuf], tag=tag + "u")

    def transpose_to(dst, dst_cols, src, src_cols, pt, tag):
        p.op("pe", lambda e, s=src, sc=src_cols, pt=pt: e.transpose(
            pt.ap[:, 0:128], s.ap[:, sc[0]:sc[1]], idb_r.ap),
            reads=[src, idb_r], writes=[pt], tag=tag)
        p.op("act", lambda e, d=dst, dc=dst_cols, pt=pt: e.copy(
            d.ap[:, dc[0]:dc[1]], pt.ap[:, 0:128]),
            reads=[pt], pwrites=[dst], tag=tag + "e")

    for dst, src in ((winx_bf, winx), (winz_bf, winz), (wout_bf, wout_in),
                     (expw_bf, expw_in), (cdmb_bf, cdmb_in)):
        p.op("pool", lambda e, d=dst, s=src: e.dma_start(d, s), dma=True, tag="cast")
    castw = Buf(None, "castw")
    castw.writers = [p.ops["pool"][-1]["h"]]

    for dst, src in ((wdt_r, wdt_in), (wxdt_r, wxdt_in), (wxbc_r, wxbc_in),
                     (idb_r, ident_in)):
        p.op("pool", lambda e, d=dst, s=src: e.dma_start(d.ap, s), writes=[dst], dma=True, tag="res")
    for dst, src in ((convw_r, convw_in), (convb_r, convb_in), (bdt_r, bdt_in),
                     (dchan_r, dchan_in), (bout_r, bout_in), (idf_r, ident_in)):
        dma(dst, src, tag="res")
    p.op("dve", lambda e: e.memset(cst.ap[:, 0:1], 1e-5), writes=[cst], tag="cst0")
    p.op("dve", lambda e: e.memset(cst.ap[:, 1:2], 1.0), pwrites=[cst], tag="cst1")
    dma(sqs, alog_in, cols=(0, DB * NST), tag="alog")
    p.op("act", lambda e: e.activation(sqs.ap[:, DB * NST:2 * DB * NST], sqs.ap[:, 0:DB * NST], AF.Exp),
         reads=[sqs], pwrites=[sqs], tag="aexp")
    p.op("dve", lambda e: e.tensor_scalar(a_r.ap, sqs.ap[:, DB * NST:2 * DB * NST], -1.0, None, OP.mult),
         reads=[sqs], writes=[a_r], tag="aneg")

    p.barrier("ph0")

    for sqi in range(NSEQ):
        p.op("dve", lambda e: e.memset(state.ap, 0.0), writes=[state], tag="st0")
        p.op("dve", lambda e: e.memset(carry.ap, 0.0), writes=[carry], tag="cy0")
        for ci in range(NCH):
            t0 = sqi * SEQ + ci * Tc

            for tt in range(TT):
                xb, ub, st = xt[tt % 2], ut[tt % 2], lnt[tt % 2]
                dma(xb, x_in[t0 + tt * 128: t0 + (tt + 1) * 128, :], tag="xload")
                tap(f"xt_{sqi}_{ci}_{tt}", [xb], xb.ap)
                ln_phase(xb, ub, st, "ln")
                tap(f"lnt_{sqi}_{ci}_{tt}", [st], st.ap)
                tap(f"ut_{sqi}_{ci}_{tt}", [ub], ub.ap)
                for kb in range(KB):
                    transpose_to(u_f, (kb * Tc + tt * 128, kb * Tc + (tt + 1) * 128),
                                 ub, (kb * 128, (kb + 1) * 128), ptb[kb % 2], "tru")
            tap(f"uf_{sqi}_{ci}", [u_f], u_f.ap)

            for s in range(XSL):
                slb = slab[s % 2]
                p.op("sp", lambda e, d=slb, s=s: e.dma_start(d.ap, winx_bf[s]),
                     reads=[castw], writes=[slb], dma=True, tag="wslab")
                nmb = SLM // 128
                for mb in range(nmb):
                    dblk = s * nmb + mb
                    pg = psg[dblk % 2]
                    for kb in range(KB):
                        mm(pg, slb, (kb * SLM + mb * 128, kb * SLM + (mb + 1) * 128),
                           u_f, (kb * Tc, (kb + 1) * Tc), kb == 0, kb == KB - 1, tag="inproj")
                    xf = xfull[dblk % 2]
                    p.op("dve", lambda e, xf=xf, dblk=dblk: e.tensor_copy(
                        xf.ap[:, 0:KCV - 1], carry.ap[:, dblk * (KCV - 1):(dblk + 1) * (KCV - 1)]),
                        reads=[carry], writes=[xf], tag="cv_c", small=True)
                    p.op("act", lambda e, xf=xf, pg=pg: e.copy(xf.ap[:, KCV - 1:KCV - 1 + Tc], pg.ap),
                         reads=[pg], pwrites=[xf], tag="cv_ev")
                    p.op("dve", lambda e, xf=xf, dblk=dblk: e.tensor_copy(
                        carry.ap[:, dblk * (KCV - 1):(dblk + 1) * (KCV - 1)],
                        xf.ap[:, Tc:Tc + KCV - 1]),
                        reads=[xf], pwrites=[carry], tag="cv_s")
                    ca, cb = cva[dblk % 2], cvb[dblk % 2]
                    p.op("dve", lambda e, xf=xf, dblk=dblk, o=ca: e.tensor_scalar(
                        o.ap, xf.ap[:, 0:Tc], convw_r.ap[:, dblk * KCV:dblk * KCV + 1], None, OP.mult),
                        reads=[xf, convw_r], writes=[ca], tag="cv0")
                    src_, dst_ = ca, cb
                    for k in range(1, KCV):
                        p.op("dve", lambda e, xf=xf, dblk=dblk, k=k, s2=src_, d2=dst_: e.scalar_tensor_tensor(
                            d2.ap, xf.ap[:, k:k + Tc], convw_r.ap[:, dblk * KCV + k:dblk * KCV + k + 1],
                            s2.ap, OP.mult, OP.add),
                            reads=[xf, convw_r, src_], writes=[dst_], tag="cvk")
                        src_, dst_ = dst_, src_
                    p.op("act", lambda e, s2=src_, dblk=dblk: e.activation(
                        xcy[dblk].ap, s2.ap, AF.Silu, bias=convb_r.ap[:, dblk:dblk + 1]),
                        reads=[src_, convb_r], writes=[xcy[dblk]], tag="cvsilu")
            tap(f"xc_{sqi}_{ci}", xcy, xcy_all_ap)
            p.handoff(M01set, M23set)
            if stop == "M1":
                p.handoff(Mall + Fall, Mall)
                continue

            pgdt = psg[2]
            for kb in range(DB):
                mm(pgdt, wxdt_r, (kb * RDT, (kb + 1) * RDT), xcy[kb], (0, Tc),
                   kb == 0, kb == DB - 1, tag="xd_dt")
            ev_copy(dtin_bf, pgdt, tag="dtin_ev")
            pgbc = psg[3]
            for kb in range(DB):
                mm(pgbc, wxbc_r, (kb * 2 * NST, (kb + 1) * 2 * NST), xcy[kb], (0, Tc),
                   kb == 0, kb == DB - 1, pslice=(0, 2 * NST, 0, Tc), tag="xd_bc")
            p.op("act", lambda e: e.copy(bcf.ap[0:2 * NST, :], pgbc.ap[0:2 * NST, 0:Tc]),
                 reads=[pgbc], writes=[bcf], tag="bc_ev")
            bcd = bc_dram[ci % 2]
            p.op("sp", lambda e, d=bcd.ap: e.dma_start(d, bcf.ap[0:2 * NST, :]),
                 reads=[bcf], writes=[bcd], dma=True, tag="bc_w")
            for n in range(NST):
                p.op("sp", lambda e, n=n, s=bcd.ap: e.dma_start(
                    bbc[n].ap,
                    s[n:n + 1, :].partition_broadcast(128).squeeze(1)),
                    reads=[bcd], writes=[bbc[n]], dma=True, tag="bbcast")
            for n in range(NST):
                p.op("sp", lambda e, n=n, s=bcd.ap: e.dma_start(
                    cbc[n].ap,
                    s[NST + n:NST + n + 1, :].partition_broadcast(128).squeeze(1)),
                    reads=[bcd], writes=[cbc[n]], dma=True, tag="cbcast")
            tap(f"dtin_{sqi}_{ci}", [dtin_bf], dtin_bf.ap)
            tap(f"bbc_{sqi}_{ci}", bbc, bbc[0].ap)

            for dblk in range(DB):
                pg = psg[dblk % 2]
                mm(pg, wdt_r, (dblk * 128, (dblk + 1) * 128), dtin_bf, (0, Tc), True, True, tag="dtmm")
                s0, s1, s2 = sfp
                p.op("act", lambda e, pg=pg, dblk=dblk: e.activation(
                    s0.ap, pg.ap, AF.Abs, bias=bdt_r.ap[:, dblk:dblk + 1]),
                    reads=[pg, bdt_r], writes=[s0], tag="sp_abs")
                p.op("act", lambda e: e.activation(s1.ap, s0.ap, AF.Exp, scale=-1.0),
                     reads=[s0], writes=[s1], tag="sp_exp")
                p.op("act", lambda e: e.activation(s2.ap, s1.ap, AF.Ln, bias=cst.ap[:, 1:2]),
                     reads=[s1, cst], writes=[s2], tag="sp_ln")
                p.op("act", lambda e, pg=pg, dblk=dblk: e.activation(
                    s0.ap, pg.ap, AF.Relu, bias=bdt_r.ap[:, dblk:dblk + 1]),
                    reads=[pg, bdt_r], writes=[s0], tag="sp_relu")
                dl = delta[dblk % 2]
                scan_hs = []
                p.op("dve", lambda e, d=dl: e.tensor_tensor(d.ap, s2.ap, s0.ap, OP.add),
                     reads=[s2, s0], writes=[dl], tag="sp_add")
                db_ = du[dblk % 2]
                p.op("dve", lambda e, d=dl, o=db_, dblk=dblk: e.tensor_tensor(
                    o.ap, d.ap, xcy[dblk].ap, OP.mult),
                    reads=[dl, xcy[dblk]], writes=[db_], tag="du")
                p.op("dve", lambda e, dblk=dblk: e.tensor_scalar(
                    yacc.ap, xcy[dblk].ap, dchan_r.ap[:, dblk:dblk + 1], None, OP.mult),
                    reads=[xcy[dblk], dchan_r], writes=[yacc], tag="yinit")
                for n in range(NST):
                    col = dblk * NST + n
                    at = atile[n % 2]
                    p.op("act", lambda e, d=dl, o=at, col=col: e.activation(
                        o.ap, d.ap, AF.Exp, scale=a_r.ap[:, col:col + 1]),
                        reads=[dl, a_r], writes=[at], tag="aexp")
                    dbt = dub[n % 2]
                    p.op("dve", lambda e, d=db_, o=dbt, n=n: e.tensor_tensor(
                        o.ap, d.ap, bbc[n].ap, OP.mult),
                        reads=[db_, bbc[n]], writes=[dbt], tag="dub")
                    stl = stile[n % 2]
                    sh = p.op("dve", lambda e, a=at, b=dbt, o=stl, col=col: e.tensor_tensor_scan(
                        o.ap, a.ap, b.ap, state.ap[:, col:col + 1], OP.mult, OP.add),
                        reads=[at, dbt, state], writes=[stl], tag="scan")
                    scan_hs.append(sh)
                    p.op("act", lambda e, s=stl, col=col: e.copy(
                        state.ap[:, col:col + 1], s.ap[:, Tc - 1:Tc]),
                        reads=[stl], pwrites=[state], tag="sc_cy")
                    et = e4t[n % 2]
                    p.op("dve", lambda e, s=stl, o=et, n=n: e.tensor_tensor(
                        o.ap, s.ap, cbc[n].ap, OP.mult),
                        reads=[stl, cbc[n]], writes=[et], tag="e4m")
                    if n == 1:
                        p.op("pool", lambda e, t=et: e.tensor_copy(yfin[0].ap, t.ap),
                             reads=[et], writes=[yfin[0]], tag="yo0")
                    elif n % 2 == 1:
                        p.op("pool", lambda e, t=et: e.tensor_tensor(
                            yfin[0].ap, yfin[0].ap, t.ap, OP.add),
                            reads=[et, yfin[0]], pwrites=[yfin[0]], tag="yoadd")
                    else:
                        p.op("dve", lambda e, t=et: e.tensor_tensor(
                            yacc.ap, yacc.ap, t.ap, OP.add),
                            reads=[et, yacc], pwrites=[yacc], tag="yeadd")
                nmb = SLM // 128
                zs, zmb = dblk // nmb, dblk % nmb
                zsl = slab[zs % 2]
                if zmb == 0:
                    p.op("sp", lambda e, d=zsl, s=zs: e.dma_start(d.ap, winz_bf[s]),
                         reads=[castw], writes=[zsl], dma=True, tag="zslab")
                pgz = psg[2 + dblk % 2]
                for kb in range(KB):
                    lap = zsl.ap[:, kb * SLM + zmb * 128: kb * SLM + (zmb + 1) * 128]
                    rap = u_f.ap[:, kb * Tc:(kb + 1) * Tc]
                    pap = pgz.ap[:, 0:Tc]
                    p.op("pe",
                         lambda e, o=pap, l=lap, r=rap, s=(kb == 0), t=(kb == KB - 1):
                         e.matmul(o, l, r, start=s, stop=t),
                         reads=[zsl, u_f],
                         writes=[pgz] if kb == 0 else (),
                         pwrites=[pgz] if kb != 0 else (),
                         after=[scan_hs[min(kb, len(scan_hs) - 1)]],
                         tag="zmm")
                sz = szb[dblk % 2]
                z1, z2 = sfp[1], sfp[2]
                p.op("act", lambda e, pg=pgz, o=z1: e.activation(o.ap, pg.ap, AF.Tanh, scale=0.5),
                     reads=[pgz], writes=[z1], tag="ztanh")
                p.op("dve", lambda e, a=z1, o=z2: e.tensor_scalar(
                    o.ap, a.ap, 0.5, 0.5, OP.mult, OP.add),
                    reads=[z1], writes=[z2], tag="zsig")
                p.op("dve", lambda e, pg=pgz, a=z2, o=sz: e.tensor_tensor(o.ap, pg.ap, a.ap, OP.mult),
                     reads=[pgz, z2], writes=[sz], tag="zmul")
                yf = sfp[0]
                p.op("dve", lambda e, o=yf: e.tensor_tensor(
                    o.ap, yacc.ap, yfin[0].ap, OP.add),
                    reads=[yacc, yfin[0]], writes=[yf], tag="ysum")
                p.op("dve", lambda e, a=yf, b=sz, dblk=dblk: e.tensor_tensor(
                    xcy[dblk].ap, a.ap, b.ap, OP.mult),
                    reads=[yf, sz], writes=[xcy[dblk]], tag="ymul")
            tap(f"ybf_{sqi}_{ci}", xcy, xcy_all_ap)
            p.handoff(M23set, M4set)
            if stop == "M3":
                p.handoff(Mall + Fall, Mall)
                continue

            nmb = SLM // 128
            for s in range(OSL):
                for kh in range(DB // WKH):
                    wsl = wo_slab[kh % 2]
                    p.op("sp", lambda e, d=wsl, s=s, kh=kh: e.dma_start(d.ap, wout_bf[s, kh]),
                         reads=[castw], writes=[wsl], dma=True, tag="woslab")
                    for mb in range(nmb):
                        pg = psg[mb]
                        for kk in range(WKH):
                            kb = kh * WKH + kk
                            mm(pg, wsl, (kk * SLM + mb * 128, kk * SLM + (mb + 1) * 128),
                               xcy[kb], (0, Tc), kb == 0, kb == DB - 1, tag="womm")
                for mb in range(nmb):
                    mo = s * nmb + mb
                    p.op("act", lambda e, pg=psg[mb], mo=mo: e.activation(
                        mo_bf[mo].ap, pg.ap, AF.Identity,
                        bias=bout_r.ap[:, mo:mo + 1]),
                        reads=[psg[mb], bout_r], writes=[mo_bf[mo]], tag="wo_ev")
            tap(f"mobf_{sqi}_{ci}", mo_bf, mo_bf[0].ap)
            for tt in range(TT):
                mt = mobt[tt % 2]
                for mo in range(MBo):
                    transpose_to(mt, (mo * 128, (mo + 1) * 128),
                                 mo_bf[mo], (tt * 128, (tt + 1) * 128),
                                 ptb[mo % 2], "trmo")
                tap(f"mot_{sqi}_{ci}_{tt}", [mt], mt.ap)
                xrb = xr[tt % 2]
                dma(xrb, x_in[t0 + tt * 128: t0 + (tt + 1) * 128, :], tag="xreload")
                xo = x1t[tt % 2]
                p.op("dve", lambda e, a=xrb, b=mt, o=xo: e.tensor_tensor(
                    o.ap, a.ap, b.ap, OP.add), reads=[xrb, mt], writes=[xo], tag="res1")
                tap(f"x1t_{sqi}_{ci}_{tt}", [xo], xo.ap)
                p.op("sp", lambda e, o=xo, tt=tt, t0=t0: e.dma_start(
                    x1_dram.ap[t0 + tt * 128: t0 + (tt + 1) * 128, :], o.ap),
                    reads=[xo], pwrites=[x1_dram], dma=True, tag="x1w")
            tap(f"x1_{sqi}_{ci}", [x1_dram], x1_dram.ap[t0:t0 + Tc, :])
            p.handoff(Mall, Fall)
            if stop == "M4":
                p.handoff(Mall + Fall, Mall)
                continue

            for tt in range(TT):
                xb, ub, st = xf1[tt % 2], u2t[tt % 2], lnt[tt % 2]
                p.op("sp", lambda e, d=xb, tt=tt, t0=t0: e.dma_start(
                    d.ap, x1_dram.ap[t0 + tt * 128: t0 + (tt + 1) * 128, :]),
                    reads=[x1_dram], writes=[xb], dma=True, tag="x1r")
                ln_phase(xb, ub, st, "l2")
                for kb in range(KB):
                    transpose_to(u_f, (kb * Tc + tt * 128, kb * Tc + (tt + 1) * 128),
                                 ub, (kb * 128, (kb + 1) * 128), ptb[kb % 2], "tru2")
            tap(f"u2f_{sqi}_{ci}", [u_f], u_f.ap)
            p.handoff(F0set, F23set)
            if stop == "F0":
                p.handoff(Mall + Fall, Mall)
                continue

            for dst, src in ((gate_r, gate_in), (rgT_r, rgT_in), (ruT_r, ruT_in),
                             (rdT_r, rdT_in), (srgT_r, srgT_in), (sruT_r, sruT_in),
                             (sugT_r, sugT_in), (suuT_r, suuT_in), (sudT_r, sudT_in),
                             (scgT_r, scgT_in), (scuT_r, scuT_in), (srdT_r, srdT_in),
                             (scdT_r, scdT_in)):
                p.op("pool", lambda e, d=dst, s=src: e.dma_start(d.ap, s),
                     writes=[dst], dma=True, tag="fwt")
            for tt in range(TT):
                pg = psg[4]
                for kb in range(KB):
                    mm(pg, u_f, (kb * Tc + tt * 128, kb * Tc + (tt + 1) * 128),
                       gate_r, (kb * E, (kb + 1) * E), kb == 0, kb == KB - 1,
                       pslice=(0, 128, 0, E), tag="gmm")
                es, tcu, tcv, gs = esb[tt % 2], tcur[tt % 2], tc2[tt % 2], gsm[tt % 2]
                p.op("dve", lambda e, pg=pg, g=gs: e.reduce_max(
                    g.ap[:, 0:1], pg.ap[:, 0:E], axis=AX.X, negate=True),
                    reads=[pg], writes=[gs], tag="g_max", small=True)
                p.op("act", lambda e, pg=pg, g=gs, o=es: e.activation(
                    o.ap, pg.ap[:, 0:E], AF.Exp, bias=g.ap[:, 0:1]),
                    reads=[pg, gs], writes=[es], tag="g_exp", small=True)
                p.op("dve", lambda e, a=es, o=tcu: e.tensor_copy(o.ap, a.ap),
                     reads=[es], writes=[tcu], tag="g_cp", small=True)
                cur, nxt = tcu, tcv
                for it in range(3):
                    p.op("dve", lambda e, t=cur, g=gs: e.reduce_max(
                        g.ap[:, 1:2], t.ap, axis=AX.X), reads=[cur], pwrites=[gs], tag="g_m", small=True)
                    p.op("dve", lambda e, t=cur, g=gs: e.tensor_scalar(
                        sqs.ap[:, 0:E], t.ap, g.ap[:, 1:2], None, OP.is_equal),
                        reads=[cur, gs], writes=[sqs], tag="g_eq", small=True)
                    p.op("dve", lambda e, t=cur, o=nxt: e.scalar_tensor_tensor(
                        o.ap, sqs.ap[:, 0:E], NEG_BIG, t.ap, OP.mult, OP.add),
                        reads=[sqs, cur], writes=[nxt], tag="g_rm", small=True)
                    cur, nxt = nxt, cur
                p.op("dve", lambda e, t=cur, g=gs: e.reduce_max(
                    g.ap[:, 1:2], t.ap, axis=AX.X), reads=[cur], pwrites=[gs], tag="g_m4", small=True)
                p.op("dve", lambda e, a=es, g=gs: e.tensor_scalar(
                    sqs.ap[:, 0:E], a.ap, g.ap[:, 1:2], None, OP.is_le),
                    reads=[es, gs], writes=[sqs], tag="g_le", small=True)
                p.op("dve", lambda e, a=es, g=gs: e.tensor_scalar(
                    sqs.ap[:, E:2 * E], a.ap, g.ap[:, 1:2], None, OP.is_equal),
                    reads=[es, gs], pwrites=[sqs], tag="g_eq4", small=True)
                p.op("dve", lambda e: e.tensor_tensor(
                    sqs.ap[:, 2 * E:3 * E], sqs.ap[:, E:2 * E], sqs.ap[:, 0:E], OP.subtract),
                    reads=[sqs], pwrites=[sqs], tag="g_k1", small=True)
                p.op("dve", lambda e, a=es, t=cur: e.scalar_tensor_tensor(
                    t.ap, sqs.ap[:, 2 * E:3 * E], 1.0, a.ap, OP.add, OP.mult),
                    reads=[sqs, es], writes=[cur], tag="g_wr", small=True)
                p.op("dve", lambda e, t=cur, g=gs: e.reduce_sum(
                    g.ap[:, 2:3], t.ap, axis=AX.X), reads=[cur], pwrites=[gs], tag="g_s4", small=True)
                p.op("dve", lambda e, g=gs: e.tensor_scalar(
                    g.ap[:, 3:4], g.ap[:, 2:3], 1e-20, None, OP.add),
                    reads=[gs], pwrites=[gs], tag="g_eps", small=True)
                p.op("dve", lambda e, g=gs: e.reciprocal(g.ap[:, 4:5], g.ap[:, 3:4]),
                     reads=[gs], pwrites=[gs], tag="g_rcp", small=True)
                p.op("dve", lambda e, t=cur, g=gs, tt=tt: e.tensor_scalar(
                    we.ap[:, tt * E:(tt + 1) * E], t.ap, g.ap[:, 4:5], None, OP.mult),
                    reads=[cur, gs], pwrites=[we], tag="g_we", small=True)
            wde = wall_dram[ci % 2]
            for tt in range(TT):
                p.op("pe", lambda e, tt=tt: e.transpose(
                    ptf.ap[0:E, 0:128], we.ap[:, tt * E:(tt + 1) * E], idf_r.ap),
                    reads=[we, idf_r], writes=[ptf], tag="trwe")
                p.op("dve", lambda e, tt=tt: e.tensor_copy(
                    sqs.ap[0:E, tt * 128:(tt + 1) * 128], ptf.ap[0:E, 0:128]),
                    reads=[ptf], pwrites=[sqs], tag="trwe_ev")
            p.op("sp", lambda e, d=wde.ap: e.dma_start(d, sqs.ap[0:E, 0:Tc]),
                 reads=[sqs], writes=[wde], dma=True, tag="wall_w")
            tap(f"we_{sqi}_{ci}", [we], we.ap)
            if stop == "F1":
                p.handoff(Mall + Fall, Mall)
                continue

            def kproj(wbuf, dstbuf, pgi, tag):
                pg = psg[pgi]
                for kb in range(KB):
                    mm(pg, wbuf, (kb * 128, (kb + 1) * 128), u_f, (kb * Tc, (kb + 1) * Tc),
                       kb == 0, kb == KB - 1, tag=tag)
                ev_copy(dstbuf, pg, tag=tag + "ev")
                return pg

            kproj(rgT_r, rg_bf, 0, "rg")
            kproj(ruT_r, ru_bf, 1, "ru")
            kproj(srgT_r, h1a[0], 2, "srg")
            pg = psg[2]
            mm(pg, sugT_r, (0, 128), h1a[0], (0, Tc), True, True, tag="sug")
            ev_copy(h1a[1], pg, tag="sugev")
            kproj(sruT_r, h1b[0], 3, "sru")
            pg = psg[3]
            mm(pg, suuT_r, (0, 128), h1b[0], (0, Tc), True, True, tag="suu")
            ev_copy(h1b[1], pg, tag="suuev")
            HF = SIB // 2
            pgr = psg[4]
            for hf in range(2):
                for ib in range(HF):
                    sib = hf * HF + ib
                    pg = psg[ib % 2]
                    mm(pg, scgT_r, (sib * 128, (sib + 1) * 128), h1a[1], (0, Tc), True, True, tag="scg")
                    p.op("act", lambda e, pg=pg, ib=ib: e.activation(
                        gtile[ib].ap, pg.ap, AF.Silu),
                        reads=[pg], writes=[gtile[ib]], tag="scgsilu")
                for ib in range(HF):
                    sib = hf * HF + ib
                    pg = psg[ib % 2]
                    mm(pg, scuT_r, (sib * 128, (sib + 1) * 128), h1b[1], (0, Tc), True, True, tag="scu")
                    gb = gut[ib % 2]
                    p.op("dve", lambda e, pg=pg, ib=ib, o=gb: e.scalar_tensor_tensor(
                        o.ap, pg.ap, 1.0, gtile[ib].ap, OP.mult, OP.mult),
                        reads=[pg, gtile[ib]], writes=[gb], tag="sgu")
                    mm(pgr, srdT_r, (sib * 128, (sib + 1) * 128), gb, (0, Tc),
                       sib == 0, sib == SIB - 1, tag="srd")
            ev_copy(h1a[0], pgr, tag="srdev")
            pg = psg[2]
            mm(pg, sudT_r, (0, 128), h1a[0], (0, Tc), True, True, tag="sud")
            ev_copy(h2sh, pg, tag="sudev")
            tap(f"h2sh_{sqi}_{ci}", [h2sh], h2sh.ap)
            if stop == "F2":
                p.handoff(Mall + Fall, Mall)
                continue

            rot = [0]

            def npg():
                pg = psg[rot[0] % 4]
                rot[0] += 1
                return pg

            for ei in range(E):
                ew = expwt[ei % 2]
                p.op("sp", lambda e, d=ew, ei=ei: e.dma_start(d.ap, expw_bf[ei]),
                     reads=[castw], writes=[ew], dma=True, tag="expw")
                wb = wbc[ei % 2]
                p.op("sp", lambda e, d=wb, ei=ei, w=wde.ap: e.dma_start(
                    d.ap, w[ei:ei + 1, :].partition_broadcast(128).squeeze(1)),
                    reads=[wde], writes=[wb], dma=True, tag="wbcast")
                h1g, h1u = h1a[ei % 2], h1b[ei % 2]
                pg = npg()
                mm(pg, ew, (0, 128), rg_bf, (0, Tc), True, True, tag="ug")
                ev_copy(h1g, pg, tag="ugev")
                pg = npg()
                mm(pg, ew, (FI + 128, FI + 256), ru_bf, (0, Tc), True, True, tag="uu")
                ev_copy(h1u, pg, tag="uuev")
                for ib in range(IB):
                    pg = npg()
                    mm(pg, ew, (128 + ib * 128, 128 + (ib + 1) * 128), h1g, (0, Tc),
                       True, True, tag="cg")
                    p.op("act", lambda e, pg=pg, ib=ib: e.activation(
                        gtile[ib].ap, pg.ap, AF.Silu),
                        reads=[pg], writes=[gtile[ib]], tag="cgsilu")
                pgrd = psg[4] if ei % 2 == 0 else ptf
                for ib in range(IB):
                    pg = npg()
                    mm(pg, ew, (FI + 256 + ib * 128, FI + 256 + (ib + 1) * 128), h1u, (0, Tc),
                       True, True, tag="cu")
                    gb = gut[ib % 2]
                    p.op("dve", lambda e, pg=pg, ib=ib, o=gb: e.scalar_tensor_tensor(
                        o.ap, pg.ap, 1.0, gtile[ib].ap, OP.mult, OP.mult),
                        reads=[pg, gtile[ib]], writes=[gb], tag="gu")
                    mm(pgrd, rdT_r, (ib * 128, (ib + 1) * 128), gb, (0, Tc),
                       ib == 0, ib == IB - 1, tag="rdmm")
                rw = rdw[ei % 2]
                p.op("dve", lambda e, pg=pgrd, o=rw, w=wb: e.scalar_tensor_tensor(
                    o.ap, pg.ap, 1.0, w.ap, OP.mult, OP.mult),
                    reads=[pgrd, wb], writes=[rw], tag="rdwm")
                pg = npg()
                mm(pg, ew, (2 * FI + 256, 2 * FI + 384), rw, (0, Tc), True, True, tag="ud")
                p.op("act", lambda e, pg=pg, ei=ei: e.copy(
                    h2e[ei].ap, pg.ap), reads=[pg], writes=[h2e[ei]], tag="h2ev")
            tap(f"h2e_{sqi}_{ci}", h2e, h2e[0].ap)
            p.handoff(F23set + F0set, F4set)
            if stop == "F3":
                p.handoff(Mall + Fall, Mall)
                continue

            for mo in range(MBo):
                cd = cdt[mo % 2]
                p.op("sp", lambda e, d=cd, mo=mo: e.dma_start(d.ap, cdmb_bf[mo]),
                     reads=[castw], writes=[cd], dma=True, tag="cdT")
                pg = psg[mo % 2]
                for ei in range(E):
                    mm(pg, cd, (ei * 128, (ei + 1) * 128), h2e[ei], (0, Tc),
                       ei == 0, False, tag="cdmm")
                mm(pg, scdT_r, (mo * 128, (mo + 1) * 128), h2sh, (0, Tc), False, True, tag="scdmm")
                p.op("act", lambda e, pg=pg, mo=mo: e.copy(
                    fo_bf[mo].ap, pg.ap), reads=[pg], writes=[fo_bf[mo]], tag="foev")
            for tt in range(TT):
                ft = fot[tt % 2]
                for mo in range(MBo):
                    transpose_to(ft, (mo * 128, (mo + 1) * 128),
                                 fo_bf[mo], (tt * 128, (tt + 1) * 128),
                                 ptb[mo % 2], "trfo")
                xb = xf4[tt % 2]
                p.op("sp", lambda e, d=xb, tt=tt, t0=t0: e.dma_start(
                    d.ap, x1_dram.ap[t0 + tt * 128: t0 + (tt + 1) * 128, :]),
                    reads=[x1_dram], writes=[xb], dma=True, tag="x1r4")
                ob = ot[tt % 2]
                p.op("dve", lambda e, a=xb, b=ft, o=ob: e.tensor_tensor(
                    o.ap, a.ap, b.ap, OP.add), reads=[xb, ft], writes=[ob], tag="res2")
                p.op("sp", lambda e, o=ob, tt=tt, t0=t0: e.dma_start(
                    out_hbm[t0 + tt * 128: t0 + (tt + 1) * 128, :], o.ap),
                    reads=[ob], dma=True, tag="outw")
            p.handoff(Fall, Mall)

    ndma_sp = sum(1 for o in p.ops["sp"] if o["h"].dma)
    sems, dsems = p.emit(nc)
    with nc.Block() as blk:
        @blk.sync
        def _(sync):
            for lane in range(NLANES):
                cnt = ndma_sp // NLANES + (1 if lane < ndma_sp % NLANES else 0)
                if cnt:
                    sync.wait_ge(dsems["sp"][lane], 16 * cnt)

    return nc




def kernel(**inputs):
    cfg = make_cfg()
    N_CORES = 8
    B = 16
    BPC = B // N_CORES

    x = np.asarray(inputs["L0_attn_hidden_states"], np.float32)
    packed = pack_host(cfg, {k: np.asarray(v, np.float32) for k, v in inputs.items()
                             if k != "L0_attn_hidden_states"})

    nc = build_program(cfg)

    in_maps = []
    for c in range(N_CORES):
        m = dict(packed)
        xs = x[c * BPC:(c + 1) * BPC].reshape(BPC * cfg["SEQ"], cfg["D"])
        m["x"] = np.ascontiguousarray(xs)
        in_maps.append(m)

    res = run_bass_kernel_spmd(nc, in_maps, core_ids=list(range(N_CORES)))
    outs = [np.asarray(res.results[c]["out"]).reshape(BPC, cfg["SEQ"], cfg["D"])
            for c in range(N_CORES)]
    return np.concatenate(outs, axis=0)


# version 15
# speedup vs baseline: 1.0801x; 1.0362x over previous
import numpy as np
import concourse.bass as bass
import concourse.mybir as mybir
from concourse.bass_utils import run_bass_kernel_spmd
from concourse.alu_op_type import AluOpType as OP

F32 = mybir.dt.float32
BF16 = mybir.dt.bfloat16
AF = mybir.ActivationFunctionType
AX = mybir.AxisListType

NEG_BIG = -1.0e30
NLANES = 8

ENGINES = ("pe", "act", "dve", "pool", "sp")
SECTION = dict(pe="tensor", act="scalar", dve="vector", pool="gpsimd", sp="sync")


class H:
    __slots__ = ("eng", "idx", "dma", "rank", "lane")

    def __init__(self, eng, idx, dma):
        self.eng, self.idx, self.dma = eng, idx, dma
        self.rank = 0
        self.lane = 0


class Buf:
    __slots__ = ("ap", "writers", "readers", "name")

    def __init__(self, ap, name=""):
        self.ap = ap
        self.writers = []
        self.readers = []
        self.name = name


class Prog:
    def __init__(self, sync_all=False):
        self.ops = {e: [] for e in ENGINES}
        self.sync_all = sync_all
        self.dma_counts = {e: 0 for e in ENGINES}

    @staticmethod
    def _reduce(handles):
        best = {}
        for h in handles:
            key = (h.eng, h.dma, h.lane)
            if key not in best or h.idx > best[key].idx:
                best[key] = h
        return list(best.values())

    def op(self, eng, fn, reads=(), writes=(), pwrites=(), dma=False, tag="", small=False, sync_self=False, after=()):
        deps = list(after)
        for b in reads:
            deps += b.writers
        for b in writes:
            deps += b.writers + b.readers
        for b in pwrites:
            deps += b.writers + b.readers
        h = H(eng, len(self.ops[eng]), dma)
        if dma:
            h.lane = self.dma_counts[eng] % NLANES
            self.dma_counts[eng] += 1
        if len(deps) > 12:
            deps = Prog._reduce(deps)
        self.ops[eng].append(dict(fn=fn, deps=deps, h=h, tag=tag, marked=False,
                                  small=small, sync_self=sync_self))
        for b in reads:
            b.readers.append(h)
        for b in writes:
            b.writers = [h]
            b.readers = []
        for b in pwrites:
            b.writers.append(h)
            b.readers = []
        return h

    def wait_only(self, eng, deps, tag="w"):
        h = H(eng, len(self.ops[eng]), False)
        self.ops[eng].append(dict(fn=None, deps=list(deps), h=h, tag=tag, marked=False,
                                  small=False, sync_self=False))
        return h

    @staticmethod
    def handoff(outs, ins):
        acc = []
        for b in outs:
            acc += b.writers
            acc += b.readers
        deps = Prog._reduce(acc)
        for nb in ins:
            nb.writers = Prog._reduce(nb.writers + deps)

    def barrier(self, tag="bar"):
        lasts = []
        for e in ENGINES:
            lastc = None
            lastd = {}
            nd = 0
            for o in self.ops[e]:
                if o["h"].dma:
                    lastd[nd % NLANES] = o["h"]
                    nd += 1
                elif o["fn"] is not None:
                    lastc = o["h"]
            if lastc is not None:
                lasts.append(lastc)
            lasts += list(lastd.values())
        for e in ENGINES:
            deps = [h for h in lasts if not (h.eng == e and not h.dma)]
            if deps:
                self.wait_only(e, deps, tag=tag)

    def emit(self, nc):
        for e in ENGINES:
            lane_cnt = [0] * NLANES
            for o in self.ops[e]:
                if o["h"].dma:
                    lane = o["h"].lane
                    lane_cnt[lane] += 1
                    o["h"].rank = (lane, lane_cnt[lane])
        for e in ENGINES:
            for o in self.ops[e]:
                byeng = {}
                for d in o["deps"]:
                    if (d.eng == e and not d.dma and not self.ops[e][d.idx]["small"]
                            and not o["sync_self"] and not self.sync_all):
                        continue
                    key = (d.eng, d.dma, d.rank[0] if d.dma else 0)
                    if key not in byeng or d.idx > byeng[key].idx:
                        byeng[key] = d
                o["wdeps"] = list(byeng.values())
                for d in o["wdeps"]:
                    self.ops[d.eng][d.idx]["marked"] = True
        for e in ENGINES:
            r = 0
            for o in self.ops[e]:
                if not o["h"].dma and o["marked"]:
                    r += 1
                    o["h"].rank = r
        sems = {e: nc.alloc_semaphore(f"sem_{e}") for e in ENGINES}
        dsems = {e: [nc.alloc_semaphore(f"dsem_{e}{i}") for i in range(NLANES)]
                 for e in ("sp", "pool")}

        with nc.Block() as blk:
            for e in ENGINES:

                def make(e):
                    def body(eng):
                        hw = {}
                        for o in self.ops[e]:
                            for d in o["wdeps"]:
                                if d.dma:
                                    lane, cnt = d.rank
                                    key = ("d", d.eng, lane)
                                    val = 16 * cnt
                                    sem = dsems[d.eng][lane]
                                else:
                                    key = ("c", d.eng)
                                    val = d.rank
                                    sem = sems[d.eng]
                                if hw.get(key, 0) < val:
                                    eng.wait_ge(sem, val)
                                    hw[key] = val
                            if o["fn"] is not None:
                                if o["h"].dma:
                                    lane, cnt = o["h"].rank
                                    if cnt >= 2:
                                        key = ("d", e, lane)
                                        val = 16 * (cnt - 1)
                                        if hw.get(key, 0) < val:
                                            eng.wait_ge(dsems[e][lane], val)
                                            hw[key] = val
                                inst = o["fn"](eng)
                                if o["h"].dma:
                                    inst.then_inc(dsems[e][o["h"].rank[0]], 16)
                                elif o["marked"]:
                                    inst.then_inc(sems[e], 1)

                    return body

                getattr(blk, SECTION[e])(make(e))
        return sems, dsems


class Carve:
    def __init__(self, ap, cap, unit):
        self.ap_full = ap
        self.cap = cap
        self.off = 0
        self.hi = 0
        self.stack = []
        self.unit = unit

    def take(self, n, name=""):
        assert self.off + n <= self.cap, f"arena overflow {name}: {self.off}+{n}>{self.cap}"
        ap = self.ap_full[:, self.off:self.off + n]
        self.off += n
        self.hi = max(self.hi, self.off)
        return Buf(ap, name)

    def push(self):
        self.stack.append(self.off)

    def pop(self):
        self.off = self.stack.pop()




def make_cfg(D=2048, DI=4096, NST=16, RDT=128, KCV=4, SEQ=2048, NSEQ=2, Tc=512,
             E=32, FI=1408, SI=2816, taps=(), stop_after=None):
    c = dict(D=D, DI=DI, NST=NST, RDT=RDT, KCV=KCV, SEQ=SEQ, NSEQ=NSEQ, Tc=Tc,
             E=E, FI=FI, SI=SI, taps=set(taps), stop_after=stop_after)
    c["KB"] = D // 128
    c["DB"] = DI // 128
    c["MBo"] = D // 128
    c["IB"] = FI // 128
    c["SIB"] = SI // 128
    c["NCH"] = SEQ // Tc
    c["TT"] = Tc // 128
    c["SLM"] = Tc
    c["XSL"] = DI // Tc
    c["OSL"] = D // Tc
    c["WKH"] = max(1, c["DB"] // 4)
    return c


def _kblock(a):
    K, M = a.shape
    return np.ascontiguousarray(
        a.reshape(K // 128, 128, M).transpose(1, 0, 2).reshape(128, (K // 128) * M)
    )


def pack_host(cfg, w):
    D, DI, NST, RDT = cfg["D"], cfg["DI"], cfg["NST"], cfg["RDT"]
    E = cfg["E"]
    KB, DB, MBo = cfg["KB"], cfg["DB"], cfg["MBo"]
    SLM, XSL, OSL, WKH = cfg["SLM"], cfg["XSL"], cfg["OSL"], cfg["WKH"]

    out = {}
    W_in = w["L0_attn_W_in"]

    def slabify(Wrows):
        a = Wrows.reshape(XSL, SLM, KB, 128)
        return np.ascontiguousarray(a.transpose(0, 3, 2, 1))

    out["winx"] = slabify(W_in[:DI])
    out["winz"] = slabify(W_in[DI:])

    WoT = np.ascontiguousarray(w["L0_attn_W_out"].T)
    a = WoT.reshape(DB // WKH, WKH, 128, OSL, SLM)
    out["wout"] = np.ascontiguousarray(a.transpose(3, 0, 2, 1, 4))

    W_x = w["L0_attn_W_x"]
    out["wxdt"] = _kblock(np.ascontiguousarray(W_x[:RDT].T))
    out["wxbc"] = _kblock(np.ascontiguousarray(W_x[RDT:].T))
    out["wdt"] = _kblock(np.ascontiguousarray(w["L0_attn_W_dt"].T))

    def chan(v):
        return np.ascontiguousarray(v.reshape(DB, 128).T)

    out["convw"] = np.ascontiguousarray(
        w["L0_attn_conv_w"].reshape(DB, 128, cfg["KCV"]).transpose(1, 0, 2).reshape(128, DB * cfg["KCV"])
    )
    out["convb"] = chan(w["L0_attn_conv_b"])
    out["bdt"] = chan(w["L0_attn_b_dt"])
    out["dchan"] = chan(w["L0_attn_D"])
    out["bout"] = np.ascontiguousarray(w["L0_attn_b_out"].reshape(MBo, 128).T)
    out["alog"] = np.ascontiguousarray(
        w["L0_attn_A_log"].reshape(DB, 128, NST).transpose(1, 0, 2).reshape(128, DB * NST)
    )

    out["gate"] = _kblock(np.ascontiguousarray(w["L0_ffn_gate_w"].T))
    out["rgT"] = _kblock(np.ascontiguousarray(w["L0_ffn_Rg"].T))
    out["ruT"] = _kblock(np.ascontiguousarray(w["L0_ffn_Ru"].T))
    out["rdT"] = _kblock(np.ascontiguousarray(w["L0_ffn_Rd"].T))
    out["srgT"] = _kblock(np.ascontiguousarray(w["L0_ffn_s_Rg"].T))
    out["sruT"] = _kblock(np.ascontiguousarray(w["L0_ffn_s_Ru"].T))
    out["sugT"] = np.ascontiguousarray(w["L0_ffn_s_Ug"].T)
    out["suuT"] = np.ascontiguousarray(w["L0_ffn_s_Uu"].T)
    out["sudT"] = np.ascontiguousarray(w["L0_ffn_s_Ud"].T)
    out["scgT"] = np.ascontiguousarray(w["L0_ffn_s_Cg"].T)
    out["scuT"] = np.ascontiguousarray(w["L0_ffn_s_Cu"].T)
    out["srdT"] = _kblock(np.ascontiguousarray(w["L0_ffn_s_Rd"].T))
    out["scdT"] = np.ascontiguousarray(w["L0_ffn_s_Cd"].T)

    expw = np.concatenate(
        [
            w["L0_ffn_Ug"].transpose(0, 2, 1),
            w["L0_ffn_Cg"].transpose(0, 2, 1),
            w["L0_ffn_Uu"].transpose(0, 2, 1),
            w["L0_ffn_Cu"].transpose(0, 2, 1),
            w["L0_ffn_Ud"].transpose(0, 2, 1),
        ],
        axis=2,
    )
    out["expw"] = np.ascontiguousarray(expw)
    CdT = w["L0_ffn_Cd"].transpose(0, 2, 1)
    cdmb = CdT.reshape(E, 128, MBo, 128).transpose(2, 1, 0, 3).reshape(MBo, 128, E * 128)
    out["cdmb"] = np.ascontiguousarray(cdmb)

    out["ident"] = np.eye(128, dtype=np.float32)
    return out




def build_program(cfg):
    D, DI, NST, RDT, KCV = cfg["D"], cfg["DI"], cfg["NST"], cfg["RDT"], cfg["KCV"]
    SEQ, NSEQ, Tc = cfg["SEQ"], cfg["NSEQ"], cfg["Tc"]
    E, FI, SI = cfg["E"], cfg["FI"], cfg["SI"]
    KB, DB, MBo, IB, SIB = cfg["KB"], cfg["DB"], cfg["MBo"], cfg["IB"], cfg["SIB"]
    NCH, TT, SLM, XSL, OSL, WKH = (
        cfg["NCH"], cfg["TT"], cfg["SLM"], cfg["XSL"], cfg["OSL"], cfg["WKH"]
    )
    EXPC = 2 * FI + 3 * 128
    TOK = NSEQ * SEQ
    stop = cfg["stop_after"]

    nc = bass.Bass("TRN2", target_bir_lowering=False)
    p = Prog(sync_all=cfg.get("sync_all", False))

    def din(name, shape, dtype=F32):
        return nc.dram_tensor(name, list(shape), dtype, kind="ExternalInput").ap()

    x_in = din("x", (TOK, D))
    winx = din("winx", (XSL, 128, KB * SLM))
    winz = din("winz", (XSL, 128, KB * SLM))
    wout_in = din("wout", (OSL, DB // WKH, 128, WKH * SLM))
    wxdt_in = din("wxdt", (128, DB * RDT))
    wxbc_in = din("wxbc", (128, DB * 2 * NST))
    wdt_in = din("wdt", (128, DB * 128))
    convw_in = din("convw", (128, DB * KCV))
    convb_in = din("convb", (128, DB))
    bdt_in = din("bdt", (128, DB))
    dchan_in = din("dchan", (128, DB))
    bout_in = din("bout", (128, MBo))
    alog_in = din("alog", (128, DB * NST))
    gate_in = din("gate", (128, KB * E))
    rgT_in = din("rgT", (128, KB * 128))
    ruT_in = din("ruT", (128, KB * 128))
    rdT_in = din("rdT", (128, IB * 128))
    srgT_in = din("srgT", (128, KB * 128))
    sruT_in = din("sruT", (128, KB * 128))
    sugT_in = din("sugT", (128, 128))
    suuT_in = din("suuT", (128, 128))
    sudT_in = din("sudT", (128, 128))
    scgT_in = din("scgT", (128, SI))
    scuT_in = din("scuT", (128, SI))
    srdT_in = din("srdT", (128, SIB * 128))
    scdT_in = din("scdT", (128, D))
    expw_in = din("expw", (E, 128, EXPC))
    cdmb_in = din("cdmb", (MBo, 128, E * 128))
    ident_in = din("ident", (128, 128))

    out_hbm = nc.dram_tensor("out", [TOK, D], F32, kind="ExternalOutput").ap()

    def scr(name, shape, dtype=BF16):
        return nc.dram_tensor(name, list(shape), dtype, kind="Internal").ap()

    winx_bf = scr("winx_bf", (XSL, 128, KB * SLM))
    winz_bf = scr("winz_bf", (XSL, 128, KB * SLM))
    wout_bf = scr("wout_bf", (OSL, DB // WKH, 128, WKH * SLM))
    expw_bf = scr("expw_bf", (E, 128, EXPC))
    cdmb_bf = scr("cdmb_bf", (MBo, 128, E * 128))
    bc_dram = [Buf(scr(f"bc_dram{i}", (2 * NST, Tc)), f"bcd{i}") for i in range(2)]
    wall_dram = [Buf(scr(f"wall_dram{i}", (E, Tc), F32), f"wal{i}") for i in range(2)]
    x1_dram = Buf(scr("x1_dram", (TOK, D), F32), "x1d")

    def sbp(name, cols, dtype=F32):
        t = nc.alloc_sbuf_tensor(name, [128, cols], dtype)
        return Buf(t.ap(), name)

    wdt_r = sbp("wdt_r", DB * 128, BF16)
    wxdt_r = sbp("wxdt_r", DB * RDT, BF16)
    wxbc_r = sbp("wxbc_r", DB * 2 * NST, BF16)
    idb_r = sbp("idb_r", 128, BF16)
    u_f = sbp("u_f", KB * Tc, BF16)
    convw_r = sbp("convw_r", DB * KCV)
    convb_r = sbp("convb_r", DB)
    bdt_r = sbp("bdt_r", DB)
    dchan_r = sbp("dchan_r", DB)
    bout_r = sbp("bout_r", MBo)
    a_r = sbp("a_r", DB * NST)
    idf_r = sbp("idf_r", 128)
    state = sbp("state", DB * NST)
    carry = sbp("carry", DB * (KCV - 1))
    lnt = [sbp(f"lnt{i}", 8) for i in range(2)]
    cst = sbp("cst", 2)

    FFNWT_COLS = (KB * E + 2 * KB * 128 + IB * 128 + 2 * KB * 128 + 3 * 128
                  + 2 * SI + SIB * 128 + D)
    M_BF = DB * Tc + 2 * KB * SLM + max(
        2 * D,
        6 * Tc + NST * Tc + NST * Tc,
        2 * WKH * SLM + MBo * Tc + 2 * D)
    F_BF = E * Tc + FFNWT_COLS + 3 * Tc + max(
        2 * D,
        4 * Tc + Tc + max(IB, SIB // 2) * Tc + 2 * Tc + 2 * Tc + 2 * EXPC,
        2 * E * 128 + MBo * Tc + 2 * D)
    NBF = max(M_BF, F_BF) + 64
    M_F32 = max(2 * D + 2 * (Tc + KCV) + 4 * Tc,
                16 * Tc,
                4 * D)
    F_F32 = (TT * E + 6 * E + 16 + 2 * Tc) + max(2 * D, 4 * D)
    NF32 = 2048 + max(M_F32, F_F32) + 32
    arb_t = nc.alloc_sbuf_tensor("arena_bf", [128, NBF], BF16)
    arf_t = nc.alloc_sbuf_tensor("arena_f32", [128, NF32], F32)
    bfar = Carve(arb_t.ap(), NBF, 2)
    far = Carve(arf_t.ap(), NF32, 4)

    sqs = far.take(2048, "sqs")

    bfar.push()
    far.push()
    xcy_all_ap = bfar.ap_full[:, bfar.off:bfar.off + DB * Tc]
    xcy = [Buf(xcy_all_ap[:, i * Tc:(i + 1) * Tc], f"xcy{i}") for i in range(DB)]
    bfar.off += DB * Tc
    slab = [bfar.take(KB * SLM, f"slab{i}") for i in range(2)]
    bfar.push()
    far.push()
    ut = [bfar.take(D, f"ut{i}") for i in range(2)]
    xt = [far.take(D, f"xt{i}") for i in range(2)]
    xfull = [far.take(Tc + KCV, f"xfull{i}") for i in range(2)]
    cva = [far.take(Tc, f"cva{i}") for i in range(2)]
    cvb = [far.take(Tc, f"cvb{i}") for i in range(2)]
    bfar.pop()
    far.pop()
    bfar.push()
    far.push()
    dtin_bf = bfar.take(Tc, "dtin")
    bcf = bfar.take(Tc, "bcf")
    cbc = [bfar.take(Tc, f"cbc{i}") for i in range(NST)]
    bbc = [bfar.take(Tc, f"bbc{i}") for i in range(NST)]
    szb = [far.take(Tc, f"szb{i}") for i in range(2)]
    delta = [far.take(Tc, f"delta{i}") for i in range(2)]
    sfp = [far.take(Tc, f"sfp{i}") for i in range(3)]
    du = [bfar.take(Tc, f"du{i}") for i in range(2)]
    atile = [far.take(Tc, f"at{i}") for i in range(2)]
    dub = [bfar.take(Tc, f"dub{i}") for i in range(2)]
    stile = [far.take(Tc, f"st{i}") for i in range(2)]
    e4t = [far.take(Tc, f"e4{i}") for i in range(2)]
    yacc = far.take(Tc, "yacc")
    yfin = [far.take(Tc, "yfin0")]
    bfar.pop()
    far.pop()
    bfar.push()
    far.push()
    wo_slab = [bfar.take(WKH * SLM, f"wosl{i}") for i in range(2)]
    mo_bf = [bfar.take(Tc, f"mo_bf{i}") for i in range(MBo)]
    mobt = [bfar.take(D, f"mobt{i}") for i in range(2)]
    xr = [far.take(D, f"xr{i}") for i in range(2)]
    x1t = [far.take(D, f"x1t{i}") for i in range(2)]
    bfar.pop()
    far.pop()
    M_bf_hi, M_f_hi = bfar.hi, far.hi
    bfar.pop()
    far.pop()

    bfar.push()
    far.push()
    h2e = [bfar.take(Tc, f"h2e{i}") for i in range(E)]
    gate_r = bfar.take(KB * E, "gate_r")
    rgT_r = bfar.take(KB * 128, "rgT_r")
    ruT_r = bfar.take(KB * 128, "ruT_r")
    rdT_r = bfar.take(IB * 128, "rdT_r")
    srgT_r = bfar.take(KB * 128, "srgT_r")
    sruT_r = bfar.take(KB * 128, "sruT_r")
    sugT_r = bfar.take(128, "sugT_r")
    suuT_r = bfar.take(128, "suuT_r")
    sudT_r = bfar.take(128, "sudT_r")
    scgT_r = bfar.take(SI, "scgT_r")
    scuT_r = bfar.take(SI, "scuT_r")
    srdT_r = bfar.take(SIB * 128, "srdT_r")
    scdT_r = bfar.take(D, "scdT_r")
    rg_bf = bfar.take(Tc, "rg_bf")
    ru_bf = bfar.take(Tc, "ru_bf")
    h2sh = bfar.take(Tc, "h2sh")
    we = far.take(TT * E, "we")
    esb = [far.take(E, f"esb{i}") for i in range(2)]
    tcur = [far.take(E, f"tcur{i}") for i in range(2)]
    tc2 = [far.take(E, f"tc2{i}") for i in range(2)]
    gsm = [far.take(8, f"gsm{i}") for i in range(2)]
    wbc = [far.take(Tc, f"wbc{i}") for i in range(2)]
    bfar.push()
    far.push()
    u2t = [bfar.take(D, f"u2t{i}") for i in range(2)]
    xf1 = [far.take(D, f"xf1_{i}") for i in range(2)]
    bfar.pop()
    far.pop()
    bfar.push()
    far.push()
    h1a = [bfar.take(Tc, f"h1a{i}") for i in range(2)]
    h1b = [bfar.take(Tc, f"h1b{i}") for i in range(2)]
    gtile = [bfar.take(Tc, f"gt{i}") for i in range(max(IB, SIB // 2))]
    gut = [bfar.take(Tc, f"gut{i}") for i in range(2)]
    rdw = [bfar.take(Tc, f"rdw{i}") for i in range(2)]
    expwt = [bfar.take(EXPC, f"expwt{i}") for i in range(2)]
    bfar.pop()
    far.pop()
    bfar.push()
    far.push()
    cdt = [bfar.take(E * 128, f"cdt{i}") for i in range(2)]
    fo_bf = [bfar.take(Tc, f"fo_bf{i}") for i in range(MBo)]
    fot = [bfar.take(D, f"fot{i}") for i in range(2)]
    xf4 = [far.take(D, f"xf4_{i}") for i in range(2)]
    ot = [far.take(D, f"ot{i}") for i in range(2)]
    bfar.pop()
    far.pop()
    F_bf_hi, F_f_hi = bfar.hi, far.hi
    bfar.pop()
    far.pop()

    M01set = ut + xt + xfull + cva + cvb
    M23set = ([dtin_bf, bcf, yacc] + cbc + bbc + szb + delta + sfp + du + atile
              + dub + stile + e4t + yfin)
    M4set = wo_slab + mo_bf + mobt + xr + x1t
    Mcommon = list(xcy) + slab
    Mall = M01set + M23set + M4set + Mcommon
    F0set = u2t + xf1
    F23set = h1a + h1b + gtile + gut + rdw + expwt
    F4set = cdt + fo_bf + fot + xf4 + ot
    Fcommon = h2e + [gate_r, rgT_r, ruT_r, rdT_r, srgT_r, sruT_r, sugT_r, suuT_r,
               sudT_r, scgT_r, scuT_r, srdT_r, scdT_r, rg_bf, ru_bf, h2sh, we,
               wbc[0], wbc[1]] + esb + tcur + tc2 + gsm
    Fall = F0set + F23set + F4set + Fcommon

    def psb(name, cols, dtype=F32):
        t = nc.alloc_psum_tensor(name, [128, cols], dtype)
        return Buf(t.ap(), name)

    psg = [psb(f"psg{i}", Tc if Tc <= 512 else 512) for i in range(5)]
    ptb = [psb(f"ptb{i}", 1024, BF16) for i in range(2)]
    ptf = psb("ptf", Tc if Tc <= 512 else 512, F32)

    taps = []

    def tap(name, bufs, ap):
        if name not in cfg["taps"]:
            return
        t = nc.dram_tensor("tap_" + name, list(ap.shape), ap.dtype, kind="ExternalOutput").ap()
        p.op("sp", lambda e, t=t, ap=ap: e.dma_start(t, ap), reads=list(bufs), dma=True, tag="tap")
        taps.append(name)

    def dma(dst_buf, src_ap, eng="sp", cols=None, tag=""):
        if cols is None:
            dap, w, pw = dst_buf.ap, [dst_buf], []
        else:
            dap, w, pw = dst_buf.ap[:, cols[0]:cols[1]], [], [dst_buf]
        return p.op(eng, lambda e, d=dap, s=src_ap: e.dma_start(d, s),
                    writes=w, pwrites=pw, dma=True, tag=tag)

    def mm(psbuf, lhs, lcols, rhs, rcols, start, last, pslice=None, tag="mm"):
        lap = lhs.ap[:, lcols[0]:lcols[1]]
        rap = rhs.ap[:, rcols[0]:rcols[1]]
        pap = psbuf.ap[:, 0:rcols[1] - rcols[0]] if pslice is None else psbuf.ap[
            pslice[0]:pslice[1], pslice[2]:pslice[3]]
        return p.op("pe",
                    lambda e, o=pap, l=lap, r=rap, s=start, t=last: e.matmul(o, l, r, start=s, stop=t),
                    reads=[lhs, rhs],
                    writes=[psbuf] if start else (),
                    pwrites=[psbuf] if not start else (),
                    tag=tag)

    def ev_copy(dst, src, eng="act", dcols=None, scols=None, srows=None, tag="ev"):
        dap = dst.ap if dcols is None else dst.ap[:, dcols[0]:dcols[1]]
        sap = src.ap
        if scols is not None:
            sap = sap[:, scols[0]:scols[1]]
        if srows is not None:
            sap = sap[srows[0]:srows[1], :]
        fn = (lambda e, d=dap, s=sap: e.copy(d, s)) if eng == "act" else (
            lambda e, d=dap, s=sap: e.tensor_copy(d, s))
        return p.op(eng, fn, reads=[src],
                    writes=[dst] if dcols is None else (),
                    pwrites=[dst] if dcols is not None else (), tag=tag)

    def ln_phase(xbuf, ubuf, st, tag):
        p.op("dve", lambda e, x=xbuf, s=st: e.reduce_sum(
            s.ap[:, 0:1], x.ap, axis=AX.X, negate=True), reads=[xbuf], writes=[st],
            tag=tag + "s", small=True)
        p.op("act", lambda e, x=xbuf: e.activation(
            sqs.ap[:, 0:D], x.ap, AF.Square),
            reads=[xbuf], writes=[sqs], tag=tag + "sq")
        p.op("dve", lambda e, s=st: e.reduce_sum(
            s.ap[:, 1:2], sqs.ap[:, 0:D], axis=AX.X), reads=[sqs], pwrites=[st],
            tag=tag + "sq2", small=True)
        p.op("dve", lambda e, s=st: e.tensor_scalar(
            s.ap[:, 2:3], s.ap[:, 0:1], 1.0 / D, None, OP.mult), reads=[st], pwrites=[st],
            tag=tag + "m", small=True)
        p.op("dve", lambda e, s=st: e.tensor_scalar(
            s.ap[:, 3:4], s.ap[:, 1:2], 1.0 / D, None, OP.mult), reads=[st], pwrites=[st],
            tag=tag + "e2", small=True)
        p.op("dve", lambda e, s=st: e.tensor_tensor(
            s.ap[:, 4:5], s.ap[:, 2:3], s.ap[:, 2:3], OP.mult), reads=[st], pwrites=[st],
            tag=tag + "m2", small=True)
        p.op("dve", lambda e, s=st: e.tensor_tensor(
            s.ap[:, 5:6], s.ap[:, 3:4], s.ap[:, 4:5], OP.subtract), reads=[st], pwrites=[st],
            tag=tag + "v", small=True)
        p.op("act", lambda e, s=st: e.activation(
            s.ap[:, 6:7], s.ap[:, 5:6], AF.Ln, bias=cst.ap[:, 0:1]),
            reads=[st, cst], pwrites=[st], tag=tag + "ln", small=True)
        p.op("act", lambda e, s=st: e.activation(
            s.ap[:, 7:8], s.ap[:, 6:7], AF.Exp, scale=-0.5), reads=[st], pwrites=[st],
            tag=tag + "rs", small=True)
        p.op("dve", lambda e, x=xbuf, s=st, u=ubuf: e.tensor_scalar(
            u.ap, x.ap, s.ap[:, 2:3], s.ap[:, 7:8], OP.add, OP.mult),
            reads=[xbuf, st], writes=[ubuf], tag=tag + "u")

    def transpose_to(dst, dst_cols, src, src_cols, pt, tag):
        p.op("pe", lambda e, s=src, sc=src_cols, pt=pt: e.transpose(
            pt.ap[:, 0:128], s.ap[:, sc[0]:sc[1]], idb_r.ap),
            reads=[src, idb_r], writes=[pt], tag=tag)
        p.op("act", lambda e, d=dst, dc=dst_cols, pt=pt: e.copy(
            d.ap[:, dc[0]:dc[1]], pt.ap[:, 0:128]),
            reads=[pt], pwrites=[dst], tag=tag + "e")

    for dst, src in ((winx_bf, winx), (winz_bf, winz), (wout_bf, wout_in),
                     (expw_bf, expw_in), (cdmb_bf, cdmb_in)):
        p.op("pool", lambda e, d=dst, s=src: e.dma_start(d, s), dma=True, tag="cast")
    castw = Buf(None, "castw")
    castw.writers = [p.ops["pool"][-1]["h"]]

    for dst, src in ((wdt_r, wdt_in), (wxdt_r, wxdt_in), (wxbc_r, wxbc_in),
                     (idb_r, ident_in)):
        p.op("pool", lambda e, d=dst, s=src: e.dma_start(d.ap, s), writes=[dst], dma=True, tag="res")
    for dst, src in ((convw_r, convw_in), (convb_r, convb_in), (bdt_r, bdt_in),
                     (dchan_r, dchan_in), (bout_r, bout_in), (idf_r, ident_in)):
        dma(dst, src, tag="res")
    p.op("dve", lambda e: e.memset(cst.ap[:, 0:1], 1e-5), writes=[cst], tag="cst0")
    p.op("dve", lambda e: e.memset(cst.ap[:, 1:2], 1.0), pwrites=[cst], tag="cst1")
    dma(sqs, alog_in, cols=(0, DB * NST), tag="alog")
    p.op("act", lambda e: e.activation(sqs.ap[:, DB * NST:2 * DB * NST], sqs.ap[:, 0:DB * NST], AF.Exp),
         reads=[sqs], pwrites=[sqs], tag="aexp")
    p.op("dve", lambda e: e.tensor_scalar(a_r.ap, sqs.ap[:, DB * NST:2 * DB * NST], -1.0, None, OP.mult),
         reads=[sqs], writes=[a_r], tag="aneg")

    p.barrier("ph0")

    for sqi in range(NSEQ):
        p.op("dve", lambda e: e.memset(state.ap, 0.0), writes=[state], tag="st0")
        p.op("dve", lambda e: e.memset(carry.ap, 0.0), writes=[carry], tag="cy0")
        for ci in range(NCH):
            t0 = sqi * SEQ + ci * Tc

            for tt in range(TT):
                xb, ub, st = xt[tt % 2], ut[tt % 2], lnt[tt % 2]
                dma(xb, x_in[t0 + tt * 128: t0 + (tt + 1) * 128, :], tag="xload")
                tap(f"xt_{sqi}_{ci}_{tt}", [xb], xb.ap)
                ln_phase(xb, ub, st, "ln")
                tap(f"lnt_{sqi}_{ci}_{tt}", [st], st.ap)
                tap(f"ut_{sqi}_{ci}_{tt}", [ub], ub.ap)
                for kb in range(KB):
                    transpose_to(u_f, (kb * Tc + tt * 128, kb * Tc + (tt + 1) * 128),
                                 ub, (kb * 128, (kb + 1) * 128), ptb[kb % 2], "tru")
            tap(f"uf_{sqi}_{ci}", [u_f], u_f.ap)

            for s in range(XSL):
                slb = slab[s % 2]
                p.op("sp", lambda e, d=slb, s=s: e.dma_start(d.ap, winx_bf[s]),
                     reads=[castw], writes=[slb], dma=True, tag="wslab")
                nmb = SLM // 128
                for mb in range(nmb):
                    dblk = s * nmb + mb
                    pg = psg[dblk % 2]
                    for kb in range(KB):
                        mm(pg, slb, (kb * SLM + mb * 128, kb * SLM + (mb + 1) * 128),
                           u_f, (kb * Tc, (kb + 1) * Tc), kb == 0, kb == KB - 1, tag="inproj")
                    xf = xfull[dblk % 2]
                    p.op("dve", lambda e, xf=xf, dblk=dblk: e.tensor_copy(
                        xf.ap[:, 0:KCV - 1], carry.ap[:, dblk * (KCV - 1):(dblk + 1) * (KCV - 1)]),
                        reads=[carry], writes=[xf], tag="cv_c", small=True)
                    p.op("act", lambda e, xf=xf, pg=pg: e.copy(xf.ap[:, KCV - 1:KCV - 1 + Tc], pg.ap),
                         reads=[pg], pwrites=[xf], tag="cv_ev")
                    p.op("dve", lambda e, xf=xf, dblk=dblk: e.tensor_copy(
                        carry.ap[:, dblk * (KCV - 1):(dblk + 1) * (KCV - 1)],
                        xf.ap[:, Tc:Tc + KCV - 1]),
                        reads=[xf], pwrites=[carry], tag="cv_s")
                    ca, cb = cva[dblk % 2], cvb[dblk % 2]
                    p.op("dve", lambda e, xf=xf, dblk=dblk, o=ca: e.tensor_scalar(
                        o.ap, xf.ap[:, 0:Tc], convw_r.ap[:, dblk * KCV:dblk * KCV + 1], None, OP.mult),
                        reads=[xf, convw_r], writes=[ca], tag="cv0")
                    src_, dst_ = ca, cb
                    for k in range(1, KCV):
                        p.op("dve", lambda e, xf=xf, dblk=dblk, k=k, s2=src_, d2=dst_: e.scalar_tensor_tensor(
                            d2.ap, xf.ap[:, k:k + Tc], convw_r.ap[:, dblk * KCV + k:dblk * KCV + k + 1],
                            s2.ap, OP.mult, OP.add),
                            reads=[xf, convw_r, src_], writes=[dst_], tag="cvk")
                        src_, dst_ = dst_, src_
                    p.op("act", lambda e, s2=src_, dblk=dblk: e.activation(
                        xcy[dblk].ap, s2.ap, AF.Silu, bias=convb_r.ap[:, dblk:dblk + 1]),
                        reads=[src_, convb_r], writes=[xcy[dblk]], tag="cvsilu")
            tap(f"xc_{sqi}_{ci}", xcy, xcy_all_ap)
            p.handoff(M01set, M23set)
            if stop == "M1":
                p.handoff(Mall + Fall, Mall)
                continue

            pgdt = psg[2]
            for kb in range(DB):
                mm(pgdt, wxdt_r, (kb * RDT, (kb + 1) * RDT), xcy[kb], (0, Tc),
                   kb == 0, kb == DB - 1, tag="xd_dt")
            ev_copy(dtin_bf, pgdt, tag="dtin_ev")
            pgbc = psg[3]
            for kb in range(DB):
                mm(pgbc, wxbc_r, (kb * 2 * NST, (kb + 1) * 2 * NST), xcy[kb], (0, Tc),
                   kb == 0, kb == DB - 1, pslice=(0, 2 * NST, 0, Tc), tag="xd_bc")
            p.op("act", lambda e: e.copy(bcf.ap[0:2 * NST, :], pgbc.ap[0:2 * NST, 0:Tc]),
                 reads=[pgbc], writes=[bcf], tag="bc_ev")
            bcd = bc_dram[ci % 2]
            p.op("sp", lambda e, d=bcd.ap: e.dma_start(d, bcf.ap[0:2 * NST, :]),
                 reads=[bcf], writes=[bcd], dma=True, tag="bc_w")
            for n in range(NST):
                p.op("sp", lambda e, n=n, s=bcd.ap: e.dma_start(
                    bbc[n].ap,
                    s[n:n + 1, :].partition_broadcast(128).squeeze(1)),
                    reads=[bcd], writes=[bbc[n]], dma=True, tag="bbcast")
            for n in range(NST):
                p.op("sp", lambda e, n=n, s=bcd.ap: e.dma_start(
                    cbc[n].ap,
                    s[NST + n:NST + n + 1, :].partition_broadcast(128).squeeze(1)),
                    reads=[bcd], writes=[cbc[n]], dma=True, tag="cbcast")
            tap(f"dtin_{sqi}_{ci}", [dtin_bf], dtin_bf.ap)
            tap(f"bbc_{sqi}_{ci}", bbc, bbc[0].ap)

            for dblk in range(DB):
                pg = psg[dblk % 2]
                mm(pg, wdt_r, (dblk * 128, (dblk + 1) * 128), dtin_bf, (0, Tc), True, True, tag="dtmm")
                s0, s1, s2 = sfp
                p.op("act", lambda e, pg=pg, dblk=dblk: e.activation(
                    s0.ap, pg.ap, AF.Abs, bias=bdt_r.ap[:, dblk:dblk + 1]),
                    reads=[pg, bdt_r], writes=[s0], tag="sp_abs")
                p.op("act", lambda e: e.activation(s1.ap, s0.ap, AF.Exp, scale=-1.0),
                     reads=[s0], writes=[s1], tag="sp_exp")
                p.op("act", lambda e: e.activation(s2.ap, s1.ap, AF.Ln, bias=cst.ap[:, 1:2]),
                     reads=[s1, cst], writes=[s2], tag="sp_ln")
                p.op("act", lambda e, pg=pg, dblk=dblk: e.activation(
                    s0.ap, pg.ap, AF.Relu, bias=bdt_r.ap[:, dblk:dblk + 1]),
                    reads=[pg, bdt_r], writes=[s0], tag="sp_relu")
                dl = delta[dblk % 2]
                scan_hs = []
                p.op("dve", lambda e, d=dl: e.tensor_tensor(d.ap, s2.ap, s0.ap, OP.add),
                     reads=[s2, s0], writes=[dl], tag="sp_add")
                db_ = du[dblk % 2]
                p.op("dve", lambda e, d=dl, o=db_, dblk=dblk: e.tensor_tensor(
                    o.ap, d.ap, xcy[dblk].ap, OP.mult),
                    reads=[dl, xcy[dblk]], writes=[db_], tag="du")
                p.op("dve", lambda e, dblk=dblk: e.tensor_scalar(
                    yacc.ap, xcy[dblk].ap, dchan_r.ap[:, dblk:dblk + 1], None, OP.mult),
                    reads=[xcy[dblk], dchan_r], writes=[yacc], tag="yinit")
                for n in range(NST):
                    col = dblk * NST + n
                    at = atile[n % 2]
                    p.op("act", lambda e, d=dl, o=at, col=col: e.activation(
                        o.ap, d.ap, AF.Exp, scale=a_r.ap[:, col:col + 1]),
                        reads=[dl, a_r], writes=[at], tag="aexp")
                    dbt = dub[n % 2]
                    p.op("dve", lambda e, d=db_, o=dbt, n=n: e.tensor_tensor(
                        o.ap, d.ap, bbc[n].ap, OP.mult),
                        reads=[db_, bbc[n]], writes=[dbt], tag="dub")
                    stl = stile[n % 2]
                    sh = p.op("dve", lambda e, a=at, b=dbt, o=stl, col=col: e.tensor_tensor_scan(
                        o.ap, a.ap, b.ap, state.ap[:, col:col + 1], OP.mult, OP.add),
                        reads=[at, dbt, state], writes=[stl], tag="scan")
                    scan_hs.append(sh)
                    p.op("act", lambda e, s=stl, col=col: e.copy(
                        state.ap[:, col:col + 1], s.ap[:, Tc - 1:Tc]),
                        reads=[stl], pwrites=[state], tag="sc_cy")
                    et = e4t[n % 2]
                    p.op("dve", lambda e, s=stl, o=et, n=n: e.tensor_tensor(
                        o.ap, s.ap, cbc[n].ap, OP.mult),
                        reads=[stl, cbc[n]], writes=[et], tag="e4m")
                    if n == 1:
                        p.op("pool", lambda e, t=et: e.tensor_copy(yfin[0].ap, t.ap),
                             reads=[et], writes=[yfin[0]], tag="yo0")
                    elif n % 2 == 1:
                        p.op("pool", lambda e, t=et: e.tensor_tensor(
                            yfin[0].ap, yfin[0].ap, t.ap, OP.add),
                            reads=[et, yfin[0]], pwrites=[yfin[0]], tag="yoadd")
                    else:
                        p.op("dve", lambda e, t=et: e.tensor_tensor(
                            yacc.ap, yacc.ap, t.ap, OP.add),
                            reads=[et, yacc], pwrites=[yacc], tag="yeadd")
                nmb = SLM // 128
                zs, zmb = dblk // nmb, dblk % nmb
                zsl = slab[zs % 2]
                if zmb == 0:
                    p.op("sp", lambda e, d=zsl, s=zs: e.dma_start(d.ap, winz_bf[s]),
                         reads=[castw], writes=[zsl], dma=True, tag="zslab")
                pgz = psg[2 + dblk % 2]
                for kb in range(KB):
                    lap = zsl.ap[:, kb * SLM + zmb * 128: kb * SLM + (zmb + 1) * 128]
                    rap = u_f.ap[:, kb * Tc:(kb + 1) * Tc]
                    pap = pgz.ap[:, 0:Tc]
                    p.op("pe",
                         lambda e, o=pap, l=lap, r=rap, s=(kb == 0), t=(kb == KB - 1):
                         e.matmul(o, l, r, start=s, stop=t),
                         reads=[zsl, u_f],
                         writes=[pgz] if kb == 0 else (),
                         pwrites=[pgz] if kb != 0 else (),
                         after=[scan_hs[min(kb, len(scan_hs) - 1)]],
                         tag="zmm")
                sz = szb[dblk % 2]
                z1, z2 = sfp[1], sfp[2]
                p.op("act", lambda e, pg=pgz, o=z1: e.activation(o.ap, pg.ap, AF.Tanh, scale=0.5),
                     reads=[pgz], writes=[z1], tag="ztanh")
                p.op("dve", lambda e, a=z1, o=z2: e.tensor_scalar(
                    o.ap, a.ap, 0.5, 0.5, OP.mult, OP.add),
                    reads=[z1], writes=[z2], tag="zsig")
                p.op("dve", lambda e, pg=pgz, a=z2, o=sz: e.tensor_tensor(o.ap, pg.ap, a.ap, OP.mult),
                     reads=[pgz, z2], writes=[sz], tag="zmul")
                yf = sfp[0]
                p.op("dve", lambda e, o=yf: e.tensor_tensor(
                    o.ap, yacc.ap, yfin[0].ap, OP.add),
                    reads=[yacc, yfin[0]], writes=[yf], tag="ysum")
                p.op("dve", lambda e, a=yf, b=sz, dblk=dblk: e.tensor_tensor(
                    xcy[dblk].ap, a.ap, b.ap, OP.mult),
                    reads=[yf, sz], writes=[xcy[dblk]], tag="ymul")
            tap(f"ybf_{sqi}_{ci}", xcy, xcy_all_ap)
            p.handoff(M23set, M4set)
            if stop == "M3":
                p.handoff(Mall + Fall, Mall)
                continue

            nmb = SLM // 128
            for s in range(OSL):
                for kh in range(DB // WKH):
                    wsl = wo_slab[kh % 2]
                    p.op("sp", lambda e, d=wsl, s=s, kh=kh: e.dma_start(d.ap, wout_bf[s, kh]),
                         reads=[castw], writes=[wsl], dma=True, tag="woslab")
                    for mb in range(nmb):
                        pg = psg[mb]
                        for kk in range(WKH):
                            kb = kh * WKH + kk
                            mm(pg, wsl, (kk * SLM + mb * 128, kk * SLM + (mb + 1) * 128),
                               xcy[kb], (0, Tc), kb == 0, kb == DB - 1, tag="womm")
                for mb in range(nmb):
                    mo = s * nmb + mb
                    p.op("act", lambda e, pg=psg[mb], mo=mo: e.activation(
                        mo_bf[mo].ap, pg.ap, AF.Identity,
                        bias=bout_r.ap[:, mo:mo + 1]),
                        reads=[psg[mb], bout_r], writes=[mo_bf[mo]], tag="wo_ev")
            tap(f"mobf_{sqi}_{ci}", mo_bf, mo_bf[0].ap)
            for tt in range(TT):
                mt = mobt[tt % 2]
                for mo in range(MBo):
                    transpose_to(mt, (mo * 128, (mo + 1) * 128),
                                 mo_bf[mo], (tt * 128, (tt + 1) * 128),
                                 ptb[mo % 2], "trmo")
                tap(f"mot_{sqi}_{ci}_{tt}", [mt], mt.ap)
                xrb = xr[tt % 2]
                dma(xrb, x_in[t0 + tt * 128: t0 + (tt + 1) * 128, :], tag="xreload")
                xo = x1t[tt % 2]
                p.op("dve", lambda e, a=xrb, b=mt, o=xo: e.tensor_tensor(
                    o.ap, a.ap, b.ap, OP.add), reads=[xrb, mt], writes=[xo], tag="res1")
                tap(f"x1t_{sqi}_{ci}_{tt}", [xo], xo.ap)
                p.op("sp", lambda e, o=xo, tt=tt, t0=t0: e.dma_start(
                    x1_dram.ap[t0 + tt * 128: t0 + (tt + 1) * 128, :], o.ap),
                    reads=[xo], pwrites=[x1_dram], dma=True, tag="x1w")
            tap(f"x1_{sqi}_{ci}", [x1_dram], x1_dram.ap[t0:t0 + Tc, :])
            p.handoff(Mall, Fall)
            if stop == "M4":
                p.handoff(Mall + Fall, Mall)
                continue

            for tt in range(TT):
                xb, ub, st = xf1[tt % 2], u2t[tt % 2], lnt[tt % 2]
                p.op("sp", lambda e, d=xb, tt=tt, t0=t0: e.dma_start(
                    d.ap, x1_dram.ap[t0 + tt * 128: t0 + (tt + 1) * 128, :]),
                    reads=[x1_dram], writes=[xb], dma=True, tag="x1r")
                ln_phase(xb, ub, st, "l2")
                for kb in range(KB):
                    transpose_to(u_f, (kb * Tc + tt * 128, kb * Tc + (tt + 1) * 128),
                                 ub, (kb * 128, (kb + 1) * 128), ptb[kb % 2], "tru2")
            tap(f"u2f_{sqi}_{ci}", [u_f], u_f.ap)
            p.handoff(F0set, F23set)
            if stop == "F0":
                p.handoff(Mall + Fall, Mall)
                continue

            for dst, src in ((gate_r, gate_in), (rgT_r, rgT_in), (ruT_r, ruT_in),
                             (rdT_r, rdT_in), (srgT_r, srgT_in), (sruT_r, sruT_in),
                             (sugT_r, sugT_in), (suuT_r, suuT_in), (sudT_r, sudT_in),
                             (scgT_r, scgT_in), (scuT_r, scuT_in), (srdT_r, srdT_in),
                             (scdT_r, scdT_in)):
                p.op("pool", lambda e, d=dst, s=src: e.dma_start(d.ap, s),
                     writes=[dst], dma=True, tag="fwt")
            for tt in range(TT):
                pg = psg[4]
                for kb in range(KB):
                    mm(pg, u_f, (kb * Tc + tt * 128, kb * Tc + (tt + 1) * 128),
                       gate_r, (kb * E, (kb + 1) * E), kb == 0, kb == KB - 1,
                       pslice=(0, 128, 0, E), tag="gmm")
                es, tcu, tcv, gs = esb[tt % 2], tcur[tt % 2], tc2[tt % 2], gsm[tt % 2]
                p.op("dve", lambda e, pg=pg, g=gs: e.reduce_max(
                    g.ap[:, 0:1], pg.ap[:, 0:E], axis=AX.X, negate=True),
                    reads=[pg], writes=[gs], tag="g_max", small=True)
                p.op("act", lambda e, pg=pg, g=gs, o=es: e.activation(
                    o.ap, pg.ap[:, 0:E], AF.Exp, bias=g.ap[:, 0:1]),
                    reads=[pg, gs], writes=[es], tag="g_exp", small=True)
                p.op("dve", lambda e, a=es, o=tcu: e.tensor_copy(o.ap, a.ap),
                     reads=[es], writes=[tcu], tag="g_cp", small=True)
                cur, nxt = tcu, tcv
                for it in range(3):
                    p.op("dve", lambda e, t=cur, g=gs: e.reduce_max(
                        g.ap[:, 1:2], t.ap, axis=AX.X), reads=[cur], pwrites=[gs], tag="g_m", small=True)
                    p.op("dve", lambda e, t=cur, g=gs: e.tensor_scalar(
                        sqs.ap[:, 0:E], t.ap, g.ap[:, 1:2], None, OP.is_equal),
                        reads=[cur, gs], writes=[sqs], tag="g_eq", small=True)
                    p.op("dve", lambda e, t=cur, o=nxt: e.scalar_tensor_tensor(
                        o.ap, sqs.ap[:, 0:E], NEG_BIG, t.ap, OP.mult, OP.add),
                        reads=[sqs, cur], writes=[nxt], tag="g_rm", small=True)
                    cur, nxt = nxt, cur
                p.op("dve", lambda e, t=cur, g=gs: e.reduce_max(
                    g.ap[:, 1:2], t.ap, axis=AX.X), reads=[cur], pwrites=[gs], tag="g_m4", small=True)
                p.op("dve", lambda e, a=es, g=gs: e.tensor_scalar(
                    sqs.ap[:, 0:E], a.ap, g.ap[:, 1:2], None, OP.is_le),
                    reads=[es, gs], writes=[sqs], tag="g_le", small=True)
                p.op("dve", lambda e, a=es, g=gs: e.tensor_scalar(
                    sqs.ap[:, E:2 * E], a.ap, g.ap[:, 1:2], None, OP.is_equal),
                    reads=[es, gs], pwrites=[sqs], tag="g_eq4", small=True)
                p.op("dve", lambda e: e.tensor_tensor(
                    sqs.ap[:, 2 * E:3 * E], sqs.ap[:, E:2 * E], sqs.ap[:, 0:E], OP.subtract),
                    reads=[sqs], pwrites=[sqs], tag="g_k1", small=True)
                p.op("dve", lambda e, a=es, t=cur: e.scalar_tensor_tensor(
                    t.ap, sqs.ap[:, 2 * E:3 * E], 1.0, a.ap, OP.add, OP.mult),
                    reads=[sqs, es], writes=[cur], tag="g_wr", small=True)
                p.op("dve", lambda e, t=cur, g=gs: e.reduce_sum(
                    g.ap[:, 2:3], t.ap, axis=AX.X), reads=[cur], pwrites=[gs], tag="g_s4", small=True)
                p.op("dve", lambda e, g=gs: e.tensor_scalar(
                    g.ap[:, 3:4], g.ap[:, 2:3], 1e-20, None, OP.add),
                    reads=[gs], pwrites=[gs], tag="g_eps", small=True)
                p.op("dve", lambda e, g=gs: e.reciprocal(g.ap[:, 4:5], g.ap[:, 3:4]),
                     reads=[gs], pwrites=[gs], tag="g_rcp", small=True)
                p.op("dve", lambda e, t=cur, g=gs, tt=tt: e.tensor_scalar(
                    we.ap[:, tt * E:(tt + 1) * E], t.ap, g.ap[:, 4:5], None, OP.mult),
                    reads=[cur, gs], pwrites=[we], tag="g_we", small=True)
            wde = wall_dram[ci % 2]
            for tt in range(TT):
                p.op("pe", lambda e, tt=tt: e.transpose(
                    ptf.ap[0:E, 0:128], we.ap[:, tt * E:(tt + 1) * E], idf_r.ap),
                    reads=[we, idf_r], writes=[ptf], tag="trwe")
                p.op("dve", lambda e, tt=tt: e.tensor_copy(
                    sqs.ap[0:E, tt * 128:(tt + 1) * 128], ptf.ap[0:E, 0:128]),
                    reads=[ptf], pwrites=[sqs], tag="trwe_ev")
            p.op("sp", lambda e, d=wde.ap: e.dma_start(d, sqs.ap[0:E, 0:Tc]),
                 reads=[sqs], writes=[wde], dma=True, tag="wall_w")
            tap(f"we_{sqi}_{ci}", [we], we.ap)
            if stop == "F1":
                p.handoff(Mall + Fall, Mall)
                continue

            def kproj(wbuf, dstbuf, pgi, tag):
                pg = psg[pgi]
                for kb in range(KB):
                    mm(pg, wbuf, (kb * 128, (kb + 1) * 128), u_f, (kb * Tc, (kb + 1) * Tc),
                       kb == 0, kb == KB - 1, tag=tag)
                ev_copy(dstbuf, pg, tag=tag + "ev")
                return pg

            kproj(rgT_r, rg_bf, 0, "rg")
            kproj(ruT_r, ru_bf, 1, "ru")
            kproj(srgT_r, h1a[0], 2, "srg")
            pg = psg[2]
            mm(pg, sugT_r, (0, 128), h1a[0], (0, Tc), True, True, tag="sug")
            ev_copy(h1a[1], pg, tag="sugev")
            kproj(sruT_r, h1b[0], 3, "sru")
            pg = psg[3]
            mm(pg, suuT_r, (0, 128), h1b[0], (0, Tc), True, True, tag="suu")
            ev_copy(h1b[1], pg, tag="suuev")
            HF = SIB // 2
            pgr = psg[4]
            for hf in range(2):
                for ib in range(HF):
                    sib = hf * HF + ib
                    pg = psg[ib % 2]
                    mm(pg, scgT_r, (sib * 128, (sib + 1) * 128), h1a[1], (0, Tc), True, True, tag="scg")
                    p.op("act", lambda e, pg=pg, ib=ib: e.activation(
                        gtile[ib].ap, pg.ap, AF.Silu),
                        reads=[pg], writes=[gtile[ib]], tag="scgsilu")
                for ib in range(HF):
                    sib = hf * HF + ib
                    pg = psg[ib % 2]
                    mm(pg, scuT_r, (sib * 128, (sib + 1) * 128), h1b[1], (0, Tc), True, True, tag="scu")
                    gb = gut[ib % 2]
                    p.op("dve", lambda e, pg=pg, ib=ib, o=gb: e.scalar_tensor_tensor(
                        o.ap, pg.ap, 1.0, gtile[ib].ap, OP.mult, OP.mult),
                        reads=[pg, gtile[ib]], writes=[gb], tag="sgu")
                    mm(pgr, srdT_r, (sib * 128, (sib + 1) * 128), gb, (0, Tc),
                       sib == 0, sib == SIB - 1, tag="srd")
            ev_copy(h1a[0], pgr, tag="srdev")
            pg = psg[2]
            mm(pg, sudT_r, (0, 128), h1a[0], (0, Tc), True, True, tag="sud")
            ev_copy(h2sh, pg, tag="sudev")
            tap(f"h2sh_{sqi}_{ci}", [h2sh], h2sh.ap)
            if stop == "F2":
                p.handoff(Mall + Fall, Mall)
                continue

            rot = [0]

            def npg():
                pg = psg[rot[0] % 4]
                rot[0] += 1
                return pg

            for ei in range(E):
                ew = expwt[ei % 2]
                p.op("sp", lambda e, d=ew, ei=ei: e.dma_start(d.ap, expw_bf[ei]),
                     reads=[castw], writes=[ew], dma=True, tag="expw")
                wb = wbc[ei % 2]
                p.op("sp", lambda e, d=wb, ei=ei, w=wde.ap: e.dma_start(
                    d.ap, w[ei:ei + 1, :].partition_broadcast(128).squeeze(1)),
                    reads=[wde], writes=[wb], dma=True, tag="wbcast")
                h1g, h1u = h1a[ei % 2], h1b[ei % 2]
                pg = npg()
                mm(pg, ew, (0, 128), rg_bf, (0, Tc), True, True, tag="ug")
                ev_copy(h1g, pg, tag="ugev")
                pg = npg()
                mm(pg, ew, (FI + 128, FI + 256), ru_bf, (0, Tc), True, True, tag="uu")
                ev_copy(h1u, pg, tag="uuev")
                for ib in range(IB):
                    pg = npg()
                    mm(pg, ew, (128 + ib * 128, 128 + (ib + 1) * 128), h1g, (0, Tc),
                       True, True, tag="cg")
                    p.op("act", lambda e, pg=pg, ib=ib: e.activation(
                        gtile[ib].ap, pg.ap, AF.Silu),
                        reads=[pg], writes=[gtile[ib]], tag="cgsilu")
                pgrd = psg[4] if ei % 2 == 0 else ptf
                for ib in range(IB):
                    pg = npg()
                    mm(pg, ew, (FI + 256 + ib * 128, FI + 256 + (ib + 1) * 128), h1u, (0, Tc),
                       True, True, tag="cu")
                    gb = gut[ib % 2]
                    p.op("dve", lambda e, pg=pg, ib=ib, o=gb: e.scalar_tensor_tensor(
                        o.ap, pg.ap, 1.0, gtile[ib].ap, OP.mult, OP.mult),
                        reads=[pg, gtile[ib]], writes=[gb], tag="gu")
                    mm(pgrd, rdT_r, (ib * 128, (ib + 1) * 128), gb, (0, Tc),
                       ib == 0, ib == IB - 1, tag="rdmm")
                rw = rdw[ei % 2]
                p.op("dve", lambda e, pg=pgrd, o=rw, w=wb: e.scalar_tensor_tensor(
                    o.ap, pg.ap, 1.0, w.ap, OP.mult, OP.mult),
                    reads=[pgrd, wb], writes=[rw], tag="rdwm")
                pg = npg()
                mm(pg, ew, (2 * FI + 256, 2 * FI + 384), rw, (0, Tc), True, True, tag="ud")
                p.op("act", lambda e, pg=pg, ei=ei: e.copy(
                    h2e[ei].ap, pg.ap), reads=[pg], writes=[h2e[ei]], tag="h2ev")
            tap(f"h2e_{sqi}_{ci}", h2e, h2e[0].ap)
            p.handoff(F23set + F0set, F4set)
            if stop == "F3":
                p.handoff(Mall + Fall, Mall)
                continue

            for mo in range(MBo):
                cd = cdt[mo % 2]
                p.op("sp", lambda e, d=cd, mo=mo: e.dma_start(d.ap, cdmb_bf[mo]),
                     reads=[castw], writes=[cd], dma=True, tag="cdT")
                pg = psg[mo % 2]
                for ei in range(E):
                    mm(pg, cd, (ei * 128, (ei + 1) * 128), h2e[ei], (0, Tc),
                       ei == 0, False, tag="cdmm")
                mm(pg, scdT_r, (mo * 128, (mo + 1) * 128), h2sh, (0, Tc), False, True, tag="scdmm")
                p.op("act", lambda e, pg=pg, mo=mo: e.copy(
                    fo_bf[mo].ap, pg.ap), reads=[pg], writes=[fo_bf[mo]], tag="foev")
            for tt in range(TT):
                ft = fot[tt % 2]
                for mo in range(MBo):
                    transpose_to(ft, (mo * 128, (mo + 1) * 128),
                                 fo_bf[mo], (tt * 128, (tt + 1) * 128),
                                 ptb[mo % 2], "trfo")
                xb = xf4[tt % 2]
                p.op("sp", lambda e, d=xb, tt=tt, t0=t0: e.dma_start(
                    d.ap, x1_dram.ap[t0 + tt * 128: t0 + (tt + 1) * 128, :]),
                    reads=[x1_dram], writes=[xb], dma=True, tag="x1r4")
                ob = ot[tt % 2]
                p.op("dve", lambda e, a=xb, b=ft, o=ob: e.tensor_tensor(
                    o.ap, a.ap, b.ap, OP.add), reads=[xb, ft], writes=[ob], tag="res2")
                p.op("sp", lambda e, o=ob, tt=tt, t0=t0: e.dma_start(
                    out_hbm[t0 + tt * 128: t0 + (tt + 1) * 128, :], o.ap),
                    reads=[ob], dma=True, tag="outw")
            p.handoff(Fall, Mall)

    ndma_sp = sum(1 for o in p.ops["sp"] if o["h"].dma)
    sems, dsems = p.emit(nc)
    with nc.Block() as blk:
        @blk.sync
        def _(sync):
            for lane in range(NLANES):
                cnt = ndma_sp // NLANES + (1 if lane < ndma_sp % NLANES else 0)
                if cnt:
                    sync.wait_ge(dsems["sp"][lane], 16 * cnt)

    return nc




def kernel(**inputs):
    cfg = make_cfg()
    N_CORES = 8
    B = 16
    BPC = B // N_CORES

    x = np.asarray(inputs["L0_attn_hidden_states"], np.float32)
    packed = pack_host(cfg, {k: np.asarray(v, np.float32) for k, v in inputs.items()
                             if k != "L0_attn_hidden_states"})

    nc = build_program(cfg)

    in_maps = []
    for c in range(N_CORES):
        m = dict(packed)
        xs = x[c * BPC:(c + 1) * BPC].reshape(BPC * cfg["SEQ"], cfg["D"])
        m["x"] = np.ascontiguousarray(xs)
        in_maps.append(m)

    res = run_bass_kernel_spmd(nc, in_maps, core_ids=list(range(N_CORES)))
    outs = [np.asarray(res.results[c]["out"]).reshape(BPC, cfg["SEQ"], cfg["D"])
            for c in range(N_CORES)]
    return np.concatenate(outs, axis=0)


# version 16
# speedup vs baseline: 1.1265x; 1.0429x over previous
import numpy as np
import concourse.bass as bass
import concourse.mybir as mybir
from concourse.bass_utils import run_bass_kernel_spmd
from concourse.alu_op_type import AluOpType as OP

F32 = mybir.dt.float32
BF16 = mybir.dt.bfloat16
AF = mybir.ActivationFunctionType
AX = mybir.AxisListType

NEG_BIG = -1.0e30
NLANES = 8

ENGINES = ("pe", "act", "dve", "pool", "sp")
SECTION = dict(pe="tensor", act="scalar", dve="vector", pool="gpsimd", sp="sync")


class H:
    __slots__ = ("eng", "idx", "dma", "rank", "lane")

    def __init__(self, eng, idx, dma):
        self.eng, self.idx, self.dma = eng, idx, dma
        self.rank = 0
        self.lane = 0


class Buf:
    __slots__ = ("ap", "writers", "readers", "name")

    def __init__(self, ap, name=""):
        self.ap = ap
        self.writers = []
        self.readers = []
        self.name = name


class Prog:
    def __init__(self, sync_all=False):
        self.ops = {e: [] for e in ENGINES}
        self.sync_all = sync_all
        self.dma_counts = {e: 0 for e in ENGINES}

    @staticmethod
    def _reduce(handles):
        best = {}
        for h in handles:
            key = (h.eng, h.dma, h.lane)
            if key not in best or h.idx > best[key].idx:
                best[key] = h
        return list(best.values())

    def op(self, eng, fn, reads=(), writes=(), pwrites=(), dma=False, tag="", small=False, sync_self=False, after=()):
        deps = list(after)
        for b in reads:
            deps += b.writers
        for b in writes:
            deps += b.writers + b.readers
        for b in pwrites:
            deps += b.writers + b.readers
        h = H(eng, len(self.ops[eng]), dma)
        if dma:
            h.lane = self.dma_counts[eng] % NLANES
            self.dma_counts[eng] += 1
        if len(deps) > 12:
            deps = Prog._reduce(deps)
        self.ops[eng].append(dict(fn=fn, deps=deps, h=h, tag=tag, marked=False,
                                  small=small, sync_self=sync_self))
        for b in reads:
            b.readers.append(h)
        for b in writes:
            b.writers = [h]
            b.readers = []
        for b in pwrites:
            b.writers.append(h)
            b.readers = []
        return h

    def wait_only(self, eng, deps, tag="w"):
        h = H(eng, len(self.ops[eng]), False)
        self.ops[eng].append(dict(fn=None, deps=list(deps), h=h, tag=tag, marked=False,
                                  small=False, sync_self=False))
        return h

    @staticmethod
    def handoff(outs, ins):
        acc = []
        for b in outs:
            acc += b.writers
            acc += b.readers
        deps = Prog._reduce(acc)
        for nb in ins:
            nb.writers = Prog._reduce(nb.writers + deps)

    def barrier(self, tag="bar"):
        lasts = []
        for e in ENGINES:
            lastc = None
            lastd = {}
            nd = 0
            for o in self.ops[e]:
                if o["h"].dma:
                    lastd[nd % NLANES] = o["h"]
                    nd += 1
                elif o["fn"] is not None:
                    lastc = o["h"]
            if lastc is not None:
                lasts.append(lastc)
            lasts += list(lastd.values())
        for e in ENGINES:
            deps = [h for h in lasts if not (h.eng == e and not h.dma)]
            if deps:
                self.wait_only(e, deps, tag=tag)

    def emit(self, nc):
        for e in ENGINES:
            lane_cnt = [0] * NLANES
            for o in self.ops[e]:
                if o["h"].dma:
                    lane = o["h"].lane
                    lane_cnt[lane] += 1
                    o["h"].rank = (lane, lane_cnt[lane])
        for e in ENGINES:
            for o in self.ops[e]:
                byeng = {}
                for d in o["deps"]:
                    if (d.eng == e and not d.dma and not self.ops[e][d.idx]["small"]
                            and not o["sync_self"] and not self.sync_all):
                        continue
                    key = (d.eng, d.dma, d.rank[0] if d.dma else 0)
                    if key not in byeng or d.idx > byeng[key].idx:
                        byeng[key] = d
                o["wdeps"] = list(byeng.values())
                for d in o["wdeps"]:
                    self.ops[d.eng][d.idx]["marked"] = True
        for e in ENGINES:
            r = 0
            for o in self.ops[e]:
                if not o["h"].dma and o["marked"]:
                    r += 1
                    o["h"].rank = r
        sems = {e: nc.alloc_semaphore(f"sem_{e}") for e in ENGINES}
        dsems = {e: [nc.alloc_semaphore(f"dsem_{e}{i}") for i in range(NLANES)]
                 for e in ("sp", "pool")}

        with nc.Block() as blk:
            for e in ENGINES:

                def make(e):
                    def body(eng):
                        hw = {}
                        for o in self.ops[e]:
                            for d in o["wdeps"]:
                                if d.dma:
                                    lane, cnt = d.rank
                                    key = ("d", d.eng, lane)
                                    val = 16 * cnt
                                    sem = dsems[d.eng][lane]
                                else:
                                    key = ("c", d.eng)
                                    val = d.rank
                                    sem = sems[d.eng]
                                if hw.get(key, 0) < val:
                                    eng.wait_ge(sem, val)
                                    hw[key] = val
                            if o["fn"] is not None:
                                if o["h"].dma:
                                    lane, cnt = o["h"].rank
                                    if cnt >= 2:
                                        key = ("d", e, lane)
                                        val = 16 * (cnt - 1)
                                        if hw.get(key, 0) < val:
                                            eng.wait_ge(dsems[e][lane], val)
                                            hw[key] = val
                                inst = o["fn"](eng)
                                if o["h"].dma:
                                    inst.then_inc(dsems[e][o["h"].rank[0]], 16)
                                elif o["marked"]:
                                    inst.then_inc(sems[e], 1)

                    return body

                getattr(blk, SECTION[e])(make(e))
        return sems, dsems


class Carve:
    def __init__(self, ap, cap, unit):
        self.ap_full = ap
        self.cap = cap
        self.off = 0
        self.hi = 0
        self.stack = []
        self.unit = unit

    def take(self, n, name=""):
        assert self.off + n <= self.cap, f"arena overflow {name}: {self.off}+{n}>{self.cap}"
        ap = self.ap_full[:, self.off:self.off + n]
        self.off += n
        self.hi = max(self.hi, self.off)
        return Buf(ap, name)

    def push(self):
        self.stack.append(self.off)

    def pop(self):
        self.off = self.stack.pop()




def make_cfg(D=2048, DI=4096, NST=16, RDT=128, KCV=4, SEQ=2048, NSEQ=2, Tc=512,
             E=32, FI=1408, SI=2816, taps=(), stop_after=None):
    c = dict(D=D, DI=DI, NST=NST, RDT=RDT, KCV=KCV, SEQ=SEQ, NSEQ=NSEQ, Tc=Tc,
             E=E, FI=FI, SI=SI, taps=set(taps), stop_after=stop_after)
    c["KB"] = D // 128
    c["DB"] = DI // 128
    c["MBo"] = D // 128
    c["IB"] = FI // 128
    c["SIB"] = SI // 128
    c["NCH"] = SEQ // Tc
    c["TT"] = Tc // 128
    c["SLM"] = Tc
    c["XSL"] = DI // Tc
    c["OSL"] = D // Tc
    c["WKH"] = max(1, c["DB"] // 4)
    return c


def _kblock(a):
    K, M = a.shape
    return np.ascontiguousarray(
        a.reshape(K // 128, 128, M).transpose(1, 0, 2).reshape(128, (K // 128) * M)
    )


def pack_host(cfg, w):
    D, DI, NST, RDT = cfg["D"], cfg["DI"], cfg["NST"], cfg["RDT"]
    E = cfg["E"]
    KB, DB, MBo = cfg["KB"], cfg["DB"], cfg["MBo"]
    SLM, XSL, OSL, WKH = cfg["SLM"], cfg["XSL"], cfg["OSL"], cfg["WKH"]

    out = {}
    W_in = w["L0_attn_W_in"]

    def slabify(Wrows):
        a = Wrows.reshape(XSL, SLM, KB, 128)
        return np.ascontiguousarray(a.transpose(0, 3, 2, 1))

    out["winx"] = slabify(W_in[:DI])
    out["winz"] = slabify(W_in[DI:])

    WoT = np.ascontiguousarray(w["L0_attn_W_out"].T)
    a = WoT.reshape(DB // WKH, WKH, 128, OSL, SLM)
    out["wout"] = np.ascontiguousarray(a.transpose(3, 0, 2, 1, 4))

    W_x = w["L0_attn_W_x"]
    out["wxdt"] = _kblock(np.ascontiguousarray(W_x[:RDT].T))
    out["wxbc"] = _kblock(np.ascontiguousarray(W_x[RDT:].T))
    out["wdt"] = _kblock(np.ascontiguousarray(w["L0_attn_W_dt"].T))

    def chan(v):
        return np.ascontiguousarray(v.reshape(DB, 128).T)

    out["convw"] = np.ascontiguousarray(
        w["L0_attn_conv_w"].reshape(DB, 128, cfg["KCV"]).transpose(1, 0, 2).reshape(128, DB * cfg["KCV"])
    )
    out["convb"] = chan(w["L0_attn_conv_b"])
    out["bdt"] = chan(w["L0_attn_b_dt"])
    out["dchan"] = chan(w["L0_attn_D"])
    out["bout"] = np.ascontiguousarray(w["L0_attn_b_out"].reshape(MBo, 128).T)
    out["alog"] = np.ascontiguousarray(
        w["L0_attn_A_log"].reshape(DB, 128, NST).transpose(1, 0, 2).reshape(128, DB * NST)
    )

    out["gate"] = _kblock(np.ascontiguousarray(w["L0_ffn_gate_w"].T))
    out["rgT"] = _kblock(np.ascontiguousarray(w["L0_ffn_Rg"].T))
    out["ruT"] = _kblock(np.ascontiguousarray(w["L0_ffn_Ru"].T))
    out["rdT"] = _kblock(np.ascontiguousarray(w["L0_ffn_Rd"].T))
    out["srgT"] = _kblock(np.ascontiguousarray(w["L0_ffn_s_Rg"].T))
    out["sruT"] = _kblock(np.ascontiguousarray(w["L0_ffn_s_Ru"].T))
    out["sugT"] = np.ascontiguousarray(w["L0_ffn_s_Ug"].T)
    out["suuT"] = np.ascontiguousarray(w["L0_ffn_s_Uu"].T)
    out["sudT"] = np.ascontiguousarray(w["L0_ffn_s_Ud"].T)
    out["scgT"] = np.ascontiguousarray(w["L0_ffn_s_Cg"].T)
    out["scuT"] = np.ascontiguousarray(w["L0_ffn_s_Cu"].T)
    out["srdT"] = _kblock(np.ascontiguousarray(w["L0_ffn_s_Rd"].T))
    out["scdT"] = np.ascontiguousarray(w["L0_ffn_s_Cd"].T)

    expw = np.concatenate(
        [
            w["L0_ffn_Ug"].transpose(0, 2, 1),
            w["L0_ffn_Cg"].transpose(0, 2, 1),
            w["L0_ffn_Uu"].transpose(0, 2, 1),
            w["L0_ffn_Cu"].transpose(0, 2, 1),
            w["L0_ffn_Ud"].transpose(0, 2, 1),
        ],
        axis=2,
    )
    out["expw"] = np.ascontiguousarray(expw)
    CdT = w["L0_ffn_Cd"].transpose(0, 2, 1)
    cdmb = CdT.reshape(E, 128, MBo, 128).transpose(2, 1, 0, 3).reshape(MBo, 128, E * 128)
    out["cdmb"] = np.ascontiguousarray(cdmb)

    out["ident"] = np.eye(128, dtype=np.float32)
    return out




def build_program(cfg):
    D, DI, NST, RDT, KCV = cfg["D"], cfg["DI"], cfg["NST"], cfg["RDT"], cfg["KCV"]
    SEQ, NSEQ, Tc = cfg["SEQ"], cfg["NSEQ"], cfg["Tc"]
    E, FI, SI = cfg["E"], cfg["FI"], cfg["SI"]
    KB, DB, MBo, IB, SIB = cfg["KB"], cfg["DB"], cfg["MBo"], cfg["IB"], cfg["SIB"]
    NCH, TT, SLM, XSL, OSL, WKH = (
        cfg["NCH"], cfg["TT"], cfg["SLM"], cfg["XSL"], cfg["OSL"], cfg["WKH"]
    )
    EXPC = 2 * FI + 3 * 128
    TOK = NSEQ * SEQ
    stop = cfg["stop_after"]

    nc = bass.Bass("TRN2", target_bir_lowering=False)
    p = Prog(sync_all=cfg.get("sync_all", False))

    def din(name, shape, dtype=F32):
        return nc.dram_tensor(name, list(shape), dtype, kind="ExternalInput").ap()

    x_in = din("x", (TOK, D))
    winx = din("winx", (XSL, 128, KB * SLM))
    winz = din("winz", (XSL, 128, KB * SLM))
    wout_in = din("wout", (OSL, DB // WKH, 128, WKH * SLM))
    wxdt_in = din("wxdt", (128, DB * RDT))
    wxbc_in = din("wxbc", (128, DB * 2 * NST))
    wdt_in = din("wdt", (128, DB * 128))
    convw_in = din("convw", (128, DB * KCV))
    convb_in = din("convb", (128, DB))
    bdt_in = din("bdt", (128, DB))
    dchan_in = din("dchan", (128, DB))
    bout_in = din("bout", (128, MBo))
    alog_in = din("alog", (128, DB * NST))
    gate_in = din("gate", (128, KB * E))
    rgT_in = din("rgT", (128, KB * 128))
    ruT_in = din("ruT", (128, KB * 128))
    rdT_in = din("rdT", (128, IB * 128))
    srgT_in = din("srgT", (128, KB * 128))
    sruT_in = din("sruT", (128, KB * 128))
    sugT_in = din("sugT", (128, 128))
    suuT_in = din("suuT", (128, 128))
    sudT_in = din("sudT", (128, 128))
    scgT_in = din("scgT", (128, SI))
    scuT_in = din("scuT", (128, SI))
    srdT_in = din("srdT", (128, SIB * 128))
    scdT_in = din("scdT", (128, D))
    expw_in = din("expw", (E, 128, EXPC))
    cdmb_in = din("cdmb", (MBo, 128, E * 128))
    ident_in = din("ident", (128, 128))

    out_hbm = nc.dram_tensor("out", [TOK, D], F32, kind="ExternalOutput").ap()

    def scr(name, shape, dtype=BF16):
        return nc.dram_tensor(name, list(shape), dtype, kind="Internal").ap()

    winx_bf = scr("winx_bf", (XSL, 128, KB * SLM))
    winz_bf = scr("winz_bf", (XSL, 128, KB * SLM))
    wout_bf = scr("wout_bf", (OSL, DB // WKH, 128, WKH * SLM))
    expw_bf = scr("expw_bf", (E, 128, EXPC))
    cdmb_bf = scr("cdmb_bf", (MBo, 128, E * 128))
    bc_dram = [Buf(scr(f"bc_dram{i}", (2 * NST, Tc)), f"bcd{i}") for i in range(2)]
    wall_dram = [Buf(scr(f"wall_dram{i}", (E, Tc), F32), f"wal{i}") for i in range(2)]
    x1_dram = Buf(scr("x1_dram", (TOK, D), F32), "x1d")

    def sbp(name, cols, dtype=F32):
        t = nc.alloc_sbuf_tensor(name, [128, cols], dtype)
        return Buf(t.ap(), name)

    wdt_r = sbp("wdt_r", DB * 128, BF16)
    wxdt_r = sbp("wxdt_r", DB * RDT, BF16)
    wxbc_r = sbp("wxbc_r", DB * 2 * NST, BF16)
    idb_r = sbp("idb_r", 128, BF16)
    u_f = sbp("u_f", KB * Tc, BF16)
    convw_r = sbp("convw_r", DB * KCV)
    convb_r = sbp("convb_r", DB)
    bdt_r = sbp("bdt_r", DB)
    dchan_r = sbp("dchan_r", DB)
    bout_r = sbp("bout_r", MBo)
    a_r = sbp("a_r", DB * NST)
    idf_r = sbp("idf_r", 128)
    state = sbp("state", DB * NST)
    carry = sbp("carry", DB * (KCV - 1))
    lnt = [sbp(f"lnt{i}", 8) for i in range(2)]
    cst = sbp("cst", 2)

    FFNWT_COLS = (KB * E + 2 * KB * 128 + IB * 128 + 2 * KB * 128 + 3 * 128
                  + 2 * SI + SIB * 128 + D)
    M_BF = DB * Tc + 2 * KB * SLM + max(
        2 * D,
        10 * Tc + NST * Tc + NST * Tc,
        2 * WKH * SLM + MBo * Tc + 2 * D)
    F_BF = E * Tc + FFNWT_COLS + 3 * Tc + max(
        2 * D,
        4 * Tc + Tc + max(IB, SIB // 2) * Tc + 2 * Tc + 2 * Tc + 2 * EXPC,
        2 * E * 128 + MBo * Tc + 2 * D)
    NBF = max(M_BF, F_BF) + 64
    M_F32 = max(2 * D + 2 * (Tc + KCV) + 4 * Tc,
                12 * Tc,
                4 * D)
    F_F32 = (TT * E + 6 * E + 16 + 2 * Tc) + max(2 * D, 4 * D)
    NF32 = 2048 + max(M_F32, F_F32) + 32
    arb_t = nc.alloc_sbuf_tensor("arena_bf", [128, NBF], BF16)
    arf_t = nc.alloc_sbuf_tensor("arena_f32", [128, NF32], F32)
    bfar = Carve(arb_t.ap(), NBF, 2)
    far = Carve(arf_t.ap(), NF32, 4)

    sqs = far.take(2048, "sqs")

    bfar.push()
    far.push()
    xcy_all_ap = bfar.ap_full[:, bfar.off:bfar.off + DB * Tc]
    xcy = [Buf(xcy_all_ap[:, i * Tc:(i + 1) * Tc], f"xcy{i}") for i in range(DB)]
    bfar.off += DB * Tc
    slab = [bfar.take(KB * SLM, f"slab{i}") for i in range(2)]
    bfar.push()
    far.push()
    ut = [bfar.take(D, f"ut{i}") for i in range(2)]
    xt = [far.take(D, f"xt{i}") for i in range(2)]
    xfull = [far.take(Tc + KCV, f"xfull{i}") for i in range(2)]
    cva = [far.take(Tc, f"cva{i}") for i in range(2)]
    cvb = [far.take(Tc, f"cvb{i}") for i in range(2)]
    bfar.pop()
    far.pop()
    bfar.push()
    far.push()
    dtin_bf = bfar.take(Tc, "dtin")
    bcf = bfar.take(Tc, "bcf")
    cbc = [bfar.take(Tc, f"cbc{i}") for i in range(NST)]
    bbc = [bfar.take(Tc, f"bbc{i}") for i in range(NST)]
    szb = [far.take(Tc, f"szb{i}") for i in range(2)]
    delta = [far.take(Tc, f"delta{i}") for i in range(2)]
    sfp = [far.take(Tc, f"sfp{i}") for i in range(3)]
    du = [bfar.take(Tc, f"du{i}") for i in range(2)]
    atile = [far.take(Tc, f"at{i}") for i in range(2)]
    dub = [bfar.take(Tc, f"dub{i}") for i in range(2)]
    stile = [bfar.take(Tc, f"st{i}") for i in range(2)]
    e4t = [bfar.take(Tc, f"e4{i}") for i in range(2)]
    yacc = far.take(Tc, "yacc")
    yfin = [far.take(Tc, "yfin0")]
    bfar.pop()
    far.pop()
    bfar.push()
    far.push()
    wo_slab = [bfar.take(WKH * SLM, f"wosl{i}") for i in range(2)]
    mo_bf = [bfar.take(Tc, f"mo_bf{i}") for i in range(MBo)]
    mobt = [bfar.take(D, f"mobt{i}") for i in range(2)]
    xr = [far.take(D, f"xr{i}") for i in range(2)]
    x1t = [far.take(D, f"x1t{i}") for i in range(2)]
    bfar.pop()
    far.pop()
    M_bf_hi, M_f_hi = bfar.hi, far.hi
    bfar.pop()
    far.pop()

    bfar.push()
    far.push()
    h2e = [bfar.take(Tc, f"h2e{i}") for i in range(E)]
    gate_r = bfar.take(KB * E, "gate_r")
    rgT_r = bfar.take(KB * 128, "rgT_r")
    ruT_r = bfar.take(KB * 128, "ruT_r")
    rdT_r = bfar.take(IB * 128, "rdT_r")
    srgT_r = bfar.take(KB * 128, "srgT_r")
    sruT_r = bfar.take(KB * 128, "sruT_r")
    sugT_r = bfar.take(128, "sugT_r")
    suuT_r = bfar.take(128, "suuT_r")
    sudT_r = bfar.take(128, "sudT_r")
    scgT_r = bfar.take(SI, "scgT_r")
    scuT_r = bfar.take(SI, "scuT_r")
    srdT_r = bfar.take(SIB * 128, "srdT_r")
    scdT_r = bfar.take(D, "scdT_r")
    rg_bf = bfar.take(Tc, "rg_bf")
    ru_bf = bfar.take(Tc, "ru_bf")
    h2sh = bfar.take(Tc, "h2sh")
    we = far.take(TT * E, "we")
    esb = [far.take(E, f"esb{i}") for i in range(2)]
    tcur = [far.take(E, f"tcur{i}") for i in range(2)]
    tc2 = [far.take(E, f"tc2{i}") for i in range(2)]
    gsm = [far.take(8, f"gsm{i}") for i in range(2)]
    wbc = [far.take(Tc, f"wbc{i}") for i in range(2)]
    bfar.push()
    far.push()
    u2t = [bfar.take(D, f"u2t{i}") for i in range(2)]
    xf1 = [far.take(D, f"xf1_{i}") for i in range(2)]
    bfar.pop()
    far.pop()
    bfar.push()
    far.push()
    h1a = [bfar.take(Tc, f"h1a{i}") for i in range(2)]
    h1b = [bfar.take(Tc, f"h1b{i}") for i in range(2)]
    gtile = [bfar.take(Tc, f"gt{i}") for i in range(max(IB, SIB // 2))]
    gut = [bfar.take(Tc, f"gut{i}") for i in range(2)]
    rdw = [bfar.take(Tc, f"rdw{i}") for i in range(2)]
    expwt = [bfar.take(EXPC, f"expwt{i}") for i in range(2)]
    bfar.pop()
    far.pop()
    bfar.push()
    far.push()
    cdt = [bfar.take(E * 128, f"cdt{i}") for i in range(2)]
    fo_bf = [bfar.take(Tc, f"fo_bf{i}") for i in range(MBo)]
    fot = [bfar.take(D, f"fot{i}") for i in range(2)]
    xf4 = [far.take(D, f"xf4_{i}") for i in range(2)]
    ot = [far.take(D, f"ot{i}") for i in range(2)]
    bfar.pop()
    far.pop()
    F_bf_hi, F_f_hi = bfar.hi, far.hi
    bfar.pop()
    far.pop()

    M01set = ut + xt + xfull + cva + cvb
    M23set = ([dtin_bf, bcf, yacc] + cbc + bbc + szb + delta + sfp + du + atile
              + dub + stile + e4t + yfin)
    M4set = wo_slab + mo_bf + mobt + xr + x1t
    Mcommon = list(xcy) + slab
    Mall = M01set + M23set + M4set + Mcommon
    F0set = u2t + xf1
    F23set = h1a + h1b + gtile + gut + rdw + expwt
    F4set = cdt + fo_bf + fot + xf4 + ot
    Fcommon = h2e + [gate_r, rgT_r, ruT_r, rdT_r, srgT_r, sruT_r, sugT_r, suuT_r,
               sudT_r, scgT_r, scuT_r, srdT_r, scdT_r, rg_bf, ru_bf, h2sh, we,
               wbc[0], wbc[1]] + esb + tcur + tc2 + gsm
    Fall = F0set + F23set + F4set + Fcommon

    def psb(name, cols, dtype=F32):
        t = nc.alloc_psum_tensor(name, [128, cols], dtype)
        return Buf(t.ap(), name)

    psg = [psb(f"psg{i}", Tc if Tc <= 512 else 512) for i in range(5)]
    ptb = [psb(f"ptb{i}", 1024, BF16) for i in range(2)]
    ptf = psb("ptf", Tc if Tc <= 512 else 512, F32)

    taps = []

    def tap(name, bufs, ap):
        if name not in cfg["taps"]:
            return
        t = nc.dram_tensor("tap_" + name, list(ap.shape), ap.dtype, kind="ExternalOutput").ap()
        p.op("sp", lambda e, t=t, ap=ap: e.dma_start(t, ap), reads=list(bufs), dma=True, tag="tap")
        taps.append(name)

    def dma(dst_buf, src_ap, eng="sp", cols=None, tag=""):
        if cols is None:
            dap, w, pw = dst_buf.ap, [dst_buf], []
        else:
            dap, w, pw = dst_buf.ap[:, cols[0]:cols[1]], [], [dst_buf]
        return p.op(eng, lambda e, d=dap, s=src_ap: e.dma_start(d, s),
                    writes=w, pwrites=pw, dma=True, tag=tag)

    def mm(psbuf, lhs, lcols, rhs, rcols, start, last, pslice=None, tag="mm"):
        lap = lhs.ap[:, lcols[0]:lcols[1]]
        rap = rhs.ap[:, rcols[0]:rcols[1]]
        pap = psbuf.ap[:, 0:rcols[1] - rcols[0]] if pslice is None else psbuf.ap[
            pslice[0]:pslice[1], pslice[2]:pslice[3]]
        return p.op("pe",
                    lambda e, o=pap, l=lap, r=rap, s=start, t=last: e.matmul(o, l, r, start=s, stop=t),
                    reads=[lhs, rhs],
                    writes=[psbuf] if start else (),
                    pwrites=[psbuf] if not start else (),
                    tag=tag)

    def ev_copy(dst, src, eng="act", dcols=None, scols=None, srows=None, tag="ev"):
        dap = dst.ap if dcols is None else dst.ap[:, dcols[0]:dcols[1]]
        sap = src.ap
        if scols is not None:
            sap = sap[:, scols[0]:scols[1]]
        if srows is not None:
            sap = sap[srows[0]:srows[1], :]
        fn = (lambda e, d=dap, s=sap: e.copy(d, s)) if eng == "act" else (
            lambda e, d=dap, s=sap: e.tensor_copy(d, s))
        return p.op(eng, fn, reads=[src],
                    writes=[dst] if dcols is None else (),
                    pwrites=[dst] if dcols is not None else (), tag=tag)

    def ln_phase(xbuf, ubuf, st, tag):
        p.op("dve", lambda e, x=xbuf, s=st: e.reduce_sum(
            s.ap[:, 0:1], x.ap, axis=AX.X, negate=True), reads=[xbuf], writes=[st],
            tag=tag + "s", small=True)
        p.op("act", lambda e, x=xbuf: e.activation(
            sqs.ap[:, 0:D], x.ap, AF.Square),
            reads=[xbuf], writes=[sqs], tag=tag + "sq")
        p.op("dve", lambda e, s=st: e.reduce_sum(
            s.ap[:, 1:2], sqs.ap[:, 0:D], axis=AX.X), reads=[sqs], pwrites=[st],
            tag=tag + "sq2", small=True)
        p.op("dve", lambda e, s=st: e.tensor_scalar(
            s.ap[:, 2:3], s.ap[:, 0:1], 1.0 / D, None, OP.mult), reads=[st], pwrites=[st],
            tag=tag + "m", small=True)
        p.op("dve", lambda e, s=st: e.tensor_scalar(
            s.ap[:, 3:4], s.ap[:, 1:2], 1.0 / D, None, OP.mult), reads=[st], pwrites=[st],
            tag=tag + "e2", small=True)
        p.op("dve", lambda e, s=st: e.tensor_tensor(
            s.ap[:, 4:5], s.ap[:, 2:3], s.ap[:, 2:3], OP.mult), reads=[st], pwrites=[st],
            tag=tag + "m2", small=True)
        p.op("dve", lambda e, s=st: e.tensor_tensor(
            s.ap[:, 5:6], s.ap[:, 3:4], s.ap[:, 4:5], OP.subtract), reads=[st], pwrites=[st],
            tag=tag + "v", small=True)
        p.op("act", lambda e, s=st: e.activation(
            s.ap[:, 6:7], s.ap[:, 5:6], AF.Ln, bias=cst.ap[:, 0:1]),
            reads=[st, cst], pwrites=[st], tag=tag + "ln", small=True)
        p.op("act", lambda e, s=st: e.activation(
            s.ap[:, 7:8], s.ap[:, 6:7], AF.Exp, scale=-0.5), reads=[st], pwrites=[st],
            tag=tag + "rs", small=True)
        p.op("dve", lambda e, x=xbuf, s=st, u=ubuf: e.tensor_scalar(
            u.ap, x.ap, s.ap[:, 2:3], s.ap[:, 7:8], OP.add, OP.mult),
            reads=[xbuf, st], writes=[ubuf], tag=tag + "u")

    def transpose_to(dst, dst_cols, src, src_cols, pt, tag):
        p.op("pe", lambda e, s=src, sc=src_cols, pt=pt: e.transpose(
            pt.ap[:, 0:128], s.ap[:, sc[0]:sc[1]], idb_r.ap),
            reads=[src, idb_r], writes=[pt], tag=tag)
        p.op("act", lambda e, d=dst, dc=dst_cols, pt=pt: e.copy(
            d.ap[:, dc[0]:dc[1]], pt.ap[:, 0:128]),
            reads=[pt], pwrites=[dst], tag=tag + "e")

    for dst, src in ((winx_bf, winx), (winz_bf, winz), (wout_bf, wout_in),
                     (expw_bf, expw_in), (cdmb_bf, cdmb_in)):
        p.op("pool", lambda e, d=dst, s=src: e.dma_start(d, s), dma=True, tag="cast")
    castw = Buf(None, "castw")
    castw.writers = [p.ops["pool"][-1]["h"]]

    for dst, src in ((wdt_r, wdt_in), (wxdt_r, wxdt_in), (wxbc_r, wxbc_in),
                     (idb_r, ident_in)):
        p.op("pool", lambda e, d=dst, s=src: e.dma_start(d.ap, s), writes=[dst], dma=True, tag="res")
    for dst, src in ((convw_r, convw_in), (convb_r, convb_in), (bdt_r, bdt_in),
                     (dchan_r, dchan_in), (bout_r, bout_in), (idf_r, ident_in)):
        dma(dst, src, tag="res")
    p.op("dve", lambda e: e.memset(cst.ap[:, 0:1], 1e-5), writes=[cst], tag="cst0")
    p.op("dve", lambda e: e.memset(cst.ap[:, 1:2], 1.0), pwrites=[cst], tag="cst1")
    dma(sqs, alog_in, cols=(0, DB * NST), tag="alog")
    p.op("act", lambda e: e.activation(sqs.ap[:, DB * NST:2 * DB * NST], sqs.ap[:, 0:DB * NST], AF.Exp),
         reads=[sqs], pwrites=[sqs], tag="aexp")
    p.op("dve", lambda e: e.tensor_scalar(a_r.ap, sqs.ap[:, DB * NST:2 * DB * NST], -1.0, None, OP.mult),
         reads=[sqs], writes=[a_r], tag="aneg")

    p.barrier("ph0")

    for sqi in range(NSEQ):
        p.op("dve", lambda e: e.memset(state.ap, 0.0), writes=[state], tag="st0")
        p.op("dve", lambda e: e.memset(carry.ap, 0.0), writes=[carry], tag="cy0")
        for ci in range(NCH):
            t0 = sqi * SEQ + ci * Tc

            for tt in range(TT):
                xb, ub, st = xt[tt % 2], ut[tt % 2], lnt[tt % 2]
                dma(xb, x_in[t0 + tt * 128: t0 + (tt + 1) * 128, :], tag="xload")
                tap(f"xt_{sqi}_{ci}_{tt}", [xb], xb.ap)
                ln_phase(xb, ub, st, "ln")
                tap(f"lnt_{sqi}_{ci}_{tt}", [st], st.ap)
                tap(f"ut_{sqi}_{ci}_{tt}", [ub], ub.ap)
                for kb in range(KB):
                    transpose_to(u_f, (kb * Tc + tt * 128, kb * Tc + (tt + 1) * 128),
                                 ub, (kb * 128, (kb + 1) * 128), ptb[kb % 2], "tru")
            tap(f"uf_{sqi}_{ci}", [u_f], u_f.ap)

            for s in range(XSL):
                slb = slab[s % 2]
                p.op("sp", lambda e, d=slb, s=s: e.dma_start(d.ap, winx_bf[s]),
                     reads=[castw], writes=[slb], dma=True, tag="wslab")
                nmb = SLM // 128
                for mb in range(nmb):
                    dblk = s * nmb + mb
                    pg = psg[dblk % 2]
                    for kb in range(KB):
                        mm(pg, slb, (kb * SLM + mb * 128, kb * SLM + (mb + 1) * 128),
                           u_f, (kb * Tc, (kb + 1) * Tc), kb == 0, kb == KB - 1, tag="inproj")
                    xf = xfull[dblk % 2]
                    p.op("dve", lambda e, xf=xf, dblk=dblk: e.tensor_copy(
                        xf.ap[:, 0:KCV - 1], carry.ap[:, dblk * (KCV - 1):(dblk + 1) * (KCV - 1)]),
                        reads=[carry], writes=[xf], tag="cv_c", small=True)
                    p.op("act", lambda e, xf=xf, pg=pg: e.copy(xf.ap[:, KCV - 1:KCV - 1 + Tc], pg.ap),
                         reads=[pg], pwrites=[xf], tag="cv_ev")
                    p.op("dve", lambda e, xf=xf, dblk=dblk: e.tensor_copy(
                        carry.ap[:, dblk * (KCV - 1):(dblk + 1) * (KCV - 1)],
                        xf.ap[:, Tc:Tc + KCV - 1]),
                        reads=[xf], pwrites=[carry], tag="cv_s")
                    ca, cb = cva[dblk % 2], cvb[dblk % 2]
                    p.op("dve", lambda e, xf=xf, dblk=dblk, o=ca: e.tensor_scalar(
                        o.ap, xf.ap[:, 0:Tc], convw_r.ap[:, dblk * KCV:dblk * KCV + 1], None, OP.mult),
                        reads=[xf, convw_r], writes=[ca], tag="cv0")
                    src_, dst_ = ca, cb
                    for k in range(1, KCV):
                        p.op("dve", lambda e, xf=xf, dblk=dblk, k=k, s2=src_, d2=dst_: e.scalar_tensor_tensor(
                            d2.ap, xf.ap[:, k:k + Tc], convw_r.ap[:, dblk * KCV + k:dblk * KCV + k + 1],
                            s2.ap, OP.mult, OP.add),
                            reads=[xf, convw_r, src_], writes=[dst_], tag="cvk")
                        src_, dst_ = dst_, src_
                    p.op("act", lambda e, s2=src_, dblk=dblk: e.activation(
                        xcy[dblk].ap, s2.ap, AF.Silu, bias=convb_r.ap[:, dblk:dblk + 1]),
                        reads=[src_, convb_r], writes=[xcy[dblk]], tag="cvsilu")
            tap(f"xc_{sqi}_{ci}", xcy, xcy_all_ap)
            p.handoff(M01set, M23set)
            if stop == "M1":
                p.handoff(Mall + Fall, Mall)
                continue

            pgdt = psg[2]
            for kb in range(DB):
                mm(pgdt, wxdt_r, (kb * RDT, (kb + 1) * RDT), xcy[kb], (0, Tc),
                   kb == 0, kb == DB - 1, tag="xd_dt")
            ev_copy(dtin_bf, pgdt, tag="dtin_ev")
            pgbc = psg[3]
            for kb in range(DB):
                mm(pgbc, wxbc_r, (kb * 2 * NST, (kb + 1) * 2 * NST), xcy[kb], (0, Tc),
                   kb == 0, kb == DB - 1, pslice=(0, 2 * NST, 0, Tc), tag="xd_bc")
            p.op("act", lambda e: e.copy(bcf.ap[0:2 * NST, :], pgbc.ap[0:2 * NST, 0:Tc]),
                 reads=[pgbc], writes=[bcf], tag="bc_ev")
            bcd = bc_dram[ci % 2]
            p.op("sp", lambda e, d=bcd.ap: e.dma_start(d, bcf.ap[0:2 * NST, :]),
                 reads=[bcf], writes=[bcd], dma=True, tag="bc_w")
            for n in range(NST):
                p.op("sp", lambda e, n=n, s=bcd.ap: e.dma_start(
                    bbc[n].ap,
                    s[n:n + 1, :].partition_broadcast(128).squeeze(1)),
                    reads=[bcd], writes=[bbc[n]], dma=True, tag="bbcast")
            for n in range(NST):
                p.op("sp", lambda e, n=n, s=bcd.ap: e.dma_start(
                    cbc[n].ap,
                    s[NST + n:NST + n + 1, :].partition_broadcast(128).squeeze(1)),
                    reads=[bcd], writes=[cbc[n]], dma=True, tag="cbcast")
            tap(f"dtin_{sqi}_{ci}", [dtin_bf], dtin_bf.ap)
            tap(f"bbc_{sqi}_{ci}", bbc, bbc[0].ap)

            for dblk in range(DB):
                pg = psg[dblk % 2]
                mm(pg, wdt_r, (dblk * 128, (dblk + 1) * 128), dtin_bf, (0, Tc), True, True, tag="dtmm")
                s0, s1, s2 = sfp
                p.op("act", lambda e, pg=pg, dblk=dblk: e.activation(
                    s0.ap, pg.ap, AF.Abs, bias=bdt_r.ap[:, dblk:dblk + 1]),
                    reads=[pg, bdt_r], writes=[s0], tag="sp_abs")
                p.op("act", lambda e: e.activation(s1.ap, s0.ap, AF.Exp, scale=-1.0),
                     reads=[s0], writes=[s1], tag="sp_exp")
                p.op("act", lambda e: e.activation(s2.ap, s1.ap, AF.Ln, bias=cst.ap[:, 1:2]),
                     reads=[s1, cst], writes=[s2], tag="sp_ln")
                p.op("act", lambda e, pg=pg, dblk=dblk: e.activation(
                    s0.ap, pg.ap, AF.Relu, bias=bdt_r.ap[:, dblk:dblk + 1]),
                    reads=[pg, bdt_r], writes=[s0], tag="sp_relu")
                dl = delta[dblk % 2]
                scan_hs = []
                p.op("dve", lambda e, d=dl: e.tensor_tensor(d.ap, s2.ap, s0.ap, OP.add),
                     reads=[s2, s0], writes=[dl], tag="sp_add")
                db_ = du[dblk % 2]
                p.op("dve", lambda e, d=dl, o=db_, dblk=dblk: e.tensor_tensor(
                    o.ap, d.ap, xcy[dblk].ap, OP.mult),
                    reads=[dl, xcy[dblk]], writes=[db_], tag="du")
                p.op("dve", lambda e, dblk=dblk: e.tensor_scalar(
                    yacc.ap, xcy[dblk].ap, dchan_r.ap[:, dblk:dblk + 1], None, OP.mult),
                    reads=[xcy[dblk], dchan_r], writes=[yacc], tag="yinit")
                for n in range(NST):
                    col = dblk * NST + n
                    at = atile[n % 2]
                    p.op("act", lambda e, d=dl, o=at, col=col: e.activation(
                        o.ap, d.ap, AF.Exp, scale=a_r.ap[:, col:col + 1]),
                        reads=[dl, a_r], writes=[at], tag="aexp")
                    dbt = dub[n % 2]
                    p.op("dve", lambda e, d=db_, o=dbt, n=n: e.tensor_tensor(
                        o.ap, d.ap, bbc[n].ap, OP.mult),
                        reads=[db_, bbc[n]], writes=[dbt], tag="dub")
                    stl = stile[n % 2]
                    sh = p.op("dve", lambda e, a=at, b=dbt, o=stl, col=col: e.tensor_tensor_scan(
                        o.ap, a.ap, b.ap, state.ap[:, col:col + 1], OP.mult, OP.add),
                        reads=[at, dbt, state], writes=[stl], tag="scan")
                    scan_hs.append(sh)
                    p.op("act", lambda e, s=stl, col=col: e.copy(
                        state.ap[:, col:col + 1], s.ap[:, Tc - 1:Tc]),
                        reads=[stl], pwrites=[state], tag="sc_cy")
                    et = e4t[n % 2]
                    p.op("dve", lambda e, s=stl, o=et, n=n: e.tensor_tensor(
                        o.ap, s.ap, cbc[n].ap, OP.mult),
                        reads=[stl, cbc[n]], writes=[et], tag="e4m")
                    if n == 1:
                        p.op("pool", lambda e, t=et: e.tensor_copy(yfin[0].ap, t.ap),
                             reads=[et], writes=[yfin[0]], tag="yo0")
                    elif n % 2 == 1:
                        p.op("pool", lambda e, t=et: e.tensor_tensor(
                            yfin[0].ap, yfin[0].ap, t.ap, OP.add),
                            reads=[et, yfin[0]], pwrites=[yfin[0]], tag="yoadd")
                    else:
                        p.op("dve", lambda e, t=et: e.tensor_tensor(
                            yacc.ap, yacc.ap, t.ap, OP.add),
                            reads=[et, yacc], pwrites=[yacc], tag="yeadd")
                nmb = SLM // 128
                zs, zmb = dblk // nmb, dblk % nmb
                zsl = slab[zs % 2]
                if zmb == 0:
                    p.op("sp", lambda e, d=zsl, s=zs: e.dma_start(d.ap, winz_bf[s]),
                         reads=[castw], writes=[zsl], dma=True, tag="zslab")
                pgz = psg[2 + dblk % 2]
                for kb in range(KB):
                    lap = zsl.ap[:, kb * SLM + zmb * 128: kb * SLM + (zmb + 1) * 128]
                    rap = u_f.ap[:, kb * Tc:(kb + 1) * Tc]
                    pap = pgz.ap[:, 0:Tc]
                    p.op("pe",
                         lambda e, o=pap, l=lap, r=rap, s=(kb == 0), t=(kb == KB - 1):
                         e.matmul(o, l, r, start=s, stop=t),
                         reads=[zsl, u_f],
                         writes=[pgz] if kb == 0 else (),
                         pwrites=[pgz] if kb != 0 else (),
                         after=[scan_hs[min(kb, len(scan_hs) - 1)]],
                         tag="zmm")
                sz = szb[dblk % 2]
                z1, z2 = sfp[1], sfp[2]
                p.op("act", lambda e, pg=pgz, o=z1: e.activation(o.ap, pg.ap, AF.Tanh, scale=0.5),
                     reads=[pgz], writes=[z1], tag="ztanh")
                p.op("dve", lambda e, a=z1, o=z2: e.tensor_scalar(
                    o.ap, a.ap, 0.5, 0.5, OP.mult, OP.add),
                    reads=[z1], writes=[z2], tag="zsig")
                p.op("dve", lambda e, pg=pgz, a=z2, o=sz: e.tensor_tensor(o.ap, pg.ap, a.ap, OP.mult),
                     reads=[pgz, z2], writes=[sz], tag="zmul")
                yf = sfp[0]
                p.op("dve", lambda e, o=yf: e.tensor_tensor(
                    o.ap, yacc.ap, yfin[0].ap, OP.add),
                    reads=[yacc, yfin[0]], writes=[yf], tag="ysum")
                p.op("dve", lambda e, a=yf, b=sz, dblk=dblk: e.tensor_tensor(
                    xcy[dblk].ap, a.ap, b.ap, OP.mult),
                    reads=[yf, sz], writes=[xcy[dblk]], tag="ymul")
            tap(f"ybf_{sqi}_{ci}", xcy, xcy_all_ap)
            p.handoff(M23set, M4set)
            if stop == "M3":
                p.handoff(Mall + Fall, Mall)
                continue

            nmb = SLM // 128
            for s in range(OSL):
                for kh in range(DB // WKH):
                    wsl = wo_slab[kh % 2]
                    p.op("sp", lambda e, d=wsl, s=s, kh=kh: e.dma_start(d.ap, wout_bf[s, kh]),
                         reads=[castw], writes=[wsl], dma=True, tag="woslab")
                    for mb in range(nmb):
                        pg = psg[mb]
                        for kk in range(WKH):
                            kb = kh * WKH + kk
                            mm(pg, wsl, (kk * SLM + mb * 128, kk * SLM + (mb + 1) * 128),
                               xcy[kb], (0, Tc), kb == 0, kb == DB - 1, tag="womm")
                for mb in range(nmb):
                    mo = s * nmb + mb
                    p.op("act", lambda e, pg=psg[mb], mo=mo: e.activation(
                        mo_bf[mo].ap, pg.ap, AF.Identity,
                        bias=bout_r.ap[:, mo:mo + 1]),
                        reads=[psg[mb], bout_r], writes=[mo_bf[mo]], tag="wo_ev")
            tap(f"mobf_{sqi}_{ci}", mo_bf, mo_bf[0].ap)
            for tt in range(TT):
                mt = mobt[tt % 2]
                for mo in range(MBo):
                    transpose_to(mt, (mo * 128, (mo + 1) * 128),
                                 mo_bf[mo], (tt * 128, (tt + 1) * 128),
                                 ptb[mo % 2], "trmo")
                tap(f"mot_{sqi}_{ci}_{tt}", [mt], mt.ap)
                xrb = xr[tt % 2]
                dma(xrb, x_in[t0 + tt * 128: t0 + (tt + 1) * 128, :], tag="xreload")
                xo = x1t[tt % 2]
                p.op("dve", lambda e, a=xrb, b=mt, o=xo: e.tensor_tensor(
                    o.ap, a.ap, b.ap, OP.add), reads=[xrb, mt], writes=[xo], tag="res1")
                tap(f"x1t_{sqi}_{ci}_{tt}", [xo], xo.ap)
                p.op("sp", lambda e, o=xo, tt=tt, t0=t0: e.dma_start(
                    x1_dram.ap[t0 + tt * 128: t0 + (tt + 1) * 128, :], o.ap),
                    reads=[xo], pwrites=[x1_dram], dma=True, tag="x1w")
            tap(f"x1_{sqi}_{ci}", [x1_dram], x1_dram.ap[t0:t0 + Tc, :])
            p.handoff(Mall, Fall)
            if stop == "M4":
                p.handoff(Mall + Fall, Mall)
                continue

            for tt in range(TT):
                xb, ub, st = xf1[tt % 2], u2t[tt % 2], lnt[tt % 2]
                p.op("sp", lambda e, d=xb, tt=tt, t0=t0: e.dma_start(
                    d.ap, x1_dram.ap[t0 + tt * 128: t0 + (tt + 1) * 128, :]),
                    reads=[x1_dram], writes=[xb], dma=True, tag="x1r")
                ln_phase(xb, ub, st, "l2")
                for kb in range(KB):
                    transpose_to(u_f, (kb * Tc + tt * 128, kb * Tc + (tt + 1) * 128),
                                 ub, (kb * 128, (kb + 1) * 128), ptb[kb % 2], "tru2")
            tap(f"u2f_{sqi}_{ci}", [u_f], u_f.ap)
            p.handoff(F0set, F23set)
            if stop == "F0":
                p.handoff(Mall + Fall, Mall)
                continue

            for dst, src in ((gate_r, gate_in), (rgT_r, rgT_in), (ruT_r, ruT_in),
                             (rdT_r, rdT_in), (srgT_r, srgT_in), (sruT_r, sruT_in),
                             (sugT_r, sugT_in), (suuT_r, suuT_in), (sudT_r, sudT_in),
                             (scgT_r, scgT_in), (scuT_r, scuT_in), (srdT_r, srdT_in),
                             (scdT_r, scdT_in)):
                p.op("pool", lambda e, d=dst, s=src: e.dma_start(d.ap, s),
                     writes=[dst], dma=True, tag="fwt")
            for tt in range(TT):
                pg = psg[4]
                for kb in range(KB):
                    mm(pg, u_f, (kb * Tc + tt * 128, kb * Tc + (tt + 1) * 128),
                       gate_r, (kb * E, (kb + 1) * E), kb == 0, kb == KB - 1,
                       pslice=(0, 128, 0, E), tag="gmm")
                es, tcu, tcv, gs = esb[tt % 2], tcur[tt % 2], tc2[tt % 2], gsm[tt % 2]
                p.op("dve", lambda e, pg=pg, g=gs: e.reduce_max(
                    g.ap[:, 0:1], pg.ap[:, 0:E], axis=AX.X, negate=True),
                    reads=[pg], writes=[gs], tag="g_max", small=True)
                p.op("act", lambda e, pg=pg, g=gs, o=es: e.activation(
                    o.ap, pg.ap[:, 0:E], AF.Exp, bias=g.ap[:, 0:1]),
                    reads=[pg, gs], writes=[es], tag="g_exp", small=True)
                p.op("dve", lambda e, a=es, o=tcu: e.tensor_copy(o.ap, a.ap),
                     reads=[es], writes=[tcu], tag="g_cp", small=True)
                cur, nxt = tcu, tcv
                for it in range(3):
                    p.op("dve", lambda e, t=cur, g=gs: e.reduce_max(
                        g.ap[:, 1:2], t.ap, axis=AX.X), reads=[cur], pwrites=[gs], tag="g_m", small=True)
                    p.op("dve", lambda e, t=cur, g=gs: e.tensor_scalar(
                        sqs.ap[:, 0:E], t.ap, g.ap[:, 1:2], None, OP.is_equal),
                        reads=[cur, gs], writes=[sqs], tag="g_eq", small=True)
                    p.op("dve", lambda e, t=cur, o=nxt: e.scalar_tensor_tensor(
                        o.ap, sqs.ap[:, 0:E], NEG_BIG, t.ap, OP.mult, OP.add),
                        reads=[sqs, cur], writes=[nxt], tag="g_rm", small=True)
                    cur, nxt = nxt, cur
                p.op("dve", lambda e, t=cur, g=gs: e.reduce_max(
                    g.ap[:, 1:2], t.ap, axis=AX.X), reads=[cur], pwrites=[gs], tag="g_m4", small=True)
                p.op("dve", lambda e, a=es, g=gs: e.tensor_scalar(
                    sqs.ap[:, 0:E], a.ap, g.ap[:, 1:2], None, OP.is_le),
                    reads=[es, gs], writes=[sqs], tag="g_le", small=True)
                p.op("dve", lambda e, a=es, g=gs: e.tensor_scalar(
                    sqs.ap[:, E:2 * E], a.ap, g.ap[:, 1:2], None, OP.is_equal),
                    reads=[es, gs], pwrites=[sqs], tag="g_eq4", small=True)
                p.op("dve", lambda e: e.tensor_tensor(
                    sqs.ap[:, 2 * E:3 * E], sqs.ap[:, E:2 * E], sqs.ap[:, 0:E], OP.subtract),
                    reads=[sqs], pwrites=[sqs], tag="g_k1", small=True)
                p.op("dve", lambda e, a=es, t=cur: e.scalar_tensor_tensor(
                    t.ap, sqs.ap[:, 2 * E:3 * E], 1.0, a.ap, OP.add, OP.mult),
                    reads=[sqs, es], writes=[cur], tag="g_wr", small=True)
                p.op("dve", lambda e, t=cur, g=gs: e.reduce_sum(
                    g.ap[:, 2:3], t.ap, axis=AX.X), reads=[cur], pwrites=[gs], tag="g_s4", small=True)
                p.op("dve", lambda e, g=gs: e.tensor_scalar(
                    g.ap[:, 3:4], g.ap[:, 2:3], 1e-20, None, OP.add),
                    reads=[gs], pwrites=[gs], tag="g_eps", small=True)
                p.op("dve", lambda e, g=gs: e.reciprocal(g.ap[:, 4:5], g.ap[:, 3:4]),
                     reads=[gs], pwrites=[gs], tag="g_rcp", small=True)
                p.op("dve", lambda e, t=cur, g=gs, tt=tt: e.tensor_scalar(
                    we.ap[:, tt * E:(tt + 1) * E], t.ap, g.ap[:, 4:5], None, OP.mult),
                    reads=[cur, gs], pwrites=[we], tag="g_we", small=True)
            wde = wall_dram[ci % 2]
            for tt in range(TT):
                p.op("pe", lambda e, tt=tt: e.transpose(
                    ptf.ap[0:E, 0:128], we.ap[:, tt * E:(tt + 1) * E], idf_r.ap),
                    reads=[we, idf_r], writes=[ptf], tag="trwe")
                p.op("dve", lambda e, tt=tt: e.tensor_copy(
                    sqs.ap[0:E, tt * 128:(tt + 1) * 128], ptf.ap[0:E, 0:128]),
                    reads=[ptf], pwrites=[sqs], tag="trwe_ev")
            p.op("sp", lambda e, d=wde.ap: e.dma_start(d, sqs.ap[0:E, 0:Tc]),
                 reads=[sqs], writes=[wde], dma=True, tag="wall_w")
            tap(f"we_{sqi}_{ci}", [we], we.ap)
            if stop == "F1":
                p.handoff(Mall + Fall, Mall)
                continue

            def kproj(wbuf, dstbuf, pgi, tag):
                pg = psg[pgi]
                for kb in range(KB):
                    mm(pg, wbuf, (kb * 128, (kb + 1) * 128), u_f, (kb * Tc, (kb + 1) * Tc),
                       kb == 0, kb == KB - 1, tag=tag)
                ev_copy(dstbuf, pg, tag=tag + "ev")
                return pg

            kproj(rgT_r, rg_bf, 0, "rg")
            kproj(ruT_r, ru_bf, 1, "ru")
            kproj(srgT_r, h1a[0], 2, "srg")
            pg = psg[2]
            mm(pg, sugT_r, (0, 128), h1a[0], (0, Tc), True, True, tag="sug")
            ev_copy(h1a[1], pg, tag="sugev")
            kproj(sruT_r, h1b[0], 3, "sru")
            pg = psg[3]
            mm(pg, suuT_r, (0, 128), h1b[0], (0, Tc), True, True, tag="suu")
            ev_copy(h1b[1], pg, tag="suuev")
            HF = SIB // 2
            pgr = psg[4]
            for hf in range(2):
                for ib in range(HF):
                    sib = hf * HF + ib
                    pg = psg[ib % 2]
                    mm(pg, scgT_r, (sib * 128, (sib + 1) * 128), h1a[1], (0, Tc), True, True, tag="scg")
                    p.op("act", lambda e, pg=pg, ib=ib: e.activation(
                        gtile[ib].ap, pg.ap, AF.Silu),
                        reads=[pg], writes=[gtile[ib]], tag="scgsilu")
                for ib in range(HF):
                    sib = hf * HF + ib
                    pg = psg[ib % 2]
                    mm(pg, scuT_r, (sib * 128, (sib + 1) * 128), h1b[1], (0, Tc), True, True, tag="scu")
                    gb = gut[ib % 2]
                    p.op("dve", lambda e, pg=pg, ib=ib, o=gb: e.scalar_tensor_tensor(
                        o.ap, pg.ap, 1.0, gtile[ib].ap, OP.mult, OP.mult),
                        reads=[pg, gtile[ib]], writes=[gb], tag="sgu")
                    mm(pgr, srdT_r, (sib * 128, (sib + 1) * 128), gb, (0, Tc),
                       sib == 0, sib == SIB - 1, tag="srd")
            ev_copy(h1a[0], pgr, tag="srdev")
            pg = psg[2]
            mm(pg, sudT_r, (0, 128), h1a[0], (0, Tc), True, True, tag="sud")
            ev_copy(h2sh, pg, tag="sudev")
            tap(f"h2sh_{sqi}_{ci}", [h2sh], h2sh.ap)
            if stop == "F2":
                p.handoff(Mall + Fall, Mall)
                continue

            rot = [0]

            def npg():
                pg = psg[rot[0] % 4]
                rot[0] += 1
                return pg

            for ei in range(E):
                ew = expwt[ei % 2]
                p.op("sp", lambda e, d=ew, ei=ei: e.dma_start(d.ap, expw_bf[ei]),
                     reads=[castw], writes=[ew], dma=True, tag="expw")
                wb = wbc[ei % 2]
                p.op("sp", lambda e, d=wb, ei=ei, w=wde.ap: e.dma_start(
                    d.ap, w[ei:ei + 1, :].partition_broadcast(128).squeeze(1)),
                    reads=[wde], writes=[wb], dma=True, tag="wbcast")
                h1g, h1u = h1a[ei % 2], h1b[ei % 2]
                pg = npg()
                mm(pg, ew, (0, 128), rg_bf, (0, Tc), True, True, tag="ug")
                ev_copy(h1g, pg, tag="ugev")
                pg = npg()
                mm(pg, ew, (FI + 128, FI + 256), ru_bf, (0, Tc), True, True, tag="uu")
                ev_copy(h1u, pg, tag="uuev")
                for ib in range(IB):
                    pg = npg()
                    mm(pg, ew, (128 + ib * 128, 128 + (ib + 1) * 128), h1g, (0, Tc),
                       True, True, tag="cg")
                    p.op("act", lambda e, pg=pg, ib=ib: e.activation(
                        gtile[ib].ap, pg.ap, AF.Silu),
                        reads=[pg], writes=[gtile[ib]], tag="cgsilu")
                pgrd = psg[4] if ei % 2 == 0 else ptf
                for ib in range(IB):
                    pg = npg()
                    mm(pg, ew, (FI + 256 + ib * 128, FI + 256 + (ib + 1) * 128), h1u, (0, Tc),
                       True, True, tag="cu")
                    gb = gut[ib % 2]
                    p.op("dve", lambda e, pg=pg, ib=ib, o=gb: e.scalar_tensor_tensor(
                        o.ap, pg.ap, 1.0, gtile[ib].ap, OP.mult, OP.mult),
                        reads=[pg, gtile[ib]], writes=[gb], tag="gu")
                    mm(pgrd, rdT_r, (ib * 128, (ib + 1) * 128), gb, (0, Tc),
                       ib == 0, ib == IB - 1, tag="rdmm")
                rw = rdw[ei % 2]
                p.op("dve", lambda e, pg=pgrd, o=rw, w=wb: e.scalar_tensor_tensor(
                    o.ap, pg.ap, 1.0, w.ap, OP.mult, OP.mult),
                    reads=[pgrd, wb], writes=[rw], tag="rdwm")
                pg = npg()
                mm(pg, ew, (2 * FI + 256, 2 * FI + 384), rw, (0, Tc), True, True, tag="ud")
                p.op("act", lambda e, pg=pg, ei=ei: e.copy(
                    h2e[ei].ap, pg.ap), reads=[pg], writes=[h2e[ei]], tag="h2ev")
            tap(f"h2e_{sqi}_{ci}", h2e, h2e[0].ap)
            p.handoff(F23set + F0set, F4set)
            if stop == "F3":
                p.handoff(Mall + Fall, Mall)
                continue

            for mo in range(MBo):
                cd = cdt[mo % 2]
                p.op("sp", lambda e, d=cd, mo=mo: e.dma_start(d.ap, cdmb_bf[mo]),
                     reads=[castw], writes=[cd], dma=True, tag="cdT")
                pg = psg[mo % 2]
                for ei in range(E):
                    mm(pg, cd, (ei * 128, (ei + 1) * 128), h2e[ei], (0, Tc),
                       ei == 0, False, tag="cdmm")
                mm(pg, scdT_r, (mo * 128, (mo + 1) * 128), h2sh, (0, Tc), False, True, tag="scdmm")
                p.op("act", lambda e, pg=pg, mo=mo: e.copy(
                    fo_bf[mo].ap, pg.ap), reads=[pg], writes=[fo_bf[mo]], tag="foev")
            for tt in range(TT):
                ft = fot[tt % 2]
                for mo in range(MBo):
                    transpose_to(ft, (mo * 128, (mo + 1) * 128),
                                 fo_bf[mo], (tt * 128, (tt + 1) * 128),
                                 ptb[mo % 2], "trfo")
                xb = xf4[tt % 2]
                p.op("sp", lambda e, d=xb, tt=tt, t0=t0: e.dma_start(
                    d.ap, x1_dram.ap[t0 + tt * 128: t0 + (tt + 1) * 128, :]),
                    reads=[x1_dram], writes=[xb], dma=True, tag="x1r4")
                ob = ot[tt % 2]
                p.op("dve", lambda e, a=xb, b=ft, o=ob: e.tensor_tensor(
                    o.ap, a.ap, b.ap, OP.add), reads=[xb, ft], writes=[ob], tag="res2")
                p.op("sp", lambda e, o=ob, tt=tt, t0=t0: e.dma_start(
                    out_hbm[t0 + tt * 128: t0 + (tt + 1) * 128, :], o.ap),
                    reads=[ob], dma=True, tag="outw")
            p.handoff(Fall, Mall)

    ndma_sp = sum(1 for o in p.ops["sp"] if o["h"].dma)
    sems, dsems = p.emit(nc)
    with nc.Block() as blk:
        @blk.sync
        def _(sync):
            for lane in range(NLANES):
                cnt = ndma_sp // NLANES + (1 if lane < ndma_sp % NLANES else 0)
                if cnt:
                    sync.wait_ge(dsems["sp"][lane], 16 * cnt)

    return nc




def kernel(**inputs):
    cfg = make_cfg()
    N_CORES = 8
    B = 16
    BPC = B // N_CORES

    x = np.asarray(inputs["L0_attn_hidden_states"], np.float32)
    packed = pack_host(cfg, {k: np.asarray(v, np.float32) for k, v in inputs.items()
                             if k != "L0_attn_hidden_states"})

    nc = build_program(cfg)

    in_maps = []
    for c in range(N_CORES):
        m = dict(packed)
        xs = x[c * BPC:(c + 1) * BPC].reshape(BPC * cfg["SEQ"], cfg["D"])
        m["x"] = np.ascontiguousarray(xs)
        in_maps.append(m)

    res = run_bass_kernel_spmd(nc, in_maps, core_ids=list(range(N_CORES)))
    outs = [np.asarray(res.results[c]["out"]).reshape(BPC, cfg["SEQ"], cfg["D"])
            for c in range(N_CORES)]
    return np.concatenate(outs, axis=0)
